# version 05
# baseline (speedup 1.0000x reference)
import contextlib
import numpy as np
import concourse.bass as bass
import concourse.mybir as mybir
from concourse.bass_utils import run_bass_kernel_spmd

F32 = mybir.dt.float32
BF16 = mybir.dt.bfloat16
AF = mybir.ActivationFunctionType
ALU = mybir.AluOpType
AX = mybir.AxisListType

MAGIC = float(np.float32(1.5 * 2 ** 23))

DM = 1024
DI = 2048
DS = 64
HD = 64
NH = 32
CH = 2176
NP = 2208
TPC = 4096
NT = 32
TPM = 4
NMEGA = 8
LN_EPS = 1e-5
FFN_I = 4096

IN_CH = [(0, 512), (512, 512), (1024, 512), (1536, 512), (2048, 160)]
CONV_CH = [(0, 512), (512, 512), (1024, 512), (1536, 512), (2048, 128)]


class Ser:

    def __init__(self, sems):
        self.sems = sems
        self.idx = 0
        self.count = 0
        self.prev = None

    def _pre(self, eng):
        if self.prev is not None:
            eng.wait_ge(*self.prev)
            self.prev = None
        elif self.count > 0:
            eng.wait_ge(self.sems[self.idx], self.count)

    def _post(self, inst, inc):
        inst.then_inc(self.sems[self.idx], inc)
        self.count += inc
        if self.count > 40000:
            self.prev = (self.sems[self.idx], self.count)
            self.idx += 1
            self.count = 0

    def dma(self, eng, out, in_):
        self._pre(eng)
        self._post(eng.dma_start(out=out, in_=in_), 16)

    def op(self, eng, fn, *args, **kw):
        self._pre(eng)
        self._post(fn(*args, **kw), 1)

    def fence(self, eng):
        if self.prev is not None:
            eng.wait_ge(*self.prev)
        elif self.count > 0:
            eng.wait_ge(self.sems[self.idx], self.count)


FULLRANGE_TENSORS = set()


def _ap_range(ap):
    t = ap.tensor
    if t.name in FULLRANGE_TENSORS:
        return (0, 1 << 20, 0, 1 << 40)
    off = int(ap.offset)
    pairs = [(int(p[0]), int(p[1])) for p in ap.ap]
    if type(t).__name__.startswith("DRam"):
        hi = off + sum(st * (ct - 1) for st, ct in pairs) + 1
        return (0, 1, off, hi)
    shape = list(t.shape)
    rs = 1
    for d in shape[1:]:
        rs *= d
    p0 = off // rs
    f0 = off % rs
    pspan = 1
    fhi = f0
    for st, ct in pairs:
        ext = st * (ct - 1)
        if st != 0 and st % rs == 0:
            pspan += ext // rs
        else:
            fhi += ext
    fhi += 1
    if type(t).__name__.startswith("PSum"):
        be = 2048 // (2 if t.dtype == BF16 else 4)
        f0 = (f0 // be) * be
        fhi = ((fhi + be - 1) // be) * be
        return (0, 128, f0, fhi)
    return (p0, p0 + pspan, f0, fhi)


def _is_ap(x):
    return hasattr(x, "tensor") and hasattr(x, "ap") and hasattr(x, "offset")


def _overlap(a, b):
    return a[0] < b[1] and b[0] < a[1] and a[2] < b[3] and b[2] < a[3]


class Auto:

    serialize_all = False
    serialize_dma = False

    def __init__(self, nc, engs, esg):
        self.last = None
        self.nc = nc
        self.names = {id(e): n for e, n in engs}
        self.byname = {n: e for e, n in engs}
        self.osem = {n: esg(nc.semaphore(f"O{n}")) for _, n in engs}
        self.dsem = {n: esg(nc.semaphore(f"D{n}")) for _, n in engs}
        self.ocnt = {n: 0 for _, n in engs}
        self.dcnt = {n: 0 for _, n in engs}
        self.cc_sem = esg(nc.semaphore("CC"))
        self.vc = {}
        self.tmap = {}

    def _stream(self, eng, dma):
        return ("dma" if dma else "op", self.names[id(eng)])

    def _sem_wait(self, eng, st, sq):
        if st == ("cc",):
            eng.wait_ge(self.cc_sem, 1)
            return
        kind, n = st
        if kind == "op":
            eng.wait_ge(self.osem[n], sq)
        else:
            eng.wait_ge(self.dsem[n], 16 * sq)

    def _deps(self, eng, S, reads, writes):
        need = {}

        def add(st, sq):
            if st == S and S == ("op", "PE"):
                return
            if need.get(st, -1) < sq:
                need[st] = sq

        for ap in reads:
            ent = self.tmap.get(ap.tensor.name)
            if not ent:
                continue
            r = _ap_range(ap)
            for rec in ent[0]:
                if _overlap(r, rec[0]):
                    add(rec[1], rec[2])
        for ap in writes:
            ent = self.tmap.get(ap.tensor.name)
            if not ent:
                continue
            r = _ap_range(ap)
            for rec in ent[0]:
                if _overlap(r, rec[0]):
                    add(rec[1], rec[2])
            for rec in ent[1]:
                if _overlap(r, rec[0]):
                    add(rec[1], rec[2])
        vc = self.vc.setdefault(S, {})
        for st, sq in sorted(need.items(), key=lambda x: str(x[0])):
            if st == S and st[0] == "op":
                pass
            if vc.get(st, 0) >= sq:
                continue
            self._sem_wait(eng, st, sq)
            vc[st] = sq

    def _record(self, S, seq, reads, writes):
        for ap in writes:
            ent = self.tmap.setdefault(ap.tensor.name, [[], []])
            r = _ap_range(ap)

            def covered(rec):
                q = rec[0]
                return (r[0] <= q[0] and q[1] <= r[1]
                        and r[2] <= q[2] and q[3] <= r[3])

            ent[0] = [x for x in ent[0] if not covered(x)]
            ent[1] = [x for x in ent[1] if not covered(x)]
            ent[0].append((r, S, seq))
        for ap in reads:
            ent = self.tmap.setdefault(ap.tensor.name, [[], []])
            ent[1].append((_ap_range(ap), S, seq))

    def barrier(self):
        for n, e in self.byname.items():
            for m, c in self.ocnt.items():
                if c > 0:
                    e.wait_ge(self.osem[m], c)
            for m, c in self.dcnt.items():
                if c > 0:
                    e.wait_ge(self.dsem[m], 16 * c)
        for n in self.byname:
            for S in (("op", n), ("dma", n)):
                vc = self.vc.setdefault(S, {})
                for m, c in self.ocnt.items():
                    vc[("op", m)] = c
                for m, c in self.dcnt.items():
                    vc[("dma", m)] = c
        keep = {k: v for k, v in self.tmap.items() if k == "ccgat"}
        self.tmap = keep

    def _ser(self, eng, S, isdma=False):
        if self.serialize_dma and self.last is not None and isdma:
            st, sq = self.last
            vc = self.vc.setdefault(S, {})
            if st != S and vc.get(st, 0) < sq:
                self._sem_wait(eng, st, sq)
                vc[st] = sq
        if self.serialize_all and self.last is not None:
            st, sq = self.last
            vc = self.vc.setdefault(S, {})
            if st != S and vc.get(st, 0) < sq:
                self._sem_wait(eng, st, sq)
                vc[st] = sq

    def op(self, eng, fn, *args, **kw):
        S = self._stream(eng, False)
        aps = [a for a in args if _is_ap(a)]
        aps += [a for a in kw.values() if _is_ap(a)]
        writes, reads = aps[:1], aps[1:]
        self._deps(eng, S, reads, writes)
        self._ser(eng, S)
        if self.serialize_dma and getattr(self, "lastdma", None) is not None:
            st, sq = self.lastdma
            vc = self.vc.setdefault(S, {})
            if st != S and vc.get(st, 0) < sq:
                self._sem_wait(eng, st, sq)
                vc[st] = sq
        n = self.names[id(eng)]
        self.ocnt[n] += 1
        fn(*args, **kw).then_inc(self.osem[n], 1)
        self._record(S, self.ocnt[n], reads, writes)
        self.last = (S, self.ocnt[n])

    def dma(self, eng, out, in_):
        S = self._stream(eng, True)
        self._deps(eng, S, [in_], [out])
        n = self.names[id(eng)]
        self._ser(eng, S, isdma=True)
        vc = self.vc.setdefault(S, {})
        if self.dcnt[n] > 0 and vc.get(S, 0) < self.dcnt[n]:
            eng.wait_ge(self.dsem[n], 16 * self.dcnt[n])
            vc[S] = self.dcnt[n]
        self.dcnt[n] += 1
        eng.dma_start(out=out, in_=in_).then_inc(self.dsem[n], 16)
        self._record(S, self.dcnt[n], [in_], [out])
        self.last = (S, self.dcnt[n])
        self.lastdma = (S, self.dcnt[n])


    def fence(self, eng):
        for n, c in self.ocnt.items():
            if c > 0:
                eng.wait_ge(self.osem[n], c)
        for n, c in self.dcnt.items():
            if c > 0:
                eng.wait_ge(self.dsem[n], 16 * c)


def build_program(debug=False, n_megas=NMEGA, full=True, do_cc=True, parts="fcs"):
    nc = bass.Bass("TRN2", target_bir_lowering=False)

    def din(name, shape):
        return nc.dram_tensor(name, shape, F32, kind="ExternalInput").ap()

    u = din("u", [128 * (NT + 1), DM])
    w1t = din("w1t", [DM, NP])
    convwt = din("convwt", [4, CH, CH])
    convb = din("convb", [1, CH])
    alog = din("alog", [1, NH])
    prel = din("prel", [1, 1])
    w2t = din("w2t", [DM, DI])
    w3t = din("w3t", [DI, DM])
    wgt = din("wgt", [DM, FFN_I])
    wut = din("wut", [DM, FFN_I])
    wdt = din("wdt", [FFN_I, DM])
    lng = din("lng", [1, DI])
    lnb = din("lnb", [1, DI])
    rmask = din("rmask", [1, 8])

    o = nc.dram_tensor("o", [TPC, DM], F32, kind="ExternalOutput").ap()

    dkind = "ExternalOutput" if debug else "Internal"

    convw_bf = nc.dram_tensor("convw_bf", [17 * 4 * 128, CH], BF16)
    w2bf = nc.dram_tensor("w2bf", [DM, DI], BF16)
    w3bf = nc.dram_tensor("w3bf", [DI, DM], BF16)
    wqg = nc.dram_tensor("wqg", [DM, FFN_I], BF16)
    wqu = nc.dram_tensor("wqu", [DM, FFN_I], BF16)
    wqd = nc.dram_tensor("wqd", [FFN_I, DM], BF16)
    xnspill = nc.dram_tensor("xnspill", [TPC, DM], BF16, kind=dkind)
    xqspill = nc.dram_tensor("xqspill", [TPC, DM], BF16, kind=dkind)
    yspill = nc.dram_tensor("yspill", [TPC, DI], F32, kind=dkind)
    u2spill = nc.dram_tensor("u2spill", [TPC, DI], F32, kind=dkind)
    gspill = nc.dram_tensor("gspill", [TPC, DM], F32, kind=dkind)
    ccsrc = nc.dram_tensor("ccsrc", [DS, NH * HD], F32)
    ccgat = nc.dram_tensor("ccgat", [8 * DS, NH * HD], F32)

    es = contextlib.ExitStack()
    esg = es.enter_context

    v, sc, te, g, sy = nc.vector, nc.scalar, nc.tensor, nc.gpsimd, nc.sync
    s = Auto(nc, [(te, "PE"), (v, "DVE"), (sc, "ACT"), (g, "POOL"),
                  (sy, "SP")], esg)

    ones_c = esg(nc.sbuf_tensor("ones_c", [128, 1], F32))
    ones_r = esg(nc.sbuf_tensor("ones_r", [1, 128], F32))
    ones_rb = esg(nc.sbuf_tensor("ones_rb", [1, 128], BF16))
    id_f = esg(nc.sbuf_tensor("id_f", [128, 128], F32))
    id_b = esg(nc.sbuf_tensor("id_b", [128, 128], BF16))
    triu = esg(nc.sbuf_tensor("triu", [128, 128], F32))
    c127 = esg(nc.sbuf_tensor("c127", [128, 1], F32))
    a_col = esg(nc.sbuf_tensor("a_col", [128, 1], F32))
    A_bc = esg(nc.sbuf_tensor("A_bc", [128, NH], F32))
    convb_bf = esg(nc.sbuf_tensor("convb_bf", [1, CH], BF16))
    mcols = esg(nc.sbuf_tensor("mcols", [128, 6], F32))
    isx_buf = esg(nc.sbuf_tensor("isx_buf", [128, NT], F32))
    expa_buf = esg(nc.sbuf_tensor("expa_buf", [128, NT * NH], F32))
    CT_buf = esg(nc.sbuf_tensor("CT_buf", [DS, NT * 128], BF16))
    Ptot = esg(nc.sbuf_tensor("Ptot", [1, NT * NH], F32))
    Rrow = esg(nc.sbuf_tensor("Rrow", [1, NH], F32))
    state = esg(nc.sbuf_tensor("state", [DS, DI], F32))
    colA = esg(nc.sbuf_tensor("colA", [128, 24], F32))
    rowA = esg(nc.sbuf_tensor("rowA", [1, 64], F32))

    with nc.psum_tensor("pp", [128, 2048], F32) as pp, \
         nc.sbuf_tensor("wstage", [128, 4096], F32) as wstage, \
         nc.sbuf_tensor("qstage", [128, 4096], BF16) as qstage, \
         nc.sbuf_tensor("accc", [128, 1], F32) as accc, \
         nc.sbuf_tensor("vals", [1, 8], F32) as vals:

        s.op(v, v.memset, ones_c[:], 1.0)
        s.op(v, v.memset, ones_r[:], 1.0)
        s.op(sc, sc.copy, ones_rb[:], ones_r[:])
        s.op(v, v.memset, c127[:], 127.0)
        s.op(v, v.memset, id_f[:], 1.0)
        s.op(g, g.affine_select, id_f[:], id_f[:], [[-1, 128]], ALU.is_equal,
             0.0, base=0, channel_multiplier=1)
        s.op(sc, sc.copy, id_b[:], id_f[:])
        s.op(v, v.memset, triu[:], 1.0)
        s.op(g, g.affine_select, triu[:], triu[:], [[1, 128]], ALU.is_ge,
             0.0, base=0, channel_multiplier=-1)
        s.op(v, v.memset, Rrow[:], 0.0)
        s.op(v, v.memset, state[:], 0.0)

        s.dma(sy, vals[0:1, 0:1], prel)
        s.op(te, te.matmul, pp[:, 0:1], ones_r[:], vals[0:1, 0:1],
             start=True, stop=True)
        s.op(v, v.tensor_copy, a_col[:], pp[:, 0:1])
        s.dma(sy, rowA[0:1, 0:32], alog)
        s.op(sc, sc.activation, rowA[0:1, 32:64], rowA[0:1, 0:32], AF.Exp)
        s.op(v, v.tensor_scalar_mul, rowA[0:1, 32:64], rowA[0:1, 32:64], -1.0)
        s.op(te, te.matmul, pp[:, 0:NH], ones_r[:], rowA[0:1, 32:64],
             start=True, stop=True)
        s.op(v, v.tensor_copy, A_bc[:], pp[:, 0:NH])

        s.dma(sy, wstage[0:1, 0:CH], convb)
        s.op(sc, sc.copy, convb_bf[:], wstage[0:1, 0:CH])

        cw = convwt.rearrange("k (c p) i -> k c p i", p=128)
        cb = convw_bf.ap().rearrange("(c k p) i -> c k p i", k=4, p=128)
        for c in range(17):
            for k in range(4):
                s.dma(sy, wstage[:, 0:CH], cw[k, c])
                s.op(sc, sc.copy, qstage[:, 0:CH], wstage[:, 0:CH])
                s.dma(g, cb[c, k], qstage[:, 0:CH])

        for src, dst, rows, cols in ((w2t, w2bf, DM, DI), (w3t, w3bf, DI, DM)):
            sr = src.rearrange("(n p) f -> n p f", p=128)
            dr = dst.ap().rearrange("(n p) f -> n p f", p=128)
            for i in range(rows // 128):
                s.dma(sy, wstage[:, 0:cols], sr[i])
                s.op(sc, sc.copy, qstage[:, 0:cols], wstage[:, 0:cols])
                s.dma(g, dr[i], qstage[:, 0:cols])

        for wi, (src, dst, rows, cols) in enumerate(
                ((wgt, wqg, DM, FFN_I), (wut, wqu, DM, FFN_I),
                 (wdt, wqd, FFN_I, DM))):
            sr = src.rearrange("(n p) f -> n p f", p=128)
            dr = dst.ap().rearrange("(n p) f -> n p f", p=128)
            nchunks = rows // 128
            s.op(v, v.memset, accc[:], 0.0)
            for i in range(nchunks):
                s.dma(sy, wstage[:, 0:cols], sr[i])
                s.op(v, v.tensor_reduce, colA[:, 0:1], wstage[:, 0:cols],
                     AX.X, ALU.add, apply_absolute_value=True)
                s.op(v, v.tensor_add, accc[:], accc[:], colA[:, 0:1])
            s.op(te, te.matmul, pp[0:1, 0:1], ones_c[:], accc[:],
                 start=True, stop=True)
            s.op(v, v.tensor_copy, vals[0:1, 0:1], pp[0:1, 0:1])
            s.op(v, v.tensor_scalar, vals[0:1, 1:2], vals[0:1, 0:1],
                 1.0 / (rows * cols), 1e-5, ALU.mult, ALU.max)
            s.op(v, v.reciprocal, vals[0:1, 2:3], vals[0:1, 1:2])
            s.op(te, te.matmul, pp[:, 0:2], ones_r[:], vals[0:1, 1:3],
                 start=True, stop=True)
            s.op(v, v.tensor_copy, mcols[:, 2 * wi + 1:2 * wi + 2], pp[:, 0:1])
            s.op(v, v.tensor_copy, mcols[:, 2 * wi:2 * wi + 1], pp[:, 1:2])
            for i in range(nchunks):
                s.dma(sy, wstage[:, 0:cols], sr[i])
                s.op(v, v.tensor_scalar, wstage[:, 0:cols], wstage[:, 0:cols],
                     mcols[:, 2 * wi:2 * wi + 1], MAGIC, ALU.mult, ALU.add)
                s.op(v, v.tensor_scalar, wstage[:, 0:cols], wstage[:, 0:cols],
                     -MAGIC, 1.0, ALU.add, ALU.min)
                s.op(v, v.tensor_scalar, qstage[:, 0:cols], wstage[:, 0:cols],
                     -1.0, None, ALU.max)
                s.dma(g, dr[i], qstage[:, 0:cols])

    s.barrier()
    w1r = w1t.rearrange("(n p) f -> n p f", p=128)
    ur = u.rearrange("(t p) f -> t p f", p=128)

    es1 = contextlib.ExitStack()
    _ees1 = es1.enter_context
    pA = _ees1(nc.psum_tensor("pA", [128, 2048], F32))
    pB = _ees1(nc.psum_tensor("pB", [128, 1536], F32))
    pT = _ees1(nc.psum_tensor("pT", [128, 1024], BF16))
    w1bf = _ees1(nc.sbuf_tensor("w1bf", [128, 8 * NP], BF16))
    su2 = [_ees1(nc.sbuf_tensor(f"su{i}", [128, DM], F32)) for i in (0, 1)]
    squ = _ees1(nc.sbuf_tensor("squ", [128, DM], F32))
    xnf = _ees1(nc.sbuf_tensor("xnf", [128, DM], F32))
    xnb2 = [_ees1(nc.sbuf_tensor(f"xnb{i}", [128, DM], BF16)) for i in (0, 1)]
    xqb2 = [_ees1(nc.sbuf_tensor(f"xqb{i}", [128, DM], BF16)) for i in (0, 1)]
    xnT2 = [_ees1(nc.sbuf_tensor(f"xnT{i}", [128, 8 * 128], BF16))
            for i in (0, 1)]
    xbc2 = [_ees1(nc.sbuf_tensor(f"xbc_tm{i}", [128, CH], BF16))
            for i in (0, 1)]
    xfm = _ees1(nc.sbuf_tensor("xfm", [128, 17 * 515], BF16))
    xcbf = _ees1(nc.sbuf_tensor("xcbf", [128, TPM * CH], BF16))
    dt_mega = _ees1(nc.sbuf_tensor("dt_mega", [128, TPM * NH], F32))
    wpiece = _ees1(nc.sbuf_tensor("wpiece", [128, 2 * 4 * 512], BF16))
    tmp5a = _ees1(nc.sbuf_tensor("tmp5a", [128, 512], F32))
    tmp5b = _ees1(nc.sbuf_tensor("tmp5b", [128, 512], F32))
    t32a2 = [_ees1(nc.sbuf_tensor(f"t32a{i}", [128, NH], F32)) for i in (0, 1)]
    t32b2 = [_ees1(nc.sbuf_tensor(f"t32b{i}", [128, NH], F32)) for i in (0, 1)]
    asb2 = [_ees1(nc.sbuf_tensor(f"asb{i}", [128, NH], F32)) for i in (0, 1)]
    nasb2 = [_ees1(nc.sbuf_tensor(f"nasb{i}", [128, NH], F32)) for i in (0, 1)]
    atr2 = [_ees1(nc.sbuf_tensor(f"atr{i}", [32, 128], F32)) for i in (0, 1)]
    wtr2 = [_ees1(nc.sbuf_tensor(f"wtr{i}", [32, 128], F32)) for i in (0, 1)]
    wsb2 = [_ees1(nc.sbuf_tensor(f"wsb{i}", [128, NH], F32)) for i in (0, 1)]
    aflat = _ees1(nc.sbuf_tensor("aflat", [1, NH * 128], F32))
    totrow2 = [_ees1(nc.sbuf_tensor(f"totrow{i}", [1, NH], F32))
               for i in (0, 1)]
    dtrow2 = [_ees1(nc.sbuf_tensor(f"dtrow{i}", [1, NH], F32))
              for i in (0, 1)]
    dstage2 = [_ees1(nc.sbuf_tensor(f"dstage{i}", [128, 512], F32))
               for i in (0, 1)]
    Gbuf = _ees1(nc.sbuf_tensor("Gbuf", [128, NH * 128], BF16))
    BTs2 = [_ees1(nc.sbuf_tensor(f"BTs{i}", [DS, 128], BF16)) for i in (0, 1)]
    Ssb2 = [_ees1(nc.sbuf_tensor(f"Ssb{i}", [128, 128], BF16)) for i in (0, 1)]
    Yfin2 = [_ees1(nc.sbuf_tensor(f"Yfin{i}", [128, DI], F32))
             for i in (0, 1)]
    xw = _ees1(nc.sbuf_tensor("xw", [128, DI], BF16))
    statebf = _ees1(nc.sbuf_tensor("statebf", [DS, DI], BF16))
    if True:

        for i in range(8):
            su = su2[0]
            s.dma(sy, su[:], w1r[i][:, 0:DM])
            s.dma(sy, squ[:], w1r[i][:, DM:2 * DM])
            s.op(sc, sc.copy, w1bf[:, i * NP:i * NP + DM], su[:])
            s.op(sc, sc.copy, w1bf[:, i * NP + DM:i * NP + 2 * DM], squ[:])
            s.dma(sy, su[:, 0:NP - 2 * DM], w1r[i][:, 2 * DM:NP])
            s.op(sc, sc.copy, w1bf[:, i * NP + 2 * DM:(i + 1) * NP],
                 su[:, 0:NP - 2 * DM])

        def front_tile(t):
            halo = (t == 0)
            mega_tt = 0 if halo else (t - 1) % TPM
            pp = t % 2
            su = su2[pp]
            xnb, xqb, xnT, xbc_tm = xnb2[pp], xqb2[pp], xnT2[pp], xbc2[pp]
            cb = 8 + pp * 8
            s.dma(sy, su[:], ur[t])
            s.op(v, v.reduce_sum, colA[:, cb + 0:cb + 1], su[:], AX.X)
            s.op(v, v.tensor_scalar_mul, colA[:, cb + 1:cb + 2], colA[:, cb + 0:cb + 1], 1.0 / DM)
            s.op(sc, sc.square, squ[:], su[:])
            s.op(v, v.reduce_sum, colA[:, cb + 2:cb + 3], squ[:], AX.X)
            s.op(v, v.tensor_scalar_mul, colA[:, cb + 3:cb + 4], colA[:, cb + 2:cb + 3], 1.0 / DM)
            s.op(v, v.tensor_mul, colA[:, cb + 4:cb + 5], colA[:, cb + 1:cb + 2], colA[:, cb + 1:cb + 2])
            s.op(v, v.tensor_sub, colA[:, cb + 5:cb + 6], colA[:, cb + 3:cb + 4], colA[:, cb + 4:cb + 5])
            s.op(v, v.tensor_scalar_add, colA[:, cb + 5:cb + 6], colA[:, cb + 5:cb + 6], LN_EPS)
            s.op(sc, sc.sqrt, colA[:, cb + 6:cb + 7], colA[:, cb + 5:cb + 6])
            s.op(v, v.reciprocal, colA[:, cb + 6:cb + 7], colA[:, cb + 6:cb + 7])
            s.op(v, v.tensor_mul, colA[:, cb + 7:cb + 8], colA[:, cb + 1:cb + 2], colA[:, cb + 6:cb + 7])
            s.op(v, v.tensor_scalar_mul, colA[:, cb + 7:cb + 8], colA[:, cb + 7:cb + 8], -1.0)
            s.op(sc, sc.activation, xnf[:], su[:], AF.Identity,
                 bias=colA[:, cb + 7:cb + 8], scale=colA[:, cb + 6:cb + 7])
            s.op(sc, sc.copy, xnb[:], xnf[:])
            if not halo:
                s.dma(g, xnspill.ap().rearrange("(t p) f -> t p f", p=128)[t - 1],
                      xnb[:])
                s.op(v, v.tensor_reduce, colA[:, cb + 0:cb + 1], xnf[:], AX.X, ALU.max,
                     apply_absolute_value=True)
                s.op(v, v.tensor_scalar_max, colA[:, cb + 0:cb + 1], colA[:, cb + 0:cb + 1], 1e-5)
                s.op(v, v.reciprocal, colA[:, cb + 1:cb + 2], colA[:, cb + 0:cb + 1])
                s.op(v, v.tensor_scalar_mul, colA[:, cb + 1:cb + 2], colA[:, cb + 1:cb + 2],
                     127.0)
                s.op(v, v.tensor_scalar_mul, isx_buf[:, t - 1:t],
                     colA[:, cb + 0:cb + 1], 1.0 / 127.0)
                s.op(v, v.tensor_scalar, squ[:], xnf[:], colA[:, cb + 1:cb + 2], MAGIC,
                     ALU.mult, ALU.add)
                s.op(v, v.tensor_scalar, squ[:], squ[:], -MAGIC, 127.0,
                     ALU.add, ALU.min)
                s.op(v, v.tensor_scalar, xqb[:], squ[:], -128.0, None, ALU.max)
                s.dma(g, xqspill.ap().rearrange("(t p) f -> t p f", p=128)[t - 1],
                      xqb[:])
            for k in range(8):
                s.op(te, te.transpose, pT[:, k * 128:(k + 1) * 128],
                     xnb[:, k * 128:(k + 1) * 128], id_b[:])
            s.op(sc, sc.copy, xnT[:], pT[:, 0:1024])
            for j, (o0, on) in enumerate(IN_CH):
                for k in range(8):
                    s.op(te, te.matmul, pA[:, 0:on],
                         xnT[:, k * 128:(k + 1) * 128],
                         w1bf[:, k * NP + o0:k * NP + o0 + on],
                         start=(k == 0), stop=(k == 7))
                if o0 < 2048:
                    s.op(sc, sc.copy, xbc_tm[:, o0:o0 + on], pA[:, 0:on])
                else:
                    s.op(sc, sc.copy, xbc_tm[:, 2048:2176], pA[:, 0:128])
                    s.op(v, v.tensor_copy,
                         dt_mega[:, mega_tt * NH:(mega_tt + 1) * NH],
                         pA[:, 128:160])
            xfmv = xfm[:].rearrange("p (c f) -> p c f", c=17)
            pTv = pT[:, 0:1024].rearrange("p (s f) -> p s f", s=8)
            for c0 in range(0, 17, 8):
                ng = min(8, 17 - c0)
                for i in range(ng):
                    s.op(te, te.transpose, pT[:, i * 128:(i + 1) * 128],
                         xbc_tm[:, (c0 + i) * 128:(c0 + i + 1) * 128], id_b[:])
                if halo:
                    s.op(sc, sc.copy, xfmv[:, c0:c0 + ng, 0:3],
                         pTv[:, 0:ng, 125:128])
                else:
                    s.op(sc, sc.copy,
                         xfmv[:, c0:c0 + ng, 3 + mega_tt * 128:
                              3 + (mega_tt + 1) * 128],
                         pTv[:, 0:ng, :])

        def conv_mega(m):
            cbr = convw_bf.ap().rearrange("(c k p) i -> c p k i", k=4, p=128)
            for j, (o0, on) in enumerate(CONV_CH):
                for c in range(17):
                    buf = (c % 2) * 2048
                    s.dma(sy if c % 2 == 0 else g,
                          wpiece[:, buf:buf + 4 * on].rearrange(
                              "p (k f) -> p k f", k=4),
                          cbr[c][:, :, o0:o0 + on])
                    for k in range(4):
                        for tt in range(TPM):
                            s.op(te, te.matmul,
                                 pA[:, tt * 512:tt * 512 + on],
                                 xfm[:, c * 515 + k + tt * 128:
                                     c * 515 + k + tt * 128 + 128],
                                 wpiece[:, buf + k * on:buf + (k + 1) * on],
                                 start=(c == 0 and k == 0), stop=False)
                for tt in range(TPM):
                    s.op(te, te.matmul, pA[:, tt * 512:tt * 512 + on],
                         ones_rb[:], convb_bf[0:1, o0:o0 + on],
                         start=False, stop=True)
                    s.op(v, v.tensor_scalar, tmp5a[:, 0:on],
                         pA[:, tt * 512:tt * 512 + on], 0.0, a_col[:],
                         ALU.min, ALU.mult)
                    s.op(v, v.tensor_scalar, tmp5b[:, 0:on],
                         pA[:, tt * 512:tt * 512 + on], 0.0, None, ALU.max)
                    s.op(v, v.tensor_add,
                         xcbf[:, tt * CH + o0:tt * CH + o0 + on],
                         tmp5a[:, 0:on], tmp5b[:, 0:on])

        def ssd_tile(t):
            tt = (t - 1) % TPM
            pp = (t - 1) % 2
            t32a, t32b, asb, nasb = (t32a2[pp], t32b2[pp], asb2[pp],
                                     nasb2[pp])
            atr, wtr, wsb = atr2[pp], wtr2[pp], wsb2[pp]
            totrow, dtrow, dstage = totrow2[pp], dtrow2[pp], dstage2[pp]
            BTs, Ssb, Yfin = BTs2[pp], Ssb2[pp], Yfin2[pp]
            xc = xcbf[:, tt * CH:(tt + 1) * CH]
            dtr = dt_mega[:, tt * NH:(tt + 1) * NH]
            s.op(sc, sc.activation, t32a[:], dtr, AF.Abs)
            s.op(sc, sc.activation, t32a[:], t32a[:], AF.Exp, bias=0.0,
                 scale=-1.0)
            s.op(sc, sc.activation, t32a[:], t32a[:], AF.Ln, bias=1.0)
            s.op(v, v.tensor_scalar_max, t32b[:], dtr, 0.0)
            s.op(v, v.tensor_add, t32a[:], t32a[:], t32b[:])
            s.op(v, v.tensor_mul, t32a[:], t32a[:], A_bc[:])
            s.op(te, te.matmul, pB[:, 1024:1024 + NH], triu[:], t32a[:],
                 start=True, stop=True)
            s.op(te, te.matmul, pB[0:1, 1088:1088 + NH], ones_c[:], t32a[:],
                 start=True, stop=True)
            s.op(v, v.tensor_copy, asb[:], pB[:, 1024:1024 + NH])
            s.op(v, v.tensor_copy, totrow[:], pB[0:1, 1088:1088 + NH])
            s.op(v, v.tensor_scalar_mul, nasb[:], asb[:], -1.0)
            s.op(sc, sc.activation, expa_buf[:, (t - 1) * NH:t * NH], asb[:],
                 AF.Exp)
            s.op(v, v.tensor_copy, Ptot[0:1, (t - 1) * NH:t * NH], Rrow[:])
            s.op(v, v.tensor_add, Rrow[:], Rrow[:], totrow[:])
            s.op(te, te.transpose, pB[0:32, 1152:1280], asb[:], id_f[:])
            s.op(v, v.tensor_copy, atr[:], pB[0:32, 1152:1280])
            s.op(sc, sc.activation, wtr[:], atr[:], AF.Exp,
                 bias=atr[:, 127:128], scale=-1.0)
            s.op(te, te.transpose, pB[:, 1280:1280 + NH], wtr[:],
                 id_f[0:32, 0:32])
            s.op(v, v.tensor_copy, wsb[:], pB[:, 1280:1280 + NH])
            s.dma(sy, aflat[0:1, :].rearrange("o (p f) -> o p f", p=32),
                  atr[:])
            for q in range(8):
                po = (q % 2) * 512
                s.op(te, te.matmul, pB[:, po:po + 512], ones_r[:],
                     aflat[0:1, q * 512:(q + 1) * 512], start=True, stop=True)
                for i in range(4):
                    s.op(v, v.tensor_scalar, dstage[:, i * 128:(i + 1) * 128],
                         pB[:, po + i * 128:po + (i + 1) * 128],
                         nasb[:, 4 * q + i:4 * q + i + 1], 0.0,
                         ALU.add, ALU.min)
                s.op(sc, sc.activation, Gbuf[:, q * 512:(q + 1) * 512],
                     dstage[:], AF.Exp)
            s.op(te, te.transpose, pT[0:64, 0:128], xc[:, 2048:2112], id_b[:])
            s.op(v, v.tensor_copy, BTs[:], pT[0:64, 0:128])
            s.op(te, te.transpose, pT[0:64, 128:256], xc[:, 2112:2176], id_b[:])
            s.op(v, v.tensor_copy, CT_buf[:, (t - 1) * 128:t * 128],
                 pT[0:64, 128:256])
            s.op(te, te.matmul, pB[:, 1024:1152], BTs[:],
                 CT_buf[:, (t - 1) * 128:t * 128], start=True, stop=True)
            s.op(v, v.tensor_copy, Ssb[:], pB[:, 1024:1152])
            s.op(g, g.affine_select, Ssb[:], Ssb[:], [[1, 128]], ALU.is_ge,
                 0.0, base=0, channel_multiplier=-1)
            s.op(v, v.tensor_tensor,
                 Gbuf[:].rearrange("p (h l) -> p h l", h=NH),
                 Gbuf[:].rearrange("p (h l) -> p h l", h=NH),
                 Ssb[:].unsqueeze(1).broadcast_to([128, NH, 128]), ALU.mult)
            for h in range(NH):
                s.op(te, te.matmul, pA[:, h * HD:(h + 1) * HD],
                     Gbuf[:, h * 128:(h + 1) * 128],
                     xc[:, h * HD:(h + 1) * HD], start=True, stop=True)
            s.op(v, v.tensor_copy, Yfin[:], pA[:, 0:DI])
            s.op(sc, sc.copy, statebf[:], state[:])
            for q in range(4):
                s.op(te, te.matmul, pA[:, q * 512:(q + 1) * 512],
                     CT_buf[:, (t - 1) * 128:t * 128],
                     statebf[:, q * 512:(q + 1) * 512], start=True, stop=True)
            for h in range(NH):
                s.op(v, v.scalar_tensor_tensor, Yfin[:, h * HD:(h + 1) * HD],
                     pA[:, h * HD:(h + 1) * HD],
                     expa_buf[:, (t - 1) * NH + h:(t - 1) * NH + h + 1],
                     Yfin[:, h * HD:(h + 1) * HD], ALU.mult, ALU.add)
            s.dma(g, yspill.ap().rearrange("(t p) f -> t p f", p=128)[t - 1],
                  Yfin[:])
            for h in range(NH):
                s.op(v, v.tensor_scalar, xw[:, h * HD:(h + 1) * HD],
                     xc[:, h * HD:(h + 1) * HD], wsb[:, h:h + 1], None,
                     ALU.mult)
            for q in range(4):
                s.op(te, te.matmul, pA[64:128, q * 512:(q + 1) * 512],
                     xc[:, 2048:2112], xw[:, q * 512:(q + 1) * 512],
                     start=True, stop=True)
            s.op(sc, sc.activation, dtrow[:], totrow[:], AF.Exp)
            for h in range(NH):
                s.op(v, v.tensor_copy, aflat[0:1, h * HD:(h + 1) * HD],
                     dtrow[0:1, h:h + 1].broadcast_to([1, HD]))
            for q in range(4):
                s.op(te, te.matmul, pA[0:64, q * 512:(q + 1) * 512],
                     ones_r[0:1, 0:64], aflat[0:1, q * 512:(q + 1) * 512],
                     start=True, stop=True)
            s.op(v, v.tensor_tensor, state[:], state[:], pA[0:64, 0:DI],
                 ALU.mult)
            s.op(v, v.tensor_tensor, state[:], state[:], pA[64:128, 0:DI],
                 ALU.add)

        if n_megas > 0:
            front_tile(0)
        for m in range(n_megas):
            if m > 0:
                s.op(v, v.tensor_copy,
                     xfm[:, 0:17 * 515].rearrange(
                         "p (c f) -> p c f", c=17)[:, :, 0:3],
                     xfm[:, 0:17 * 515].rearrange(
                         "p (c f) -> p c f", c=17)[:, :, 512:515])
            for tt in range(TPM):
                front_tile(m * TPM + tt + 1)
            if "c" in parts:
                conv_mega(m)
            if "s" in parts:
                for tt in range(TPM):
                    ssd_tile(m * TPM + tt + 1)

        if n_megas > 0:
            s.dma(sy, ccsrc.ap(), state[:])
        if do_cc:
            s._deps(g, ("cc",), [ccsrc.ap()], [ccgat.ap()])
            g.collective_compute(
                "AllGather", ALU.bypass, [[0, 1, 2, 3, 4, 5, 6, 7]],
                [ccsrc.ap().opt()], [ccgat.ap().opt()],
            ).then_inc(s.cc_sem)
            s._record(("cc",), 1, [ccsrc.ap()], [ccgat.ap()])

    es1.close()
    s.barrier()
    if not full:
        s.fence(sy)
        es.close()
        nc.finalize()
        return nc

    FM = 8
    es2 = contextlib.ExitStack()
    _ees2 = es2.enter_context
    pC = _ees2(nc.psum_tensor("pC", [128, 2048], F32))
    pU = _ees2(nc.psum_tensor("pU", [128, 1024], BF16))
    w2p = _ees2(nc.sbuf_tensor("w2p", [128, 8 * 512], BF16))
    xnbt = _ees2(nc.sbuf_tensor("xnbt", [128, DM], BF16))
    xnT4 = _ees2(nc.sbuf_tensor("xnT4", [128, TPM * DM], BF16))
    xqT = _ees2(nc.sbuf_tensor("xqT", [128, TPM * DM], BF16))
    pmega = _ees2(nc.sbuf_tensor("pmega", [128, TPM * FFN_I], F32))
    gw = _ees2(nc.sbuf_tensor("gw", [128, 2 * 8 * 512], BF16))
    gstage = _ees2(nc.sbuf_tensor("gstage", [128, 512], F32))
    ustage = _ees2(nc.sbuf_tensor("ustage", [128, 512], F32))
    scg = _ees2(nc.sbuf_tensor("scg", [128, 3 * TPM], F32))
    xq2b = _ees2(nc.sbuf_tensor("xq2b", [128, FFN_I], BF16))
    xq2T = _ees2(nc.sbuf_tensor("xq2T", [128, TPM * FFN_I], BF16))
    dw = _ees2(nc.sbuf_tensor("dw", [128, 2 * 4096], BF16))
    gout = _ees2(nc.sbuf_tensor("gout", [128, DM], F32))
    if True:

        w2r = w2bf.ap().rearrange("(n p) f -> p n f", p=128)
        xnsp = xnspill.ap().rearrange("(t p) f -> t p f", p=128)
        xqsp = xqspill.ap().rearrange("(t p) f -> t p f", p=128)
        u2sp = u2spill.ap().rearrange("(t p) f -> t p f", p=128)
        gsp = gspill.ap().rearrange("(t p) f -> t p f", p=128)
        wgr = wqg.ap().rearrange("(n p) f -> p n f", p=128)
        wur = wqu.ap().rearrange("(n p) f -> p n f", p=128)
        wdr = wqd.ap().rearrange("(n p) f -> p n f", p=128)

        for fm in range(FM):
            for tt in range(TPM):
                t = fm * TPM + tt + 1
                s.dma(sy, xnbt[:], xnsp[t - 1])
                for k in range(8):
                    s.op(te, te.transpose, pU[:, k * 128:(k + 1) * 128],
                         xnbt[:, k * 128:(k + 1) * 128], id_b[:])
                s.op(sc, sc.copy, xnT4[:, tt * DM:(tt + 1) * DM], pU[:])
                s.dma(sy, xnbt[:], xqsp[t - 1])
                for k in range(8):
                    s.op(te, te.transpose, pU[:, k * 128:(k + 1) * 128],
                         xnbt[:, k * 128:(k + 1) * 128], id_b[:])
                s.op(sc, sc.copy, xqT[:, tt * DM:(tt + 1) * DM], pU[:])
                s.op(v, v.tensor_mul, scg[:, 3 * tt:3 * tt + 1],
                     isx_buf[:, t - 1:t], mcols[:, 1:2])
                s.op(v, v.tensor_mul, scg[:, 3 * tt + 1:3 * tt + 2],
                     isx_buf[:, t - 1:t], mcols[:, 3:4])
            for j in range(4):
                s.dma(sy, w2p[:].rearrange("p (n f) -> p n f", n=8),
                      w2r[:, :, j * 512:(j + 1) * 512])
                for tt in range(TPM):
                    t = fm * TPM + tt + 1
                    for k in range(8):
                        s.op(te, te.matmul, pC[:, tt * 512:(tt + 1) * 512],
                             xnT4[:, tt * DM + k * 128:tt * DM + (k + 1) * 128],
                             w2p[:, k * 512:(k + 1) * 512],
                             start=(k == 0), stop=(k == 7))
                    s.op(sc, sc.activation, gstage[:],
                         pC[:, tt * 512:(tt + 1) * 512], AF.Gelu)
                    s.dma(g, u2sp[t - 1][:, j * 512:(j + 1) * 512], gstage[:])
            for j in range(8):
                bo = (j % 2) * 4096
                s.dma(sy if j % 2 == 0 else g, gw[:, bo:bo + 4096].rearrange(
                    "p (n f) -> p n f", n=8), wgr[:, :, j * 512:(j + 1) * 512])
                for tt in range(TPM):
                    for k in range(8):
                        s.op(te, te.matmul, pC[:, tt * 512:(tt + 1) * 512],
                             xqT[:, tt * DM + k * 128:tt * DM + (k + 1) * 128],
                             gw[:, bo + k * 512:bo + (k + 1) * 512],
                             start=(k == 0), stop=(k == 7))
                s.dma(g if j % 2 == 0 else sy, gw[:, bo:bo + 4096].rearrange(
                    "p (n f) -> p n f", n=8), wur[:, :, j * 512:(j + 1) * 512])
                for tt in range(TPM):
                    s.op(sc, sc.activation, gstage[:],
                         pC[:, tt * 512:(tt + 1) * 512], AF.Silu,
                         bias=0.0, scale=scg[:, 3 * tt:3 * tt + 1])
                    for k in range(8):
                        s.op(te, te.matmul, pC[:, tt * 512:(tt + 1) * 512],
                             xqT[:, tt * DM + k * 128:tt * DM + (k + 1) * 128],
                             gw[:, bo + k * 512:bo + (k + 1) * 512],
                             start=(k == 0), stop=(k == 7))
                    s.op(sc, sc.activation, ustage[:],
                         pC[:, tt * 512:(tt + 1) * 512], AF.Copy,
                         bias=0.0, scale=scg[:, 3 * tt + 1:3 * tt + 2])
                    s.op(v, v.tensor_mul,
                         pmega[:, tt * FFN_I + j * 512:
                               tt * FFN_I + (j + 1) * 512],
                         gstage[:], ustage[:])
            for tt in range(TPM):
                t = fm * TPM + tt + 1
                pm = pmega[:, tt * FFN_I:(tt + 1) * FFN_I]
                s.op(v, v.tensor_reduce, colA[:, 0:1], pm, AX.X, ALU.max,
                     apply_absolute_value=True)
                s.op(v, v.tensor_scalar_max, colA[:, 0:1], colA[:, 0:1], 1e-5)
                s.op(v, v.reciprocal, colA[:, 1:2], colA[:, 0:1])
                s.op(v, v.tensor_scalar_mul, colA[:, 1:2], colA[:, 1:2],
                     127.0)
                s.op(v, v.tensor_scalar_mul, colA[:, 2:3], colA[:, 0:1],
                     1.0 / 127.0)
                s.op(v, v.tensor_mul, scg[:, 3 * tt + 2:3 * tt + 3],
                     colA[:, 2:3], mcols[:, 5:6])
                s.op(v, v.tensor_scalar, pm, pm, colA[:, 1:2], MAGIC,
                     ALU.mult, ALU.add)
                s.op(v, v.tensor_scalar, pm, pm, -MAGIC, 127.0,
                     ALU.add, ALU.min)
                s.op(v, v.tensor_scalar, xq2b[:], pm, -128.0, None,
                     ALU.max)
                for k0 in range(0, 32, 8):
                    for i in range(8):
                        s.op(te, te.transpose, pU[:, i * 128:(i + 1) * 128],
                             xq2b[:, (k0 + i) * 128:(k0 + i + 1) * 128],
                             id_b[:])
                    s.op(sc, sc.copy,
                         xq2T[:, tt * FFN_I + k0 * 128:
                              tt * FFN_I + (k0 + 8) * 128], pU[:])
            for hp in range(2):
                for kb in range(8):
                    bo = (kb % 2) * 4096
                    s.dma(sy if kb % 2 == 0 else g,
                          dw[:, bo:bo + 4096].rearrange(
                              "p (n f) -> p n f", n=4),
                          wdr[:, 4 * kb:4 * kb + 4, :])
                    for kk in range(4):
                        k = 4 * kb + kk
                        for tt in (2 * hp, 2 * hp + 1):
                            po = (tt % 2) * 1024
                            for q in range(2):
                                s.op(te, te.matmul,
                                     pC[:, po + q * 512:po + (q + 1) * 512],
                                     xq2T[:, tt * FFN_I + k * 128:
                                          tt * FFN_I + (k + 1) * 128],
                                     dw[:, bo + kk * 1024 + q * 512:
                                        bo + kk * 1024 + (q + 1) * 512],
                                     start=(k == 0), stop=(k == 31))
                for tt in (2 * hp, 2 * hp + 1):
                    t = fm * TPM + tt + 1
                    po = (tt % 2) * 1024
                    s.op(sc, sc.activation, gout[:], pC[:, po:po + 1024],
                         AF.Silu, bias=0.0, scale=scg[:, 3 * tt + 2:3 * tt + 3])
                    s.dma(g, gsp[t - 1], gout[:])

    es2.close()
    s.barrier()
    es3 = contextlib.ExitStack()
    _ees3 = es3.enter_context
    pE = _ees3(nc.psum_tensor("pE", [128, 2048], F32))
    pF = _ees3(nc.psum_tensor("pF", [128, 1024], F32))
    pV = _ees3(nc.psum_tensor("pV", [128, 1024], BF16))
    w3sb = _ees3(nc.sbuf_tensor("w3sb", [128, 16 * DM], BF16))
    gat = _ees3(nc.sbuf_tensor("gat", [DS, DI], F32))
    rmb = _ees3(nc.sbuf_tensor("rmb", [DS, 8], F32))
    sin_a = _ees3(nc.sbuf_tensor("sin_a", [DS, DI], F32))
    sin_b = _ees3(nc.sbuf_tensor("sin_b", [DS, DI], F32))
    sinbf = _ees3(nc.sbuf_tensor("sinbf", [DS, DI], BF16))
    lngbc = _ees3(nc.sbuf_tensor("lngbc", [128, DI], F32))
    lnbbc = _ees3(nc.sbuf_tensor("lnbbc", [128, DI], F32))
    yt = _ees3(nc.sbuf_tensor("yt", [128, DI], F32))
    u2t = _ees3(nc.sbuf_tensor("u2t", [128, DI], F32))
    gt = _ees3(nc.sbuf_tensor("gt", [128, DM], F32))
    ut = _ees3(nc.sbuf_tensor("ut", [128, DM], F32))
    h2 = _ees3(nc.sbuf_tensor("h2", [128, DI], F32))
    sq2 = _ees3(nc.sbuf_tensor("sq2", [128, DI], F32))
    hnb = _ees3(nc.sbuf_tensor("hnb", [128, DI], BF16))
    hnT = _ees3(nc.sbuf_tensor("hnT", [128, 16 * 128], BF16))
    Fsb = _ees3(nc.sbuf_tensor("Fsb", [128, NH], F32))
    prow = _ees3(nc.sbuf_tensor("prow", [1, NH], F32))
    ssb = _ees3(nc.sbuf_tensor("ssb", [128, DM], F32))
    osb = _ees3(nc.sbuf_tensor("osb", [128, DM], F32))
    if True:

        w3r = w3bf.ap().rearrange("(n p) f -> n p f", p=128)
        for i in range(16):
            s.dma(sy, w3sb[:, i * DM:(i + 1) * DM], w3r[i])
        s.dma(sy, h2[0:1, :], lng)
        s.dma(sy, sq2[0:1, :], lnb)
        for q in range(4):
            s.op(te, te.matmul, pE[:, q * 512:(q + 1) * 512], ones_r[:],
                 h2[0:1, q * 512:(q + 1) * 512], start=True, stop=True)
            s.op(te, te.matmul, pF[:, (q % 2) * 512:(q % 2) * 512 + 512],
                 ones_r[:], sq2[0:1, q * 512:(q + 1) * 512],
                 start=True, stop=True)
            s.op(v, v.tensor_copy, lngbc[:, q * 512:(q + 1) * 512],
                 pE[:, q * 512:(q + 1) * 512])
            s.op(v, v.tensor_copy, lnbbc[:, q * 512:(q + 1) * 512],
                 pF[:, (q % 2) * 512:(q % 2) * 512 + 512])
        s.dma(sy, rmb[0:1, 0:8], rmask)
        s.op(te, te.matmul, pF[0:DS, 0:8], ones_r[0:1, 0:DS], rmb[0:1, 0:8],
             start=True, stop=True)
        s.op(v, v.tensor_copy, rmb[:], pF[0:DS, 0:8])
        s.op(v, v.memset, sin_a[:], 0.0)
        gr = ccgat.ap().rearrange("(r n) f -> r n f", n=DS)
        cur, nxt = sin_a, sin_b
        for r in range(8):
            s.dma(sy, gat[:], gr[r])
            s.op(v, v.scalar_tensor_tensor, nxt[:],
                 gat[:], rmb[:, r:r + 1], cur[:],
                 ALU.mult, ALU.add)
            cur, nxt = nxt, cur
        s.op(sc, sc.copy, sinbf[:], cur[:])

        ysp = yspill.ap().rearrange("(t p) f -> t p f", p=128)
        u2sp = u2spill.ap().rearrange("(t p) f -> t p f", p=128)
        gsp = gspill.ap().rearrange("(t p) f -> t p f", p=128)
        orr = o.rearrange("(t p) f -> t p f", p=128)

        for t in range(1, NT + 1):
            s.dma(sy, yt[:], ysp[t - 1])
            s.dma(g, u2t[:], u2sp[t - 1])
            s.dma(sy, gt[:], gsp[t - 1])
            s.dma(g, ut[:], ur[t])
            s.op(sc, sc.activation, prow[:], Ptot[0:1, (t - 1) * NH:t * NH],
                 AF.Exp)
            s.op(te, te.matmul, pF[:, 0:NH], ones_r[:], prow[:],
                 start=True, stop=True)
            s.op(v, v.tensor_tensor, Fsb[:], pF[:, 0:NH],
                 expa_buf[:, (t - 1) * NH:t * NH], ALU.mult)
            for q in range(4):
                s.op(te, te.matmul, pE[:, q * 512:(q + 1) * 512],
                     CT_buf[:, (t - 1) * 128:t * 128],
                     sinbf[:, q * 512:(q + 1) * 512], start=True, stop=True)
            for h in range(NH):
                s.op(v, v.scalar_tensor_tensor, yt[:, h * HD:(h + 1) * HD],
                     pE[:, h * HD:(h + 1) * HD], Fsb[:, h:h + 1],
                     yt[:, h * HD:(h + 1) * HD], ALU.mult, ALU.add)
            s.op(v, v.tensor_mul, h2[:], u2t[:], yt[:])
            s.op(v, v.reduce_sum, colA[:, 0:1], h2[:], AX.X)
            s.op(v, v.tensor_scalar_mul, colA[:, 1:2], colA[:, 0:1], 1.0 / DI)
            s.op(sc, sc.square, sq2[:], h2[:])
            s.op(v, v.reduce_sum, colA[:, 2:3], sq2[:], AX.X)
            s.op(v, v.tensor_scalar_mul, colA[:, 3:4], colA[:, 2:3], 1.0 / DI)
            s.op(v, v.tensor_mul, colA[:, 4:5], colA[:, 1:2], colA[:, 1:2])
            s.op(v, v.tensor_sub, colA[:, 5:6], colA[:, 3:4], colA[:, 4:5])
            s.op(v, v.tensor_scalar_add, colA[:, 5:6], colA[:, 5:6], LN_EPS)
            s.op(sc, sc.sqrt, colA[:, 6:7], colA[:, 5:6])
            s.op(v, v.reciprocal, colA[:, 6:7], colA[:, 6:7])
            s.op(v, v.tensor_mul, colA[:, 7:8], colA[:, 1:2], colA[:, 6:7])
            s.op(v, v.tensor_scalar_mul, colA[:, 7:8], colA[:, 7:8], -1.0)
            s.op(sc, sc.activation, sq2[:], h2[:], AF.Identity,
                 bias=colA[:, 7:8], scale=colA[:, 6:7])
            s.op(v, v.tensor_mul, sq2[:], sq2[:], lngbc[:])
            s.op(v, v.tensor_add, hnb[:], sq2[:], lnbbc[:])
            for k0 in range(0, 16, 8):
                for i in range(8):
                    s.op(te, te.transpose, pV[:, i * 128:(i + 1) * 128],
                         hnb[:, (k0 + i) * 128:(k0 + i + 1) * 128], id_b[:])
                s.op(sc, sc.copy, hnT[:, k0 * 128:(k0 + 8) * 128], pV[:])
            for j in range(2):
                for k in range(16):
                    s.op(te, te.matmul, pF[:, j * 512:(j + 1) * 512],
                         hnT[:, k * 128:(k + 1) * 128],
                         w3sb[:, k * DM + j * 512:k * DM + (j + 1) * 512],
                         start=(k == 0), stop=(k == 15))
            s.op(v, v.tensor_copy, ssb[:], pF[:, 0:DM])
            s.op(v, v.tensor_mul, osb[:], ssb[:], gt[:])
            s.op(v, v.tensor_add, osb[:], osb[:], ut[:])
            s.dma(g, orr[t - 1], osb[:])

        s.fence(sy)
    es3.close()
    es.close()
    nc.finalize()
    return nc


def make_in_maps(inputs):
    u_full = np.ascontiguousarray(inputs["L0_attn_u"], dtype=np.float32)
    w1t = np.ascontiguousarray(
        inputs["L0_attn_W_in1"].T[:, 2048:2048 + NP], dtype=np.float32)
    convwt = np.ascontiguousarray(
        np.transpose(inputs["L0_attn_conv_w"], (2, 1, 0)), dtype=np.float32)
    convb = np.ascontiguousarray(
        inputs["L0_attn_conv_b"].reshape(1, CH), dtype=np.float32)
    alog = np.ascontiguousarray(
        inputs["L0_attn_A_log"].reshape(1, NH), dtype=np.float32)
    prel = np.ascontiguousarray(
        np.asarray(inputs["L0_attn_prelu_a"]).reshape(1, 1), dtype=np.float32)
    w2t = np.ascontiguousarray(inputs["L0_attn_W_in2"].T, dtype=np.float32)
    w3t = np.ascontiguousarray(inputs["L0_attn_W_in3"].T, dtype=np.float32)
    wgt = np.ascontiguousarray(inputs["L0_ffn_w_gate"].T, dtype=np.float32)
    wut = np.ascontiguousarray(inputs["L0_ffn_w_up"].T, dtype=np.float32)
    wdt = np.ascontiguousarray(inputs["L0_ffn_w_down"].T, dtype=np.float32)
    lng = np.ascontiguousarray(
        inputs["L0_attn_ln_g"].reshape(1, DI), dtype=np.float32)
    lnb = np.ascontiguousarray(
        inputs["L0_attn_ln_b"].reshape(1, DI), dtype=np.float32)

    in_maps = []
    for c in range(8):
        b, half = c % 4, c // 4
        span = u_full[b, half * TPC:(half + 1) * TPC]
        head = np.zeros((128, DM), dtype=np.float32)
        if half == 1:
            head[125:128] = u_full[b, TPC - 3:TPC]
        ucore = np.ascontiguousarray(np.concatenate([head, span], axis=0))
        rmask = np.zeros((1, 8), dtype=np.float32)
        if half == 1:
            rmask[0, b] = 1.0
        in_maps.append(dict(
            u=ucore, w1t=w1t, convwt=convwt, convb=convb, alog=alog,
            prel=prel, w2t=w2t, w3t=w3t, wgt=wgt, wut=wut, wdt=wdt,
            lng=lng, lnb=lnb, rmask=rmask))
    return in_maps


def kernel(**inputs):
    nc = build_program()
    in_maps = make_in_maps(inputs)
    res = run_bass_kernel_spmd(nc, in_maps, core_ids=list(range(8)))
    out = np.empty((4, 2 * TPC, DM), dtype=np.float32)
    for c in range(8):
        b, half = c % 4, c // 4
        out[b, half * TPC:(half + 1) * TPC] = np.asarray(res.results[c]["o"])
    return out


# version 06
# speedup vs baseline: 1.0515x; 1.0515x over previous
import contextlib
import numpy as np
import concourse.bass as bass
import concourse.mybir as mybir
from concourse.bass_utils import run_bass_kernel_spmd

F32 = mybir.dt.float32
BF16 = mybir.dt.bfloat16
AF = mybir.ActivationFunctionType
ALU = mybir.AluOpType
AX = mybir.AxisListType

MAGIC = float(np.float32(1.5 * 2 ** 23))

DM = 1024
DI = 2048
DS = 64
HD = 64
NH = 32
CH = 2176
NP = 2208
TPC = 4096
NT = 32
TPM = 4
NMEGA = 8
LN_EPS = 1e-5
FFN_I = 4096

IN_CH = [(0, 512), (512, 512), (1024, 512), (1536, 512), (2048, 160)]
CONV_CH = [(0, 512), (512, 512), (1024, 512), (1536, 512), (2048, 128)]


class Ser:

    def __init__(self, sems):
        self.sems = sems
        self.idx = 0
        self.count = 0
        self.prev = None

    def _pre(self, eng):
        if self.prev is not None:
            eng.wait_ge(*self.prev)
            self.prev = None
        elif self.count > 0:
            eng.wait_ge(self.sems[self.idx], self.count)

    def _post(self, inst, inc):
        inst.then_inc(self.sems[self.idx], inc)
        self.count += inc
        if self.count > 40000:
            self.prev = (self.sems[self.idx], self.count)
            self.idx += 1
            self.count = 0

    def dma(self, eng, out, in_):
        self._pre(eng)
        self._post(eng.dma_start(out=out, in_=in_), 16)

    def op(self, eng, fn, *args, **kw):
        self._pre(eng)
        self._post(fn(*args, **kw), 1)

    def fence(self, eng):
        if self.prev is not None:
            eng.wait_ge(*self.prev)
        elif self.count > 0:
            eng.wait_ge(self.sems[self.idx], self.count)


FULLRANGE_TENSORS = set()


def _ap_range(ap):
    t = ap.tensor
    if t.name in FULLRANGE_TENSORS:
        return (0, 1 << 20, 0, 1 << 40)
    off = int(ap.offset)
    pairs = [(int(p[0]), int(p[1])) for p in ap.ap]
    if type(t).__name__.startswith("DRam"):
        hi = off + sum(st * (ct - 1) for st, ct in pairs) + 1
        return (0, 1, off, hi)
    shape = list(t.shape)
    rs = 1
    for d in shape[1:]:
        rs *= d
    p0 = off // rs
    f0 = off % rs
    pspan = 1
    fhi = f0
    for st, ct in pairs:
        ext = st * (ct - 1)
        if st != 0 and st % rs == 0:
            pspan += ext // rs
        else:
            fhi += ext
    fhi += 1
    if type(t).__name__.startswith("PSum"):
        be = 2048 // (2 if t.dtype == BF16 else 4)
        f0 = (f0 // be) * be
        fhi = ((fhi + be - 1) // be) * be
        return (0, 128, f0, fhi)
    return (p0, p0 + pspan, f0, fhi)


def _is_ap(x):
    return hasattr(x, "tensor") and hasattr(x, "ap") and hasattr(x, "offset")


def _overlap(a, b):
    return a[0] < b[1] and b[0] < a[1] and a[2] < b[3] and b[2] < a[3]


class Auto:

    serialize_all = False
    serialize_dma = False

    def __init__(self, nc, engs, esg):
        self.last = None
        self.nc = nc
        self.names = {id(e): n for e, n in engs}
        self.byname = {n: e for e, n in engs}
        self.osem = {n: esg(nc.semaphore(f"O{n}")) for _, n in engs}
        self.dsem = {n: esg(nc.semaphore(f"D{n}")) for _, n in engs}
        self.ocnt = {n: 0 for _, n in engs}
        self.dcnt = {n: 0 for _, n in engs}
        self.cc_sem = esg(nc.semaphore("CC"))
        self.vc = {}
        self.tmap = {}

    def _stream(self, eng, dma):
        return ("dma" if dma else "op", self.names[id(eng)])

    def _sem_wait(self, eng, st, sq):
        if st == ("cc",):
            eng.wait_ge(self.cc_sem, 1)
            return
        kind, n = st
        if kind == "op":
            eng.wait_ge(self.osem[n], sq)
        else:
            eng.wait_ge(self.dsem[n], 16 * sq)

    def _deps(self, eng, S, reads, writes):
        need = {}

        def add(st, sq):
            if st == S and S == ("op", "PE"):
                return
            if need.get(st, -1) < sq:
                need[st] = sq

        for ap in reads:
            ent = self.tmap.get(ap.tensor.name)
            if not ent:
                continue
            r = _ap_range(ap)
            for rec in ent[0]:
                if _overlap(r, rec[0]):
                    add(rec[1], rec[2])
        for ap in writes:
            ent = self.tmap.get(ap.tensor.name)
            if not ent:
                continue
            r = _ap_range(ap)
            for rec in ent[0]:
                if _overlap(r, rec[0]):
                    add(rec[1], rec[2])
            for rec in ent[1]:
                if _overlap(r, rec[0]):
                    add(rec[1], rec[2])
        vc = self.vc.setdefault(S, {})
        for st, sq in sorted(need.items(), key=lambda x: str(x[0])):
            if st == S and st[0] == "op":
                pass
            if vc.get(st, 0) >= sq:
                continue
            self._sem_wait(eng, st, sq)
            vc[st] = sq

    def _record(self, S, seq, reads, writes):
        for ap in writes:
            ent = self.tmap.setdefault(ap.tensor.name, [[], []])
            r = _ap_range(ap)

            def covered(rec):
                q = rec[0]
                return (r[0] <= q[0] and q[1] <= r[1]
                        and r[2] <= q[2] and q[3] <= r[3])

            ent[0] = [x for x in ent[0] if not covered(x)]
            ent[1] = [x for x in ent[1] if not covered(x)]
            ent[0].append((r, S, seq))
        for ap in reads:
            ent = self.tmap.setdefault(ap.tensor.name, [[], []])
            ent[1].append((_ap_range(ap), S, seq))

    def barrier(self):
        for n, e in self.byname.items():
            for m, c in self.ocnt.items():
                if c > 0:
                    e.wait_ge(self.osem[m], c)
            for m, c in self.dcnt.items():
                if c > 0:
                    e.wait_ge(self.dsem[m], 16 * c)
        for n in self.byname:
            for S in (("op", n), ("dma", n)):
                vc = self.vc.setdefault(S, {})
                for m, c in self.ocnt.items():
                    vc[("op", m)] = c
                for m, c in self.dcnt.items():
                    vc[("dma", m)] = c
        keep = {k: v for k, v in self.tmap.items() if k == "ccgat"}
        self.tmap = keep

    def _ser(self, eng, S, isdma=False):
        if self.serialize_dma and self.last is not None and isdma:
            st, sq = self.last
            vc = self.vc.setdefault(S, {})
            if st != S and vc.get(st, 0) < sq:
                self._sem_wait(eng, st, sq)
                vc[st] = sq
        if self.serialize_all and self.last is not None:
            st, sq = self.last
            vc = self.vc.setdefault(S, {})
            if st != S and vc.get(st, 0) < sq:
                self._sem_wait(eng, st, sq)
                vc[st] = sq

    def op(self, eng, fn, *args, **kw):
        S = self._stream(eng, False)
        aps = [a for a in args if _is_ap(a)]
        aps += [a for a in kw.values() if _is_ap(a)]
        writes, reads = aps[:1], aps[1:]
        self._deps(eng, S, reads, writes)
        self._ser(eng, S)
        if self.serialize_dma and getattr(self, "lastdma", None) is not None:
            st, sq = self.lastdma
            vc = self.vc.setdefault(S, {})
            if st != S and vc.get(st, 0) < sq:
                self._sem_wait(eng, st, sq)
                vc[st] = sq
        n = self.names[id(eng)]
        self.ocnt[n] += 1
        fn(*args, **kw).then_inc(self.osem[n], 1)
        self._record(S, self.ocnt[n], reads, writes)
        self.last = (S, self.ocnt[n])

    def dma(self, eng, out, in_):
        S = self._stream(eng, True)
        self._deps(eng, S, [in_], [out])
        n = self.names[id(eng)]
        self._ser(eng, S, isdma=True)
        vc = self.vc.setdefault(S, {})
        if self.dcnt[n] > 0 and vc.get(S, 0) < self.dcnt[n]:
            eng.wait_ge(self.dsem[n], 16 * self.dcnt[n])
            vc[S] = self.dcnt[n]
        self.dcnt[n] += 1
        eng.dma_start(out=out, in_=in_).then_inc(self.dsem[n], 16)
        self._record(S, self.dcnt[n], [in_], [out])
        self.last = (S, self.dcnt[n])
        self.lastdma = (S, self.dcnt[n])


    def fence(self, eng):
        for n, c in self.ocnt.items():
            if c > 0:
                eng.wait_ge(self.osem[n], c)
        for n, c in self.dcnt.items():
            if c > 0:
                eng.wait_ge(self.dsem[n], 16 * c)


def build_program(debug=False, n_megas=NMEGA, full=True, do_cc=True, parts="fcs"):
    nc = bass.Bass("TRN2", target_bir_lowering=False)

    def din(name, shape):
        return nc.dram_tensor(name, shape, F32, kind="ExternalInput").ap()

    u = din("u", [128 * (NT + 1), DM])
    w1t = din("w1t", [DM, NP])
    convwt = din("convwt", [4, CH, CH])
    convb = din("convb", [1, CH])
    alog = din("alog", [1, NH])
    prel = din("prel", [1, 1])
    w2t = din("w2t", [DM, DI])
    w3t = din("w3t", [DI, DM])
    wgt = din("wgt", [DM, FFN_I])
    wut = din("wut", [DM, FFN_I])
    wdt = din("wdt", [FFN_I, DM])
    lng = din("lng", [1, DI])
    lnb = din("lnb", [1, DI])
    rmask = din("rmask", [1, 8])

    o = nc.dram_tensor("o", [TPC, DM], F32, kind="ExternalOutput").ap()

    dkind = "ExternalOutput" if debug else "Internal"

    convw_bf = nc.dram_tensor("convw_bf", [17 * 4 * 128, CH], BF16)
    w2bf = nc.dram_tensor("w2bf", [DM, DI], BF16)
    w3bf = nc.dram_tensor("w3bf", [DI, DM], BF16)
    wqg = nc.dram_tensor("wqg", [DM, FFN_I], BF16)
    wqu = nc.dram_tensor("wqu", [DM, FFN_I], BF16)
    wqd = nc.dram_tensor("wqd", [FFN_I, DM], BF16)
    xnspill = nc.dram_tensor("xnspill", [TPC, DM], BF16, kind=dkind)
    xqspill = nc.dram_tensor("xqspill", [TPC, DM], BF16, kind=dkind)
    yspill = nc.dram_tensor("yspill", [TPC, DI], F32, kind=dkind)
    u2spill = nc.dram_tensor("u2spill", [TPC, DI], F32, kind=dkind)
    gspill = nc.dram_tensor("gspill", [TPC, DM], F32, kind=dkind)
    ccsrc = nc.dram_tensor("ccsrc", [DS, NH * HD], F32)
    ccgat = nc.dram_tensor("ccgat", [8 * DS, NH * HD], F32)

    es = contextlib.ExitStack()
    esg = es.enter_context

    v, sc, te, g, sy = nc.vector, nc.scalar, nc.tensor, nc.gpsimd, nc.sync
    s = Auto(nc, [(te, "PE"), (v, "DVE"), (sc, "ACT"), (g, "POOL"),
                  (sy, "SP")], esg)

    ones_c = esg(nc.sbuf_tensor("ones_c", [128, 1], F32))
    ones_r = esg(nc.sbuf_tensor("ones_r", [1, 128], F32))
    ones_rb = esg(nc.sbuf_tensor("ones_rb", [1, 128], BF16))
    id_f = esg(nc.sbuf_tensor("id_f", [128, 128], F32))
    id_b = esg(nc.sbuf_tensor("id_b", [128, 128], BF16))
    triu = esg(nc.sbuf_tensor("triu", [128, 128], F32))
    c127 = esg(nc.sbuf_tensor("c127", [128, 1], F32))
    a_col = esg(nc.sbuf_tensor("a_col", [128, 1], F32))
    A_bc = esg(nc.sbuf_tensor("A_bc", [128, NH], F32))
    convb_bf = esg(nc.sbuf_tensor("convb_bf", [1, CH], BF16))
    mcols = esg(nc.sbuf_tensor("mcols", [128, 6], F32))
    isx_buf = esg(nc.sbuf_tensor("isx_buf", [128, NT], F32))
    expa_buf = esg(nc.sbuf_tensor("expa_buf", [128, NT * NH], F32))
    CT_buf = esg(nc.sbuf_tensor("CT_buf", [DS, NT * 128], BF16))
    Ptot = esg(nc.sbuf_tensor("Ptot", [1, NT * NH], F32))
    Rrow = esg(nc.sbuf_tensor("Rrow", [1, NH], F32))
    state = esg(nc.sbuf_tensor("state", [DS, DI], F32))
    colA = esg(nc.sbuf_tensor("colA", [128, 24], F32))
    rowA = esg(nc.sbuf_tensor("rowA", [1, 64], F32))

    with nc.psum_tensor("pp", [128, 2048], F32) as pp, \
         nc.sbuf_tensor("wstage", [128, 4096], F32) as wstage, \
         nc.sbuf_tensor("qstage", [128, 4096], BF16) as qstage, \
         nc.sbuf_tensor("accc", [128, 1], F32) as accc, \
         nc.sbuf_tensor("vals", [1, 8], F32) as vals:

        s.op(v, v.memset, ones_c[:], 1.0)
        s.op(v, v.memset, ones_r[:], 1.0)
        s.op(sc, sc.copy, ones_rb[:], ones_r[:])
        s.op(v, v.memset, c127[:], 127.0)
        s.op(v, v.memset, id_f[:], 1.0)
        s.op(g, g.affine_select, id_f[:], id_f[:], [[-1, 128]], ALU.is_equal,
             0.0, base=0, channel_multiplier=1)
        s.op(sc, sc.copy, id_b[:], id_f[:])
        s.op(v, v.memset, triu[:], 1.0)
        s.op(g, g.affine_select, triu[:], triu[:], [[1, 128]], ALU.is_ge,
             0.0, base=0, channel_multiplier=-1)
        s.op(v, v.memset, Rrow[:], 0.0)
        s.op(v, v.memset, state[:], 0.0)

        s.dma(sy, vals[0:1, 0:1], prel)
        s.op(te, te.matmul, pp[:, 0:1], ones_r[:], vals[0:1, 0:1],
             start=True, stop=True)
        s.op(v, v.tensor_copy, a_col[:], pp[:, 0:1])
        s.dma(sy, rowA[0:1, 0:32], alog)
        s.op(sc, sc.activation, rowA[0:1, 32:64], rowA[0:1, 0:32], AF.Exp)
        s.op(v, v.tensor_scalar_mul, rowA[0:1, 32:64], rowA[0:1, 32:64], -1.0)
        s.op(te, te.matmul, pp[:, 0:NH], ones_r[:], rowA[0:1, 32:64],
             start=True, stop=True)
        s.op(v, v.tensor_copy, A_bc[:], pp[:, 0:NH])

        s.dma(sy, wstage[0:1, 0:CH], convb)
        s.op(sc, sc.copy, convb_bf[:], wstage[0:1, 0:CH])

        cw = convwt.rearrange("k (c p) i -> k c p i", p=128)
        cb = convw_bf.ap().rearrange("(c k p) i -> c k p i", k=4, p=128)
        for c in range(17):
            for k in range(4):
                s.dma(sy, wstage[:, 0:CH], cw[k, c])
                s.op(sc, sc.copy, qstage[:, 0:CH], wstage[:, 0:CH])
                s.dma(g, cb[c, k], qstage[:, 0:CH])

        for src, dst, rows, cols in ((w2t, w2bf, DM, DI), (w3t, w3bf, DI, DM)):
            sr = src.rearrange("(n p) f -> n p f", p=128)
            dr = dst.ap().rearrange("(n p) f -> n p f", p=128)
            for i in range(rows // 128):
                s.dma(sy, wstage[:, 0:cols], sr[i])
                s.op(sc, sc.copy, qstage[:, 0:cols], wstage[:, 0:cols])
                s.dma(g, dr[i], qstage[:, 0:cols])

        for wi, (src, dst, rows, cols) in enumerate(
                ((wgt, wqg, DM, FFN_I), (wut, wqu, DM, FFN_I),
                 (wdt, wqd, FFN_I, DM))):
            sr = src.rearrange("(n p) f -> n p f", p=128)
            dr = dst.ap().rearrange("(n p) f -> n p f", p=128)
            nchunks = rows // 128
            s.op(v, v.memset, accc[:], 0.0)
            for i in range(nchunks):
                s.dma(sy, wstage[:, 0:cols], sr[i])
                s.op(v, v.tensor_reduce, colA[:, 0:1], wstage[:, 0:cols],
                     AX.X, ALU.add, apply_absolute_value=True)
                s.op(v, v.tensor_add, accc[:], accc[:], colA[:, 0:1])
            s.op(te, te.matmul, pp[0:1, 0:1], ones_c[:], accc[:],
                 start=True, stop=True)
            s.op(v, v.tensor_copy, vals[0:1, 0:1], pp[0:1, 0:1])
            s.op(v, v.tensor_scalar, vals[0:1, 1:2], vals[0:1, 0:1],
                 1.0 / (rows * cols), 1e-5, ALU.mult, ALU.max)
            s.op(v, v.reciprocal, vals[0:1, 2:3], vals[0:1, 1:2])
            s.op(te, te.matmul, pp[:, 0:2], ones_r[:], vals[0:1, 1:3],
                 start=True, stop=True)
            s.op(v, v.tensor_copy, mcols[:, 2 * wi + 1:2 * wi + 2], pp[:, 0:1])
            s.op(v, v.tensor_copy, mcols[:, 2 * wi:2 * wi + 1], pp[:, 1:2])
            for i in range(nchunks):
                s.dma(sy, wstage[:, 0:cols], sr[i])
                s.op(v, v.tensor_scalar, wstage[:, 0:cols], wstage[:, 0:cols],
                     mcols[:, 2 * wi:2 * wi + 1], MAGIC, ALU.mult, ALU.add)
                s.op(v, v.tensor_scalar, wstage[:, 0:cols], wstage[:, 0:cols],
                     -MAGIC, 1.0, ALU.add, ALU.min)
                s.op(v, v.tensor_scalar, qstage[:, 0:cols], wstage[:, 0:cols],
                     -1.0, None, ALU.max)
                s.dma(g, dr[i], qstage[:, 0:cols])

    s.barrier()
    w1r = w1t.rearrange("(n p) f -> n p f", p=128)
    ur = u.rearrange("(t p) f -> t p f", p=128)

    es1 = contextlib.ExitStack()
    _ees1 = es1.enter_context
    pA = _ees1(nc.psum_tensor("pA", [128, 2048], F32))
    pB = _ees1(nc.psum_tensor("pB", [128, 1536], F32))
    pT = _ees1(nc.psum_tensor("pT", [128, 1024], BF16))
    w1bf = _ees1(nc.sbuf_tensor("w1bf", [128, 8 * NP], BF16))
    su2 = [_ees1(nc.sbuf_tensor(f"su{i}", [128, DM], F32)) for i in (0, 1)]
    squ = _ees1(nc.sbuf_tensor("squ", [128, DM], F32))
    xnf = _ees1(nc.sbuf_tensor("xnf", [128, DM], F32))
    xnb2 = [_ees1(nc.sbuf_tensor(f"xnb{i}", [128, DM], BF16)) for i in (0, 1)]
    xqb2 = [_ees1(nc.sbuf_tensor(f"xqb{i}", [128, DM], BF16)) for i in (0, 1)]
    xnT2 = [_ees1(nc.sbuf_tensor(f"xnT{i}", [128, 8 * 128], BF16))
            for i in (0, 1)]
    xbc2 = [_ees1(nc.sbuf_tensor(f"xbc_tm{i}", [128, CH], BF16))
            for i in (0, 1)]
    xfm = _ees1(nc.sbuf_tensor("xfm", [128, 17 * 515], BF16))
    xcbf = _ees1(nc.sbuf_tensor("xcbf", [128, TPM * CH], BF16))
    dt_mega = _ees1(nc.sbuf_tensor("dt_mega", [128, TPM * NH], F32))
    wpiece = _ees1(nc.sbuf_tensor("wpiece", [128, 2 * 4 * 512], BF16))
    tmp5a = _ees1(nc.sbuf_tensor("tmp5a", [128, 512], F32))
    tmp5b = _ees1(nc.sbuf_tensor("tmp5b", [128, 512], F32))
    t32a2 = [_ees1(nc.sbuf_tensor(f"t32a{i}", [128, NH], F32)) for i in (0, 1)]
    t32b2 = [_ees1(nc.sbuf_tensor(f"t32b{i}", [128, NH], F32)) for i in (0, 1)]
    asb2 = [_ees1(nc.sbuf_tensor(f"asb{i}", [128, NH], F32)) for i in (0, 1)]
    nasb2 = [_ees1(nc.sbuf_tensor(f"nasb{i}", [128, NH], F32)) for i in (0, 1)]
    atr2 = [_ees1(nc.sbuf_tensor(f"atr{i}", [32, 128], F32)) for i in (0, 1)]
    wtr2 = [_ees1(nc.sbuf_tensor(f"wtr{i}", [32, 128], F32)) for i in (0, 1)]
    wsb2 = [_ees1(nc.sbuf_tensor(f"wsb{i}", [128, NH], F32)) for i in (0, 1)]
    aflat = _ees1(nc.sbuf_tensor("aflat", [1, NH * 128], F32))
    totrow2 = [_ees1(nc.sbuf_tensor(f"totrow{i}", [1, NH], F32))
               for i in (0, 1)]
    dtrow2 = [_ees1(nc.sbuf_tensor(f"dtrow{i}", [1, NH], F32))
              for i in (0, 1)]
    dstage2 = [_ees1(nc.sbuf_tensor(f"dstage{i}", [128, 512], F32))
               for i in (0, 1)]
    Gbuf = _ees1(nc.sbuf_tensor("Gbuf", [128, NH * 128], BF16))
    BTs2 = [_ees1(nc.sbuf_tensor(f"BTs{i}", [DS, 128], BF16)) for i in (0, 1)]
    Ssb2 = [_ees1(nc.sbuf_tensor(f"Ssb{i}", [128, 128], BF16)) for i in (0, 1)]
    Yfin2 = [_ees1(nc.sbuf_tensor(f"Yfin{i}", [128, DI], F32))
             for i in (0, 1)]
    xw = _ees1(nc.sbuf_tensor("xw", [128, DI], BF16))
    statebf = _ees1(nc.sbuf_tensor("statebf", [DS, DI], BF16))
    if True:

        for i in range(8):
            su = su2[0]
            s.dma(sy, su[:], w1r[i][:, 0:DM])
            s.dma(sy, squ[:], w1r[i][:, DM:2 * DM])
            s.op(sc, sc.copy, w1bf[:, i * NP:i * NP + DM], su[:])
            s.op(sc, sc.copy, w1bf[:, i * NP + DM:i * NP + 2 * DM], squ[:])
            s.dma(sy, su[:, 0:NP - 2 * DM], w1r[i][:, 2 * DM:NP])
            s.op(sc, sc.copy, w1bf[:, i * NP + 2 * DM:(i + 1) * NP],
                 su[:, 0:NP - 2 * DM])

        def front_tile(t):
            halo = (t == 0)
            mega_tt = 0 if halo else (t - 1) % TPM
            pp = t % 2
            su = su2[pp]
            xnb, xqb, xnT, xbc_tm = xnb2[pp], xqb2[pp], xnT2[pp], xbc2[pp]
            cb = 8 + pp * 8
            s.dma(sy, su[:], ur[t])
            s.op(v, v.reduce_sum, colA[:, cb + 0:cb + 1], su[:], AX.X)
            s.op(v, v.tensor_scalar_mul, colA[:, cb + 1:cb + 2], colA[:, cb + 0:cb + 1], 1.0 / DM)
            s.op(sc, sc.square, squ[:], su[:])
            s.op(v, v.reduce_sum, colA[:, cb + 2:cb + 3], squ[:], AX.X)
            s.op(v, v.tensor_scalar_mul, colA[:, cb + 3:cb + 4], colA[:, cb + 2:cb + 3], 1.0 / DM)
            s.op(v, v.tensor_mul, colA[:, cb + 4:cb + 5], colA[:, cb + 1:cb + 2], colA[:, cb + 1:cb + 2])
            s.op(v, v.tensor_sub, colA[:, cb + 5:cb + 6], colA[:, cb + 3:cb + 4], colA[:, cb + 4:cb + 5])
            s.op(v, v.tensor_scalar_add, colA[:, cb + 5:cb + 6], colA[:, cb + 5:cb + 6], LN_EPS)
            s.op(sc, sc.sqrt, colA[:, cb + 6:cb + 7], colA[:, cb + 5:cb + 6])
            s.op(v, v.reciprocal, colA[:, cb + 6:cb + 7], colA[:, cb + 6:cb + 7])
            s.op(v, v.tensor_mul, colA[:, cb + 7:cb + 8], colA[:, cb + 1:cb + 2], colA[:, cb + 6:cb + 7])
            s.op(v, v.tensor_scalar_mul, colA[:, cb + 7:cb + 8], colA[:, cb + 7:cb + 8], -1.0)
            s.op(sc, sc.activation, xnf[:], su[:], AF.Identity,
                 bias=colA[:, cb + 7:cb + 8], scale=colA[:, cb + 6:cb + 7])
            s.op(sc, sc.copy, xnb[:], xnf[:])
            if not halo:
                s.dma(g, xnspill.ap().rearrange("(t p) f -> t p f", p=128)[t - 1],
                      xnb[:])
                s.op(v, v.tensor_reduce, colA[:, cb + 0:cb + 1], xnf[:], AX.X, ALU.max,
                     apply_absolute_value=True)
                s.op(v, v.tensor_scalar_max, colA[:, cb + 0:cb + 1], colA[:, cb + 0:cb + 1], 1e-5)
                s.op(v, v.reciprocal, colA[:, cb + 1:cb + 2], colA[:, cb + 0:cb + 1])
                s.op(v, v.tensor_scalar_mul, colA[:, cb + 1:cb + 2], colA[:, cb + 1:cb + 2],
                     127.0)
                s.op(v, v.tensor_scalar_mul, isx_buf[:, t - 1:t],
                     colA[:, cb + 0:cb + 1], 1.0 / 127.0)
                s.op(v, v.tensor_scalar, squ[:], xnf[:], colA[:, cb + 1:cb + 2], MAGIC,
                     ALU.mult, ALU.add)
                s.op(v, v.tensor_scalar, squ[:], squ[:], -MAGIC, 127.0,
                     ALU.add, ALU.min)
                s.op(v, v.tensor_scalar, xqb[:], squ[:], -128.0, None, ALU.max)
                s.dma(g, xqspill.ap().rearrange("(t p) f -> t p f", p=128)[t - 1],
                      xqb[:])
            for k in range(8):
                s.op(te, te.transpose, pT[:, k * 128:(k + 1) * 128],
                     xnb[:, k * 128:(k + 1) * 128], id_b[:])
            s.op(sc, sc.copy, xnT[:], pT[:, 0:1024])
            for j, (o0, on) in enumerate(IN_CH):
                for k in range(8):
                    s.op(te, te.matmul, pA[:, 0:on],
                         xnT[:, k * 128:(k + 1) * 128],
                         w1bf[:, k * NP + o0:k * NP + o0 + on],
                         start=(k == 0), stop=(k == 7))
                if o0 < 2048:
                    s.op(sc, sc.copy, xbc_tm[:, o0:o0 + on], pA[:, 0:on])
                else:
                    s.op(sc, sc.copy, xbc_tm[:, 2048:2176], pA[:, 0:128])
                    s.op(v, v.tensor_copy,
                         dt_mega[:, mega_tt * NH:(mega_tt + 1) * NH],
                         pA[:, 128:160])
            xfmv = xfm[:].rearrange("p (c f) -> p c f", c=17)
            pTv = pT[:, 0:1024].rearrange("p (s f) -> p s f", s=8)
            for c0 in range(0, 17, 8):
                ng = min(8, 17 - c0)
                for i in range(ng):
                    s.op(te, te.transpose, pT[:, i * 128:(i + 1) * 128],
                         xbc_tm[:, (c0 + i) * 128:(c0 + i + 1) * 128], id_b[:])
                if halo:
                    s.op(sc, sc.copy, xfmv[:, c0:c0 + ng, 0:3],
                         pTv[:, 0:ng, 125:128])
                else:
                    s.op(sc, sc.copy,
                         xfmv[:, c0:c0 + ng, 3 + mega_tt * 128:
                              3 + (mega_tt + 1) * 128],
                         pTv[:, 0:ng, :])

        def conv_mega(m):
            cbr = convw_bf.ap().rearrange("(c k p) i -> c p k i", k=4, p=128)
            for j, (o0, on) in enumerate(CONV_CH):
                for c in range(17):
                    buf = (c % 2) * 2048
                    s.dma(sy if c % 2 == 0 else g,
                          wpiece[:, buf:buf + 4 * on].rearrange(
                              "p (k f) -> p k f", k=4),
                          cbr[c][:, :, o0:o0 + on])
                    for k in range(4):
                        for tt in range(TPM):
                            s.op(te, te.matmul,
                                 pA[:, tt * 512:tt * 512 + on],
                                 xfm[:, c * 515 + k + tt * 128:
                                     c * 515 + k + tt * 128 + 128],
                                 wpiece[:, buf + k * on:buf + (k + 1) * on],
                                 start=(c == 0 and k == 0), stop=False)
                for tt in range(TPM):
                    s.op(te, te.matmul, pA[:, tt * 512:tt * 512 + on],
                         ones_rb[:], convb_bf[0:1, o0:o0 + on],
                         start=False, stop=True)
                    s.op(v, v.tensor_scalar, tmp5a[:, 0:on],
                         pA[:, tt * 512:tt * 512 + on], 0.0, a_col[:],
                         ALU.min, ALU.mult)
                    s.op(v, v.tensor_scalar, tmp5b[:, 0:on],
                         pA[:, tt * 512:tt * 512 + on], 0.0, None, ALU.max)
                    s.op(v, v.tensor_add,
                         xcbf[:, tt * CH + o0:tt * CH + o0 + on],
                         tmp5a[:, 0:on], tmp5b[:, 0:on])

        def ssd_tile(t):
            tt = (t - 1) % TPM
            pp = (t - 1) % 2
            t32a, t32b, asb, nasb = (t32a2[pp], t32b2[pp], asb2[pp],
                                     nasb2[pp])
            atr, wtr, wsb = atr2[pp], wtr2[pp], wsb2[pp]
            totrow, dtrow, dstage = totrow2[pp], dtrow2[pp], dstage2[pp]
            BTs, Ssb, Yfin = BTs2[pp], Ssb2[pp], Yfin2[pp]
            xc = xcbf[:, tt * CH:(tt + 1) * CH]
            dtr = dt_mega[:, tt * NH:(tt + 1) * NH]
            s.op(sc, sc.activation, t32a[:], dtr, AF.Abs)
            s.op(sc, sc.activation, t32a[:], t32a[:], AF.Exp, bias=0.0,
                 scale=-1.0)
            s.op(sc, sc.activation, t32a[:], t32a[:], AF.Ln, bias=1.0)
            s.op(v, v.tensor_scalar_max, t32b[:], dtr, 0.0)
            s.op(v, v.tensor_add, t32a[:], t32a[:], t32b[:])
            s.op(v, v.tensor_mul, t32a[:], t32a[:], A_bc[:])
            s.op(te, te.matmul, pB[:, 1024:1024 + NH], triu[:], t32a[:],
                 start=True, stop=True)
            s.op(te, te.matmul, pB[0:1, 1088:1088 + NH], ones_c[:], t32a[:],
                 start=True, stop=True)
            s.op(v, v.tensor_copy, asb[:], pB[:, 1024:1024 + NH])
            s.op(v, v.tensor_copy, totrow[:], pB[0:1, 1088:1088 + NH])
            s.op(v, v.tensor_scalar_mul, nasb[:], asb[:], -1.0)
            s.op(sc, sc.activation, expa_buf[:, (t - 1) * NH:t * NH], asb[:],
                 AF.Exp)
            s.op(v, v.tensor_copy, Ptot[0:1, (t - 1) * NH:t * NH], Rrow[:])
            s.op(v, v.tensor_add, Rrow[:], Rrow[:], totrow[:])
            s.op(te, te.transpose, pB[0:32, 1152:1280], asb[:], id_f[:])
            s.op(v, v.tensor_copy, atr[:], pB[0:32, 1152:1280])
            s.op(sc, sc.activation, wtr[:], atr[:], AF.Exp,
                 bias=atr[:, 127:128], scale=-1.0)
            s.op(te, te.transpose, pB[:, 1280:1280 + NH], wtr[:],
                 id_f[0:32, 0:32])
            s.op(v, v.tensor_copy, wsb[:], pB[:, 1280:1280 + NH])
            s.dma(sy, aflat[0:1, :].rearrange("o (p f) -> o p f", p=32),
                  atr[:])
            for q in range(8):
                po = (q % 2) * 512
                s.op(te, te.matmul, pB[:, po:po + 512], ones_r[:],
                     aflat[0:1, q * 512:(q + 1) * 512], start=True, stop=True)
                for i in range(4):
                    s.op(v, v.tensor_scalar, dstage[:, i * 128:(i + 1) * 128],
                         pB[:, po + i * 128:po + (i + 1) * 128],
                         nasb[:, 4 * q + i:4 * q + i + 1], 0.0,
                         ALU.add, ALU.min)
                s.op(sc, sc.activation, Gbuf[:, q * 512:(q + 1) * 512],
                     dstage[:], AF.Exp)
            s.op(te, te.transpose, pT[0:64, 0:128], xc[:, 2048:2112], id_b[:])
            s.op(v, v.tensor_copy, BTs[:], pT[0:64, 0:128])
            s.op(te, te.transpose, pT[0:64, 128:256], xc[:, 2112:2176], id_b[:])
            s.op(v, v.tensor_copy, CT_buf[:, (t - 1) * 128:t * 128],
                 pT[0:64, 128:256])
            s.op(te, te.matmul, pB[:, 1024:1152], BTs[:],
                 CT_buf[:, (t - 1) * 128:t * 128], start=True, stop=True)
            s.op(v, v.tensor_copy, Ssb[:], pB[:, 1024:1152])
            s.op(g, g.affine_select, Ssb[:], Ssb[:], [[1, 128]], ALU.is_ge,
                 0.0, base=0, channel_multiplier=-1)
            s.op(v, v.tensor_tensor,
                 Gbuf[:].rearrange("p (h l) -> p h l", h=NH),
                 Gbuf[:].rearrange("p (h l) -> p h l", h=NH),
                 Ssb[:].unsqueeze(1).broadcast_to([128, NH, 128]), ALU.mult)
            for h in range(NH):
                s.op(te, te.matmul, pA[:, h * HD:(h + 1) * HD],
                     Gbuf[:, h * 128:(h + 1) * 128],
                     xc[:, h * HD:(h + 1) * HD], start=True, stop=True)
            s.op(v, v.tensor_copy, Yfin[:], pA[:, 0:DI])
            s.op(sc, sc.copy, statebf[:], state[:])
            for q in range(4):
                s.op(te, te.matmul, pA[:, q * 512:(q + 1) * 512],
                     CT_buf[:, (t - 1) * 128:t * 128],
                     statebf[:, q * 512:(q + 1) * 512], start=True, stop=True)
            for h in range(NH):
                s.op(v, v.scalar_tensor_tensor, Yfin[:, h * HD:(h + 1) * HD],
                     pA[:, h * HD:(h + 1) * HD],
                     expa_buf[:, (t - 1) * NH + h:(t - 1) * NH + h + 1],
                     Yfin[:, h * HD:(h + 1) * HD], ALU.mult, ALU.add)
            s.dma(g, yspill.ap().rearrange("(t p) f -> t p f", p=128)[t - 1],
                  Yfin[:])
            for h in range(NH):
                s.op(v, v.tensor_scalar, xw[:, h * HD:(h + 1) * HD],
                     xc[:, h * HD:(h + 1) * HD], wsb[:, h:h + 1], None,
                     ALU.mult)
            for q in range(4):
                s.op(te, te.matmul, pA[64:128, q * 512:(q + 1) * 512],
                     xc[:, 2048:2112], xw[:, q * 512:(q + 1) * 512],
                     start=True, stop=True)
            s.op(sc, sc.activation, dtrow[:], totrow[:], AF.Exp)
            for h in range(NH):
                s.op(v, v.tensor_copy, aflat[0:1, h * HD:(h + 1) * HD],
                     dtrow[0:1, h:h + 1].broadcast_to([1, HD]))
            for q in range(4):
                s.op(te, te.matmul, pA[0:64, q * 512:(q + 1) * 512],
                     ones_r[0:1, 0:64], aflat[0:1, q * 512:(q + 1) * 512],
                     start=True, stop=True)
            s.op(v, v.tensor_tensor, state[:], state[:], pA[0:64, 0:DI],
                 ALU.mult)
            s.op(v, v.tensor_tensor, state[:], state[:], pA[64:128, 0:DI],
                 ALU.add)

        if n_megas > 0:
            front_tile(0)
        for m in range(n_megas):
            if m > 0:
                s.op(v, v.tensor_copy,
                     xfm[:, 0:17 * 515].rearrange(
                         "p (c f) -> p c f", c=17)[:, :, 0:3],
                     xfm[:, 0:17 * 515].rearrange(
                         "p (c f) -> p c f", c=17)[:, :, 512:515])
            for tt in range(TPM):
                front_tile(m * TPM + tt + 1)
            if "c" in parts:
                conv_mega(m)
            if "s" in parts:
                for tt in range(TPM):
                    ssd_tile(m * TPM + tt + 1)

        if n_megas > 0:
            s.dma(sy, ccsrc.ap(), state[:])
        if do_cc:
            s._deps(g, ("cc",), [ccsrc.ap()], [ccgat.ap()])
            g.collective_compute(
                "AllGather", ALU.bypass, [[0, 1, 2, 3, 4, 5, 6, 7]],
                [ccsrc.ap().opt()], [ccgat.ap().opt()],
            ).then_inc(s.cc_sem)
            s._record(("cc",), 1, [ccsrc.ap()], [ccgat.ap()])

    es1.close()
    s.barrier()
    if not full:
        s.fence(sy)
        es.close()
        nc.finalize()
        return nc

    FM = 8
    es2 = contextlib.ExitStack()
    _ees2 = es2.enter_context
    pC = _ees2(nc.psum_tensor("pC", [128, 2048], F32))
    pU = _ees2(nc.psum_tensor("pU", [128, 1024], BF16))
    w2p = _ees2(nc.sbuf_tensor("w2p", [128, 8 * 512], BF16))
    xnbt = _ees2(nc.sbuf_tensor("xnbt", [128, DM], BF16))
    xnT4 = _ees2(nc.sbuf_tensor("xnT4", [128, TPM * DM], BF16))
    xqT = _ees2(nc.sbuf_tensor("xqT", [128, TPM * DM], BF16))
    pmega = _ees2(nc.sbuf_tensor("pmega", [128, TPM * FFN_I], F32))
    gw = _ees2(nc.sbuf_tensor("gw", [128, 2 * 8 * 512], BF16))
    uw = _ees2(nc.sbuf_tensor("uw", [128, 8 * 512], BF16))
    gstage2 = [_ees2(nc.sbuf_tensor(f"gstage{i}", [128, 512], F32))
               for i in (0, 1)]
    ustage2 = [_ees2(nc.sbuf_tensor(f"ustage{i}", [128, 512], F32))
               for i in (0, 1)]
    scg = _ees2(nc.sbuf_tensor("scg", [128, 3 * TPM], F32))
    xq2b = _ees2(nc.sbuf_tensor("xq2b", [128, FFN_I], BF16))
    xq2T = _ees2(nc.sbuf_tensor("xq2T", [128, TPM * FFN_I], BF16))
    dw = _ees2(nc.sbuf_tensor("dw", [128, 2 * 2048], BF16))
    gout = _ees2(nc.sbuf_tensor("gout", [128, DM], F32))
    if True:

        w2r = w2bf.ap().rearrange("(n p) f -> p n f", p=128)
        xnsp = xnspill.ap().rearrange("(t p) f -> t p f", p=128)
        xqsp = xqspill.ap().rearrange("(t p) f -> t p f", p=128)
        u2sp = u2spill.ap().rearrange("(t p) f -> t p f", p=128)
        gsp = gspill.ap().rearrange("(t p) f -> t p f", p=128)
        wgr = wqg.ap().rearrange("(n p) f -> p n f", p=128)
        wur = wqu.ap().rearrange("(n p) f -> p n f", p=128)
        wdr = wqd.ap().rearrange("(n p) f -> p n f", p=128)

        for fm in range(FM):
            for tt in range(TPM):
                t = fm * TPM + tt + 1
                s.dma(sy, xnbt[:], xnsp[t - 1])
                for k in range(8):
                    s.op(te, te.transpose, pU[:, k * 128:(k + 1) * 128],
                         xnbt[:, k * 128:(k + 1) * 128], id_b[:])
                s.op(sc, sc.copy, xnT4[:, tt * DM:(tt + 1) * DM], pU[:])
                s.dma(sy, xnbt[:], xqsp[t - 1])
                for k in range(8):
                    s.op(te, te.transpose, pU[:, k * 128:(k + 1) * 128],
                         xnbt[:, k * 128:(k + 1) * 128], id_b[:])
                s.op(sc, sc.copy, xqT[:, tt * DM:(tt + 1) * DM], pU[:])
                s.op(v, v.tensor_mul, scg[:, 3 * tt:3 * tt + 1],
                     isx_buf[:, t - 1:t], mcols[:, 1:2])
                s.op(v, v.tensor_mul, scg[:, 3 * tt + 1:3 * tt + 2],
                     isx_buf[:, t - 1:t], mcols[:, 3:4])
            for j in range(4):
                s.dma(sy, w2p[:].rearrange("p (n f) -> p n f", n=8),
                      w2r[:, :, j * 512:(j + 1) * 512])
                for tt in range(TPM):
                    t = fm * TPM + tt + 1
                    gstage = gstage2[tt % 2]
                    for k in range(8):
                        s.op(te, te.matmul, pC[:, tt * 512:(tt + 1) * 512],
                             xnT4[:, tt * DM + k * 128:tt * DM + (k + 1) * 128],
                             w2p[:, k * 512:(k + 1) * 512],
                             start=(k == 0), stop=(k == 7))
                    s.op(sc, sc.activation, gstage[:],
                         pC[:, tt * 512:(tt + 1) * 512], AF.Gelu)
                    s.dma(g, u2sp[t - 1][:, j * 512:(j + 1) * 512], gstage[:])
            for j in range(8):
                bo = (j % 2) * 4096
                s.dma(sy if j % 2 == 0 else g, gw[:, bo:bo + 4096].rearrange(
                    "p (n f) -> p n f", n=8), wgr[:, :, j * 512:(j + 1) * 512])
                for tt in range(TPM):
                    for k in range(8):
                        s.op(te, te.matmul, pC[:, tt * 512:(tt + 1) * 512],
                             xqT[:, tt * DM + k * 128:tt * DM + (k + 1) * 128],
                             gw[:, bo + k * 512:bo + (k + 1) * 512],
                             start=(k == 0), stop=(k == 7))
                s.dma(g if j % 2 == 0 else sy, uw[:].rearrange(
                    "p (n f) -> p n f", n=8), wur[:, :, j * 512:(j + 1) * 512])
                for tt in range(TPM):
                    gstage, ustage = gstage2[tt % 2], ustage2[tt % 2]
                    s.op(sc, sc.activation, gstage[:],
                         pC[:, tt * 512:(tt + 1) * 512], AF.Silu,
                         bias=0.0, scale=scg[:, 3 * tt:3 * tt + 1])
                    for k in range(8):
                        s.op(te, te.matmul, pC[:, tt * 512:(tt + 1) * 512],
                             xqT[:, tt * DM + k * 128:tt * DM + (k + 1) * 128],
                             uw[:, k * 512:(k + 1) * 512],
                             start=(k == 0), stop=(k == 7))
                    s.op(sc, sc.activation, ustage[:],
                         pC[:, tt * 512:(tt + 1) * 512], AF.Copy,
                         bias=0.0, scale=scg[:, 3 * tt + 1:3 * tt + 2])
                    s.op(v, v.tensor_mul,
                         pmega[:, tt * FFN_I + j * 512:
                               tt * FFN_I + (j + 1) * 512],
                         gstage[:], ustage[:])
            for tt in range(TPM):
                t = fm * TPM + tt + 1
                pm = pmega[:, tt * FFN_I:(tt + 1) * FFN_I]
                s.op(v, v.tensor_reduce, colA[:, 0:1], pm, AX.X, ALU.max,
                     apply_absolute_value=True)
                s.op(v, v.tensor_scalar_max, colA[:, 0:1], colA[:, 0:1], 1e-5)
                s.op(v, v.reciprocal, colA[:, 1:2], colA[:, 0:1])
                s.op(v, v.tensor_scalar_mul, colA[:, 1:2], colA[:, 1:2],
                     127.0)
                s.op(v, v.tensor_scalar_mul, colA[:, 2:3], colA[:, 0:1],
                     1.0 / 127.0)
                s.op(v, v.tensor_mul, scg[:, 3 * tt + 2:3 * tt + 3],
                     colA[:, 2:3], mcols[:, 5:6])
                s.op(v, v.tensor_scalar, pm, pm, colA[:, 1:2], MAGIC,
                     ALU.mult, ALU.add)
                s.op(v, v.tensor_scalar, pm, pm, -MAGIC, 127.0,
                     ALU.add, ALU.min)
                s.op(v, v.tensor_scalar, xq2b[:], pm, -128.0, None,
                     ALU.max)
                for k0 in range(0, 32, 8):
                    for i in range(8):
                        s.op(te, te.transpose, pU[:, i * 128:(i + 1) * 128],
                             xq2b[:, (k0 + i) * 128:(k0 + i + 1) * 128],
                             id_b[:])
                    s.op(sc, sc.copy,
                         xq2T[:, tt * FFN_I + k0 * 128:
                              tt * FFN_I + (k0 + 8) * 128], pU[:])
            for hp in range(2):
                for kb in range(16):
                    bo = (kb % 2) * 2048
                    s.dma(sy if kb % 2 == 0 else g,
                          dw[:, bo:bo + 2048].rearrange(
                              "p (n f) -> p n f", n=2),
                          wdr[:, 2 * kb:2 * kb + 2, :])
                    for kk in range(2):
                        k = 2 * kb + kk
                        for tt in (2 * hp, 2 * hp + 1):
                            po = (tt % 2) * 1024
                            for q in range(2):
                                s.op(te, te.matmul,
                                     pC[:, po + q * 512:po + (q + 1) * 512],
                                     xq2T[:, tt * FFN_I + k * 128:
                                          tt * FFN_I + (k + 1) * 128],
                                     dw[:, bo + kk * 1024 + q * 512:
                                        bo + kk * 1024 + (q + 1) * 512],
                                     start=(k == 0), stop=(k == 31))
                for tt in (2 * hp, 2 * hp + 1):
                    t = fm * TPM + tt + 1
                    po = (tt % 2) * 1024
                    s.op(sc, sc.activation, gout[:], pC[:, po:po + 1024],
                         AF.Silu, bias=0.0, scale=scg[:, 3 * tt + 2:3 * tt + 3])
                    s.dma(g, gsp[t - 1], gout[:])

    es2.close()
    s.barrier()
    es3 = contextlib.ExitStack()
    _ees3 = es3.enter_context
    pE = _ees3(nc.psum_tensor("pE", [128, 2048], F32))
    pF = _ees3(nc.psum_tensor("pF", [128, 1024], F32))
    pV = _ees3(nc.psum_tensor("pV", [128, 1024], BF16))
    w3sb = _ees3(nc.sbuf_tensor("w3sb", [128, 16 * DM], BF16))
    gat = _ees3(nc.sbuf_tensor("gat", [DS, DI], F32))
    rmb = _ees3(nc.sbuf_tensor("rmb", [DS, 8], F32))
    sin_a = _ees3(nc.sbuf_tensor("sin_a", [DS, DI], F32))
    sinbf = _ees3(nc.sbuf_tensor("sinbf", [DS, DI], BF16))
    lngbc = _ees3(nc.sbuf_tensor("lngbc", [128, DI], F32))
    lnbbc = _ees3(nc.sbuf_tensor("lnbbc", [128, DI], F32))
    yt2 = [_ees3(nc.sbuf_tensor(f"yt{i}", [128, DI], F32)) for i in (0, 1)]
    u2t2 = [_ees3(nc.sbuf_tensor(f"u2t{i}", [128, DI], F32)) for i in (0, 1)]
    gt2 = [_ees3(nc.sbuf_tensor(f"gt{i}", [128, DM], F32)) for i in (0, 1)]
    ut2 = [_ees3(nc.sbuf_tensor(f"ut{i}", [128, DM], F32)) for i in (0, 1)]
    h22 = [_ees3(nc.sbuf_tensor(f"h2{i}", [128, DI], F32)) for i in (0, 1)]
    sq22 = [_ees3(nc.sbuf_tensor(f"sq2{i}", [128, DI], F32)) for i in (0, 1)]
    hnb2 = [_ees3(nc.sbuf_tensor(f"hnb{i}", [128, DI], BF16)) for i in (0, 1)]
    hnT2 = [_ees3(nc.sbuf_tensor(f"hnT{i}", [128, 16 * 128], BF16))
            for i in (0, 1)]
    Fsb2 = [_ees3(nc.sbuf_tensor(f"Fsb{i}", [128, NH], F32)) for i in (0, 1)]
    prow2 = [_ees3(nc.sbuf_tensor(f"prow{i}", [1, NH], F32)) for i in (0, 1)]
    ssb = _ees3(nc.sbuf_tensor("ssb", [128, DM], F32))
    osb2 = [_ees3(nc.sbuf_tensor(f"osb{i}", [128, DM], F32)) for i in (0, 1)]
    if True:

        w3r = w3bf.ap().rearrange("(n p) f -> n p f", p=128)
        for i in range(16):
            s.dma(sy, w3sb[:, i * DM:(i + 1) * DM], w3r[i])
        h2, sq2 = h22[0], sq22[0]
        s.dma(sy, h2[0:1, :], lng)
        s.dma(sy, sq2[0:1, :], lnb)
        for q in range(4):
            s.op(te, te.matmul, pE[:, q * 512:(q + 1) * 512], ones_r[:],
                 h2[0:1, q * 512:(q + 1) * 512], start=True, stop=True)
            s.op(te, te.matmul, pF[:, (q % 2) * 512:(q % 2) * 512 + 512],
                 ones_r[:], sq2[0:1, q * 512:(q + 1) * 512],
                 start=True, stop=True)
            s.op(v, v.tensor_copy, lngbc[:, q * 512:(q + 1) * 512],
                 pE[:, q * 512:(q + 1) * 512])
            s.op(v, v.tensor_copy, lnbbc[:, q * 512:(q + 1) * 512],
                 pF[:, (q % 2) * 512:(q % 2) * 512 + 512])
        s.dma(sy, rmb[0:1, 0:8], rmask)
        s.op(te, te.matmul, pF[0:DS, 0:8], ones_r[0:1, 0:DS], rmb[0:1, 0:8],
             start=True, stop=True)
        s.op(v, v.tensor_copy, rmb[:], pF[0:DS, 0:8])
        s.op(v, v.memset, sin_a[:], 0.0)
        gr = ccgat.ap().rearrange("(r n) f -> r n f", n=DS)
        for r in range(8):
            s.dma(sy, gat[:], gr[r])
            s.op(v, v.scalar_tensor_tensor, sin_a[:],
                 gat[:], rmb[:, r:r + 1], sin_a[:],
                 ALU.mult, ALU.add)
        s.op(sc, sc.copy, sinbf[:], sin_a[:])

        ysp = yspill.ap().rearrange("(t p) f -> t p f", p=128)
        u2sp = u2spill.ap().rearrange("(t p) f -> t p f", p=128)
        gsp = gspill.ap().rearrange("(t p) f -> t p f", p=128)
        orr = o.rearrange("(t p) f -> t p f", p=128)

        for t in range(1, NT + 1):
            pp = t % 2
            yt, u2t, gt, ut = yt2[pp], u2t2[pp], gt2[pp], ut2[pp]
            h2, sq2, hnb, hnT = h22[pp], sq22[pp], hnb2[pp], hnT2[pp]
            Fsb, prow, osb = Fsb2[pp], prow2[pp], osb2[pp]
            s.dma(sy, yt[:], ysp[t - 1])
            s.dma(g, u2t[:], u2sp[t - 1])
            s.dma(sy, gt[:], gsp[t - 1])
            s.dma(g, ut[:], ur[t])
            s.op(sc, sc.activation, prow[:], Ptot[0:1, (t - 1) * NH:t * NH],
                 AF.Exp)
            s.op(te, te.matmul, pF[:, 0:NH], ones_r[:], prow[:],
                 start=True, stop=True)
            s.op(v, v.tensor_tensor, Fsb[:], pF[:, 0:NH],
                 expa_buf[:, (t - 1) * NH:t * NH], ALU.mult)
            for q in range(4):
                s.op(te, te.matmul, pE[:, q * 512:(q + 1) * 512],
                     CT_buf[:, (t - 1) * 128:t * 128],
                     sinbf[:, q * 512:(q + 1) * 512], start=True, stop=True)
            for h in range(NH):
                s.op(v, v.scalar_tensor_tensor, yt[:, h * HD:(h + 1) * HD],
                     pE[:, h * HD:(h + 1) * HD], Fsb[:, h:h + 1],
                     yt[:, h * HD:(h + 1) * HD], ALU.mult, ALU.add)
            s.op(v, v.tensor_mul, h2[:], u2t[:], yt[:])
            s.op(v, v.reduce_sum, colA[:, 8 * pp + 0:8 * pp + 1], h2[:], AX.X)
            s.op(v, v.tensor_scalar_mul, colA[:, 8 * pp + 1:8 * pp + 2], colA[:, 8 * pp + 0:8 * pp + 1], 1.0 / DI)
            s.op(sc, sc.square, sq2[:], h2[:])
            s.op(v, v.reduce_sum, colA[:, 8 * pp + 2:8 * pp + 3], sq2[:], AX.X)
            s.op(v, v.tensor_scalar_mul, colA[:, 8 * pp + 3:8 * pp + 4], colA[:, 8 * pp + 2:8 * pp + 3], 1.0 / DI)
            s.op(v, v.tensor_mul, colA[:, 8 * pp + 4:8 * pp + 5], colA[:, 8 * pp + 1:8 * pp + 2], colA[:, 8 * pp + 1:8 * pp + 2])
            s.op(v, v.tensor_sub, colA[:, 8 * pp + 5:8 * pp + 6], colA[:, 8 * pp + 3:8 * pp + 4], colA[:, 8 * pp + 4:8 * pp + 5])
            s.op(v, v.tensor_scalar_add, colA[:, 8 * pp + 5:8 * pp + 6], colA[:, 8 * pp + 5:8 * pp + 6], LN_EPS)
            s.op(sc, sc.sqrt, colA[:, 8 * pp + 6:8 * pp + 7], colA[:, 8 * pp + 5:8 * pp + 6])
            s.op(v, v.reciprocal, colA[:, 8 * pp + 6:8 * pp + 7], colA[:, 8 * pp + 6:8 * pp + 7])
            s.op(v, v.tensor_mul, colA[:, 8 * pp + 7:8 * pp + 8], colA[:, 8 * pp + 1:8 * pp + 2], colA[:, 8 * pp + 6:8 * pp + 7])
            s.op(v, v.tensor_scalar_mul, colA[:, 8 * pp + 7:8 * pp + 8], colA[:, 8 * pp + 7:8 * pp + 8], -1.0)
            s.op(sc, sc.activation, sq2[:], h2[:], AF.Identity,
                 bias=colA[:, 8 * pp + 7:8 * pp + 8], scale=colA[:, 8 * pp + 6:8 * pp + 7])
            s.op(v, v.tensor_mul, sq2[:], sq2[:], lngbc[:])
            s.op(v, v.tensor_add, hnb[:], sq2[:], lnbbc[:])
            for k0 in range(0, 16, 8):
                for i in range(8):
                    s.op(te, te.transpose, pV[:, i * 128:(i + 1) * 128],
                         hnb[:, (k0 + i) * 128:(k0 + i + 1) * 128], id_b[:])
                s.op(sc, sc.copy, hnT[:, k0 * 128:(k0 + 8) * 128], pV[:])
            for j in range(2):
                for k in range(16):
                    s.op(te, te.matmul, pF[:, j * 512:(j + 1) * 512],
                         hnT[:, k * 128:(k + 1) * 128],
                         w3sb[:, k * DM + j * 512:k * DM + (j + 1) * 512],
                         start=(k == 0), stop=(k == 15))
            s.op(v, v.tensor_copy, ssb[:], pF[:, 0:DM])
            s.op(v, v.tensor_mul, osb[:], ssb[:], gt[:])
            s.op(v, v.tensor_add, osb[:], osb[:], ut[:])
            s.dma(g, orr[t - 1], osb[:])

        s.fence(sy)
    es3.close()
    es.close()
    nc.finalize()
    return nc


def make_in_maps(inputs):
    u_full = np.ascontiguousarray(inputs["L0_attn_u"], dtype=np.float32)
    w1t = np.ascontiguousarray(
        inputs["L0_attn_W_in1"].T[:, 2048:2048 + NP], dtype=np.float32)
    convwt = np.ascontiguousarray(
        np.transpose(inputs["L0_attn_conv_w"], (2, 1, 0)), dtype=np.float32)
    convb = np.ascontiguousarray(
        inputs["L0_attn_conv_b"].reshape(1, CH), dtype=np.float32)
    alog = np.ascontiguousarray(
        inputs["L0_attn_A_log"].reshape(1, NH), dtype=np.float32)
    prel = np.ascontiguousarray(
        np.asarray(inputs["L0_attn_prelu_a"]).reshape(1, 1), dtype=np.float32)
    w2t = np.ascontiguousarray(inputs["L0_attn_W_in2"].T, dtype=np.float32)
    w3t = np.ascontiguousarray(inputs["L0_attn_W_in3"].T, dtype=np.float32)
    wgt = np.ascontiguousarray(inputs["L0_ffn_w_gate"].T, dtype=np.float32)
    wut = np.ascontiguousarray(inputs["L0_ffn_w_up"].T, dtype=np.float32)
    wdt = np.ascontiguousarray(inputs["L0_ffn_w_down"].T, dtype=np.float32)
    lng = np.ascontiguousarray(
        inputs["L0_attn_ln_g"].reshape(1, DI), dtype=np.float32)
    lnb = np.ascontiguousarray(
        inputs["L0_attn_ln_b"].reshape(1, DI), dtype=np.float32)

    in_maps = []
    for c in range(8):
        b, half = c % 4, c // 4
        span = u_full[b, half * TPC:(half + 1) * TPC]
        head = np.zeros((128, DM), dtype=np.float32)
        if half == 1:
            head[125:128] = u_full[b, TPC - 3:TPC]
        ucore = np.ascontiguousarray(np.concatenate([head, span], axis=0))
        rmask = np.zeros((1, 8), dtype=np.float32)
        if half == 1:
            rmask[0, b] = 1.0
        in_maps.append(dict(
            u=ucore, w1t=w1t, convwt=convwt, convb=convb, alog=alog,
            prel=prel, w2t=w2t, w3t=w3t, wgt=wgt, wut=wut, wdt=wdt,
            lng=lng, lnb=lnb, rmask=rmask))
    return in_maps


def kernel(**inputs):
    nc = build_program()
    in_maps = make_in_maps(inputs)
    res = run_bass_kernel_spmd(nc, in_maps, core_ids=list(range(8)))
    out = np.empty((4, 2 * TPC, DM), dtype=np.float32)
    for c in range(8):
        b, half = c % 4, c // 4
        out[b, half * TPC:(half + 1) * TPC] = np.asarray(res.results[c]["o"])
    return out


# version 07
# speedup vs baseline: 1.0710x; 1.0185x over previous
import contextlib
import numpy as np
import concourse.bass as bass
import concourse.mybir as mybir
from concourse.bass_utils import run_bass_kernel_spmd

F32 = mybir.dt.float32
BF16 = mybir.dt.bfloat16
AF = mybir.ActivationFunctionType
ALU = mybir.AluOpType
AX = mybir.AxisListType

MAGIC = float(np.float32(1.5 * 2 ** 23))

DM = 1024
DI = 2048
DS = 64
HD = 64
NH = 32
CH = 2176
NP = 2208
TPC = 4096
NT = 32
TPM = 4
NMEGA = 8
LN_EPS = 1e-5
FFN_I = 4096

IN_CH = [(0, 512), (512, 512), (1024, 512), (1536, 512), (2048, 160)]
CONV_CH = [(0, 512), (512, 512), (1024, 512), (1536, 512), (2048, 128)]


class Ser:

    def __init__(self, sems):
        self.sems = sems
        self.idx = 0
        self.count = 0
        self.prev = None

    def _pre(self, eng):
        if self.prev is not None:
            eng.wait_ge(*self.prev)
            self.prev = None
        elif self.count > 0:
            eng.wait_ge(self.sems[self.idx], self.count)

    def _post(self, inst, inc):
        inst.then_inc(self.sems[self.idx], inc)
        self.count += inc
        if self.count > 40000:
            self.prev = (self.sems[self.idx], self.count)
            self.idx += 1
            self.count = 0

    def dma(self, eng, out, in_):
        self._pre(eng)
        self._post(eng.dma_start(out=out, in_=in_), 16)

    def op(self, eng, fn, *args, **kw):
        self._pre(eng)
        self._post(fn(*args, **kw), 1)

    def fence(self, eng):
        if self.prev is not None:
            eng.wait_ge(*self.prev)
        elif self.count > 0:
            eng.wait_ge(self.sems[self.idx], self.count)


FULLRANGE_TENSORS = set()


def _ap_range(ap):
    t = ap.tensor
    if t.name in FULLRANGE_TENSORS:
        return (0, 1 << 20, 0, 1 << 40)
    off = int(ap.offset)
    pairs = [(int(p[0]), int(p[1])) for p in ap.ap]
    if type(t).__name__.startswith("DRam"):
        hi = off + sum(st * (ct - 1) for st, ct in pairs) + 1
        return (0, 1, off, hi)
    shape = list(t.shape)
    rs = 1
    for d in shape[1:]:
        rs *= d
    p0 = off // rs
    f0 = off % rs
    pspan = 1
    fhi = f0
    for st, ct in pairs:
        ext = st * (ct - 1)
        if st != 0 and st % rs == 0:
            pspan += ext // rs
        else:
            fhi += ext
    fhi += 1
    if type(t).__name__.startswith("PSum"):
        be = 2048 // (2 if t.dtype == BF16 else 4)
        f0 = (f0 // be) * be
        fhi = ((fhi + be - 1) // be) * be
        return (0, 128, f0, fhi)
    return (p0, p0 + pspan, f0, fhi)


def _is_ap(x):
    return hasattr(x, "tensor") and hasattr(x, "ap") and hasattr(x, "offset")


def _overlap(a, b):
    return a[0] < b[1] and b[0] < a[1] and a[2] < b[3] and b[2] < a[3]


class Auto:

    serialize_all = False
    serialize_dma = False

    def __init__(self, nc, engs, esg):
        self.last = None
        self.nc = nc
        self.names = {id(e): n for e, n in engs}
        self.byname = {n: e for e, n in engs}
        self.osem = {n: esg(nc.semaphore(f"O{n}")) for _, n in engs}
        self.dsem = {n: esg(nc.semaphore(f"D{n}")) for _, n in engs}
        self.ocnt = {n: 0 for _, n in engs}
        self.dcnt = {n: 0 for _, n in engs}
        self.cc_sem = esg(nc.semaphore("CC"))
        self.vc = {}
        self.tmap = {}

    def _stream(self, eng, dma):
        return ("dma" if dma else "op", self.names[id(eng)])

    def _sem_wait(self, eng, st, sq):
        if st == ("cc",):
            eng.wait_ge(self.cc_sem, 1)
            return
        kind, n = st
        if kind == "op":
            eng.wait_ge(self.osem[n], sq)
        else:
            eng.wait_ge(self.dsem[n], 16 * sq)

    def _deps(self, eng, S, reads, writes):
        need = {}

        def add(st, sq):
            if st == S and S == ("op", "PE"):
                return
            if need.get(st, -1) < sq:
                need[st] = sq

        for ap in reads:
            ent = self.tmap.get(ap.tensor.name)
            if not ent:
                continue
            r = _ap_range(ap)
            for rec in ent[0]:
                if _overlap(r, rec[0]):
                    add(rec[1], rec[2])
        for ap in writes:
            ent = self.tmap.get(ap.tensor.name)
            if not ent:
                continue
            r = _ap_range(ap)
            for rec in ent[0]:
                if _overlap(r, rec[0]):
                    add(rec[1], rec[2])
            for rec in ent[1]:
                if _overlap(r, rec[0]):
                    add(rec[1], rec[2])
        vc = self.vc.setdefault(S, {})
        for st, sq in sorted(need.items(), key=lambda x: str(x[0])):
            if st == S and st[0] == "op":
                pass
            if vc.get(st, 0) >= sq:
                continue
            self._sem_wait(eng, st, sq)
            vc[st] = sq

    def _record(self, S, seq, reads, writes):
        for ap in writes:
            ent = self.tmap.setdefault(ap.tensor.name, [[], []])
            r = _ap_range(ap)

            def covered(rec):
                q = rec[0]
                return (r[0] <= q[0] and q[1] <= r[1]
                        and r[2] <= q[2] and q[3] <= r[3])

            ent[0] = [x for x in ent[0] if not covered(x)]
            ent[1] = [x for x in ent[1] if not covered(x)]
            ent[0].append((r, S, seq))
        for ap in reads:
            ent = self.tmap.setdefault(ap.tensor.name, [[], []])
            ent[1].append((_ap_range(ap), S, seq))

    def barrier(self):
        for n, e in self.byname.items():
            for m, c in self.ocnt.items():
                if c > 0:
                    e.wait_ge(self.osem[m], c)
            for m, c in self.dcnt.items():
                if c > 0:
                    e.wait_ge(self.dsem[m], 16 * c)
        for n in self.byname:
            for S in (("op", n), ("dma", n)):
                vc = self.vc.setdefault(S, {})
                for m, c in self.ocnt.items():
                    vc[("op", m)] = c
                for m, c in self.dcnt.items():
                    vc[("dma", m)] = c
        keep = {k: v for k, v in self.tmap.items() if k == "ccgat"}
        self.tmap = keep

    def _ser(self, eng, S, isdma=False):
        if self.serialize_dma and self.last is not None and isdma:
            st, sq = self.last
            vc = self.vc.setdefault(S, {})
            if st != S and vc.get(st, 0) < sq:
                self._sem_wait(eng, st, sq)
                vc[st] = sq
        if self.serialize_all and self.last is not None:
            st, sq = self.last
            vc = self.vc.setdefault(S, {})
            if st != S and vc.get(st, 0) < sq:
                self._sem_wait(eng, st, sq)
                vc[st] = sq

    def op(self, eng, fn, *args, **kw):
        S = self._stream(eng, False)
        aps = [a for a in args if _is_ap(a)]
        aps += [a for a in kw.values() if _is_ap(a)]
        writes, reads = aps[:1], aps[1:]
        self._deps(eng, S, reads, writes)
        self._ser(eng, S)
        if self.serialize_dma and getattr(self, "lastdma", None) is not None:
            st, sq = self.lastdma
            vc = self.vc.setdefault(S, {})
            if st != S and vc.get(st, 0) < sq:
                self._sem_wait(eng, st, sq)
                vc[st] = sq
        n = self.names[id(eng)]
        self.ocnt[n] += 1
        fn(*args, **kw).then_inc(self.osem[n], 1)
        self._record(S, self.ocnt[n], reads, writes)
        self.last = (S, self.ocnt[n])

    def dma(self, eng, out, in_):
        S = self._stream(eng, True)
        self._deps(eng, S, [in_], [out])
        n = self.names[id(eng)]
        self._ser(eng, S, isdma=True)
        vc = self.vc.setdefault(S, {})
        if self.dcnt[n] > 0 and vc.get(S, 0) < self.dcnt[n]:
            eng.wait_ge(self.dsem[n], 16 * self.dcnt[n])
            vc[S] = self.dcnt[n]
        self.dcnt[n] += 1
        eng.dma_start(out=out, in_=in_).then_inc(self.dsem[n], 16)
        self._record(S, self.dcnt[n], [in_], [out])
        self.last = (S, self.dcnt[n])
        self.lastdma = (S, self.dcnt[n])


    def fence(self, eng):
        for n, c in self.ocnt.items():
            if c > 0:
                eng.wait_ge(self.osem[n], c)
        for n, c in self.dcnt.items():
            if c > 0:
                eng.wait_ge(self.dsem[n], 16 * c)


def build_program(debug=False, n_megas=NMEGA, full=True, do_cc=True, parts="fcs"):
    nc = bass.Bass("TRN2", target_bir_lowering=False)

    def din(name, shape):
        return nc.dram_tensor(name, shape, F32, kind="ExternalInput").ap()

    u = din("u", [128 * (NT + 1), DM])
    w1t = din("w1t", [DM, NP])
    convwt = din("convwt", [4, CH, CH])
    convb = din("convb", [1, CH])
    alog = din("alog", [1, NH])
    prel = din("prel", [1, 1])
    w2t = din("w2t", [DM, DI])
    w3t = din("w3t", [DI, DM])
    wgt = din("wgt", [DM, FFN_I])
    wut = din("wut", [DM, FFN_I])
    wdt = din("wdt", [FFN_I, DM])
    lng = din("lng", [1, DI])
    lnb = din("lnb", [1, DI])
    rmask = din("rmask", [1, 8])

    o = nc.dram_tensor("o", [TPC, DM], F32, kind="ExternalOutput").ap()

    dkind = "ExternalOutput" if debug else "Internal"

    convw_bf = nc.dram_tensor("convw_bf", [17 * 4 * 128, CH], BF16)
    w2bf = nc.dram_tensor("w2bf", [DM, DI], BF16)
    w3bf = nc.dram_tensor("w3bf", [DI, DM], BF16)
    wqg = nc.dram_tensor("wqg", [DM, FFN_I], BF16)
    wqu = nc.dram_tensor("wqu", [DM, FFN_I], BF16)
    wqd = nc.dram_tensor("wqd", [FFN_I, DM], BF16)
    xnspill = nc.dram_tensor("xnspill", [TPC, DM], BF16, kind=dkind)
    xqspill = nc.dram_tensor("xqspill", [TPC, DM], BF16, kind=dkind)
    yspill = nc.dram_tensor("yspill", [TPC, DI], F32, kind=dkind)
    u2spill = nc.dram_tensor("u2spill", [TPC, DI], F32, kind=dkind)
    gspill = nc.dram_tensor("gspill", [TPC, DM], F32, kind=dkind)
    ccsrc = nc.dram_tensor("ccsrc", [DS, NH * HD], F32)
    ccgat = nc.dram_tensor("ccgat", [8 * DS, NH * HD], F32)

    es = contextlib.ExitStack()
    esg = es.enter_context

    v, sc, te, g, sy = nc.vector, nc.scalar, nc.tensor, nc.gpsimd, nc.sync
    s = Auto(nc, [(te, "PE"), (v, "DVE"), (sc, "ACT"), (g, "POOL"),
                  (sy, "SP")], esg)

    ones_c = esg(nc.sbuf_tensor("ones_c", [128, 1], F32))
    ones_r = esg(nc.sbuf_tensor("ones_r", [1, 128], F32))
    ones_rb = esg(nc.sbuf_tensor("ones_rb", [1, 128], BF16))
    id_f = esg(nc.sbuf_tensor("id_f", [128, 128], F32))
    id_b = esg(nc.sbuf_tensor("id_b", [128, 128], BF16))
    triu = esg(nc.sbuf_tensor("triu", [128, 128], F32))
    c127 = esg(nc.sbuf_tensor("c127", [128, 1], F32))
    a_col = esg(nc.sbuf_tensor("a_col", [128, 1], F32))
    A_bc = esg(nc.sbuf_tensor("A_bc", [128, NH], F32))
    convb_bf = esg(nc.sbuf_tensor("convb_bf", [1, CH], BF16))
    mcols = esg(nc.sbuf_tensor("mcols", [128, 6], F32))
    isx_buf = esg(nc.sbuf_tensor("isx_buf", [128, NT], F32))
    expa_buf = esg(nc.sbuf_tensor("expa_buf", [128, NT * NH], F32))
    CT_buf = esg(nc.sbuf_tensor("CT_buf", [DS, NT * 128], BF16))
    Ptot = esg(nc.sbuf_tensor("Ptot", [1, NT * NH], F32))
    Rrow = esg(nc.sbuf_tensor("Rrow", [1, NH], F32))
    state = esg(nc.sbuf_tensor("state", [DS, DI], F32))
    colA = esg(nc.sbuf_tensor("colA", [128, 24], F32))
    rowA = esg(nc.sbuf_tensor("rowA", [1, 64], F32))

    with nc.psum_tensor("pp", [128, 2048], F32) as pp, \
         nc.sbuf_tensor("wstage", [128, 4096], F32) as wstage, \
         nc.sbuf_tensor("qstage", [128, 4096], BF16) as qstage, \
         nc.sbuf_tensor("accc", [128, 1], F32) as accc, \
         nc.sbuf_tensor("vals", [1, 8], F32) as vals:

        s.op(v, v.memset, ones_c[:], 1.0)
        s.op(v, v.memset, ones_r[:], 1.0)
        s.op(sc, sc.copy, ones_rb[:], ones_r[:])
        s.op(v, v.memset, c127[:], 127.0)
        s.op(v, v.memset, id_f[:], 1.0)
        s.op(g, g.affine_select, id_f[:], id_f[:], [[-1, 128]], ALU.is_equal,
             0.0, base=0, channel_multiplier=1)
        s.op(sc, sc.copy, id_b[:], id_f[:])
        s.op(v, v.memset, triu[:], 1.0)
        s.op(g, g.affine_select, triu[:], triu[:], [[1, 128]], ALU.is_ge,
             0.0, base=0, channel_multiplier=-1)
        s.op(v, v.memset, Rrow[:], 0.0)
        s.op(v, v.memset, state[:], 0.0)

        s.dma(sy, vals[0:1, 0:1], prel)
        s.op(te, te.matmul, pp[:, 0:1], ones_r[:], vals[0:1, 0:1],
             start=True, stop=True)
        s.op(v, v.tensor_copy, a_col[:], pp[:, 0:1])
        s.dma(sy, rowA[0:1, 0:32], alog)
        s.op(sc, sc.activation, rowA[0:1, 32:64], rowA[0:1, 0:32], AF.Exp)
        s.op(v, v.tensor_scalar_mul, rowA[0:1, 32:64], rowA[0:1, 32:64], -1.0)
        s.op(te, te.matmul, pp[:, 0:NH], ones_r[:], rowA[0:1, 32:64],
             start=True, stop=True)
        s.op(v, v.tensor_copy, A_bc[:], pp[:, 0:NH])

        s.dma(sy, wstage[0:1, 0:CH], convb)
        s.op(sc, sc.copy, convb_bf[:], wstage[0:1, 0:CH])

        cw = convwt.rearrange("k (c p) i -> k c p i", p=128)
        cb = convw_bf.ap().rearrange("(c k p) i -> c k p i", k=4, p=128)
        for c in range(17):
            for k in range(4):
                s.dma(sy, wstage[:, 0:CH], cw[k, c])
                s.op(sc, sc.copy, qstage[:, 0:CH], wstage[:, 0:CH])
                s.dma(g, cb[c, k], qstage[:, 0:CH])

        for src, dst, rows, cols in ((w2t, w2bf, DM, DI), (w3t, w3bf, DI, DM)):
            sr = src.rearrange("(n p) f -> n p f", p=128)
            dr = dst.ap().rearrange("(n p) f -> n p f", p=128)
            for i in range(rows // 128):
                s.dma(sy, wstage[:, 0:cols], sr[i])
                s.op(sc, sc.copy, qstage[:, 0:cols], wstage[:, 0:cols])
                s.dma(g, dr[i], qstage[:, 0:cols])

        for wi, (src, dst, rows, cols) in enumerate(
                ((wgt, wqg, DM, FFN_I), (wut, wqu, DM, FFN_I),
                 (wdt, wqd, FFN_I, DM))):
            sr = src.rearrange("(n p) f -> n p f", p=128)
            dr = dst.ap().rearrange("(n p) f -> n p f", p=128)
            nchunks = rows // 128
            s.op(v, v.memset, accc[:], 0.0)
            for i in range(nchunks):
                s.dma(sy, wstage[:, 0:cols], sr[i])
                s.op(v, v.tensor_reduce, colA[:, 0:1], wstage[:, 0:cols],
                     AX.X, ALU.add, apply_absolute_value=True)
                s.op(v, v.tensor_add, accc[:], accc[:], colA[:, 0:1])
            s.op(te, te.matmul, pp[0:1, 0:1], ones_c[:], accc[:],
                 start=True, stop=True)
            s.op(v, v.tensor_copy, vals[0:1, 0:1], pp[0:1, 0:1])
            s.op(v, v.tensor_scalar, vals[0:1, 1:2], vals[0:1, 0:1],
                 1.0 / (rows * cols), 1e-5, ALU.mult, ALU.max)
            s.op(v, v.reciprocal, vals[0:1, 2:3], vals[0:1, 1:2])
            s.op(te, te.matmul, pp[:, 0:2], ones_r[:], vals[0:1, 1:3],
                 start=True, stop=True)
            s.op(v, v.tensor_copy, mcols[:, 2 * wi + 1:2 * wi + 2], pp[:, 0:1])
            s.op(v, v.tensor_copy, mcols[:, 2 * wi:2 * wi + 1], pp[:, 1:2])
            for i in range(nchunks):
                s.dma(sy, wstage[:, 0:cols], sr[i])
                s.op(v, v.tensor_scalar, wstage[:, 0:cols], wstage[:, 0:cols],
                     mcols[:, 2 * wi:2 * wi + 1], MAGIC, ALU.mult, ALU.add)
                s.op(v, v.tensor_scalar, wstage[:, 0:cols], wstage[:, 0:cols],
                     -MAGIC, 1.0, ALU.add, ALU.min)
                s.op(v, v.tensor_scalar, qstage[:, 0:cols], wstage[:, 0:cols],
                     -1.0, None, ALU.max)
                s.dma(g, dr[i], qstage[:, 0:cols])

    s.barrier()
    w1r = w1t.rearrange("(n p) f -> n p f", p=128)
    ur = u.rearrange("(t p) f -> t p f", p=128)

    es1 = contextlib.ExitStack()
    _ees1 = es1.enter_context
    pA = _ees1(nc.psum_tensor("pA", [128, 2048], F32))
    pB = _ees1(nc.psum_tensor("pB", [128, 1536], F32))
    pT = _ees1(nc.psum_tensor("pT", [128, 1024], BF16))
    w1bf = _ees1(nc.sbuf_tensor("w1bf", [128, 8 * NP], BF16))
    su2 = [_ees1(nc.sbuf_tensor(f"su{i}", [128, DM], F32)) for i in (0, 1)]
    squ = _ees1(nc.sbuf_tensor("squ", [128, DM], F32))
    xnf = _ees1(nc.sbuf_tensor("xnf", [128, DM], F32))
    xnb2 = [_ees1(nc.sbuf_tensor(f"xnb{i}", [128, DM], BF16)) for i in (0, 1)]
    xqb2 = [_ees1(nc.sbuf_tensor(f"xqb{i}", [128, DM], BF16)) for i in (0, 1)]
    xnT2 = [_ees1(nc.sbuf_tensor(f"xnT{i}", [128, 8 * 128], BF16))
            for i in (0, 1)]
    xbc2 = [_ees1(nc.sbuf_tensor(f"xbc_tm{i}", [128, CH], BF16))
            for i in (0, 1)]
    xfm = _ees1(nc.sbuf_tensor("xfm", [128, 17 * 515], BF16))
    xcbf = _ees1(nc.sbuf_tensor("xcbf", [128, TPM * CH], BF16))
    dt_mega = _ees1(nc.sbuf_tensor("dt_mega", [128, TPM * NH], F32))
    wpiece = _ees1(nc.sbuf_tensor("wpiece", [128, 2 * 4 * 512], BF16))
    tmp5a = _ees1(nc.sbuf_tensor("tmp5a", [128, 512], F32))
    tmp5b = _ees1(nc.sbuf_tensor("tmp5b", [128, 512], F32))
    t32a2 = [_ees1(nc.sbuf_tensor(f"t32a{i}", [128, NH], F32)) for i in (0, 1)]
    t32b2 = [_ees1(nc.sbuf_tensor(f"t32b{i}", [128, NH], F32)) for i in (0, 1)]
    asb2 = [_ees1(nc.sbuf_tensor(f"asb{i}", [128, NH], F32)) for i in (0, 1)]
    nasb2 = [_ees1(nc.sbuf_tensor(f"nasb{i}", [128, NH], F32)) for i in (0, 1)]
    atr2 = [_ees1(nc.sbuf_tensor(f"atr{i}", [32, 128], F32)) for i in (0, 1)]
    wtr2 = [_ees1(nc.sbuf_tensor(f"wtr{i}", [32, 128], F32)) for i in (0, 1)]
    wsb2 = [_ees1(nc.sbuf_tensor(f"wsb{i}", [128, NH], F32)) for i in (0, 1)]
    aflat = _ees1(nc.sbuf_tensor("aflat", [1, NH * 128], F32))
    totrow2 = [_ees1(nc.sbuf_tensor(f"totrow{i}", [1, NH], F32))
               for i in (0, 1)]
    dtrow2 = [_ees1(nc.sbuf_tensor(f"dtrow{i}", [1, NH], F32))
              for i in (0, 1)]
    dstage2 = [_ees1(nc.sbuf_tensor(f"dstage{i}", [128, 512], F32))
               for i in (0, 1)]
    Gbuf = _ees1(nc.sbuf_tensor("Gbuf", [128, NH * 128], BF16))
    BTs2 = [_ees1(nc.sbuf_tensor(f"BTs{i}", [DS, 128], BF16)) for i in (0, 1)]
    Ssb2 = [_ees1(nc.sbuf_tensor(f"Ssb{i}", [128, 128], BF16)) for i in (0, 1)]
    Yfin2 = [_ees1(nc.sbuf_tensor(f"Yfin{i}", [128, DI], F32))
             for i in (0, 1)]
    xw = _ees1(nc.sbuf_tensor("xw", [128, DI], BF16))
    statebf = _ees1(nc.sbuf_tensor("statebf", [DS, DI], BF16))
    if True:

        for i in range(8):
            su = su2[0]
            s.dma(sy, su[:], w1r[i][:, 0:DM])
            s.dma(sy, squ[:], w1r[i][:, DM:2 * DM])
            s.op(sc, sc.copy, w1bf[:, i * NP:i * NP + DM], su[:])
            s.op(sc, sc.copy, w1bf[:, i * NP + DM:i * NP + 2 * DM], squ[:])
            s.dma(sy, su[:, 0:NP - 2 * DM], w1r[i][:, 2 * DM:NP])
            s.op(sc, sc.copy, w1bf[:, i * NP + 2 * DM:(i + 1) * NP],
                 su[:, 0:NP - 2 * DM])

        def front_tile(t):
            halo = (t == 0)
            mega_tt = 0 if halo else (t - 1) % TPM
            pp = t % 2
            su = su2[pp]
            xnb, xqb, xnT, xbc_tm = xnb2[pp], xqb2[pp], xnT2[pp], xbc2[pp]
            cb = 8 + pp * 8
            s.dma(sy, su[:], ur[t])
            s.op(v, v.reduce_sum, colA[:, cb + 0:cb + 1], su[:], AX.X)
            s.op(v, v.tensor_scalar_mul, colA[:, cb + 1:cb + 2], colA[:, cb + 0:cb + 1], 1.0 / DM)
            s.op(sc, sc.square, squ[:], su[:])
            s.op(v, v.reduce_sum, colA[:, cb + 2:cb + 3], squ[:], AX.X)
            s.op(v, v.tensor_scalar_mul, colA[:, cb + 3:cb + 4], colA[:, cb + 2:cb + 3], 1.0 / DM)
            s.op(v, v.tensor_mul, colA[:, cb + 4:cb + 5], colA[:, cb + 1:cb + 2], colA[:, cb + 1:cb + 2])
            s.op(v, v.tensor_sub, colA[:, cb + 5:cb + 6], colA[:, cb + 3:cb + 4], colA[:, cb + 4:cb + 5])
            s.op(v, v.tensor_scalar_add, colA[:, cb + 5:cb + 6], colA[:, cb + 5:cb + 6], LN_EPS)
            s.op(sc, sc.sqrt, colA[:, cb + 6:cb + 7], colA[:, cb + 5:cb + 6])
            s.op(v, v.reciprocal, colA[:, cb + 6:cb + 7], colA[:, cb + 6:cb + 7])
            s.op(v, v.tensor_mul, colA[:, cb + 7:cb + 8], colA[:, cb + 1:cb + 2], colA[:, cb + 6:cb + 7])
            s.op(v, v.tensor_scalar_mul, colA[:, cb + 7:cb + 8], colA[:, cb + 7:cb + 8], -1.0)
            s.op(sc, sc.activation, xnf[:], su[:], AF.Identity,
                 bias=colA[:, cb + 7:cb + 8], scale=colA[:, cb + 6:cb + 7])
            s.op(sc, sc.copy, xnb[:], xnf[:])
            if not halo:
                s.dma(g, xnspill.ap().rearrange("(t p) f -> t p f", p=128)[t - 1],
                      xnb[:])
                s.op(v, v.tensor_reduce, colA[:, cb + 0:cb + 1], xnf[:], AX.X, ALU.max,
                     apply_absolute_value=True)
                s.op(v, v.tensor_scalar_max, colA[:, cb + 0:cb + 1], colA[:, cb + 0:cb + 1], 1e-5)
                s.op(v, v.reciprocal, colA[:, cb + 1:cb + 2], colA[:, cb + 0:cb + 1])
                s.op(v, v.tensor_scalar_mul, colA[:, cb + 1:cb + 2], colA[:, cb + 1:cb + 2],
                     127.0)
                s.op(v, v.tensor_scalar_mul, isx_buf[:, t - 1:t],
                     colA[:, cb + 0:cb + 1], 1.0 / 127.0)
                s.op(v, v.tensor_scalar, squ[:], xnf[:], colA[:, cb + 1:cb + 2], MAGIC,
                     ALU.mult, ALU.add)
                s.op(v, v.tensor_scalar, squ[:], squ[:], -MAGIC, 127.0,
                     ALU.add, ALU.min)
                s.op(v, v.tensor_scalar, xqb[:], squ[:], -128.0, None, ALU.max)
                s.dma(g, xqspill.ap().rearrange("(t p) f -> t p f", p=128)[t - 1],
                      xqb[:])
            for k in range(8):
                s.op(te, te.transpose, pT[:, k * 128:(k + 1) * 128],
                     xnb[:, k * 128:(k + 1) * 128], id_b[:])
            s.op(sc, sc.copy, xnT[:], pT[:, 0:1024])
            for j, (o0, on) in enumerate(IN_CH):
                po = (j % 4) * 512
                for k in range(8):
                    s.op(te, te.matmul, pA[:, po:po + on],
                         xnT[:, k * 128:(k + 1) * 128],
                         w1bf[:, k * NP + o0:k * NP + o0 + on],
                         start=(k == 0), stop=(k == 7))
                if o0 < 2048:
                    s.op(sc, sc.copy, xbc_tm[:, o0:o0 + on], pA[:, po:po + on])
                else:
                    s.op(sc, sc.copy, xbc_tm[:, 2048:2176],
                         pA[:, po:po + 128])
                    s.op(v, v.tensor_copy,
                         dt_mega[:, mega_tt * NH:(mega_tt + 1) * NH],
                         pA[:, po + 128:po + 160])
            xfmv = xfm[:].rearrange("p (c f) -> p c f", c=17)
            pTv = pT[:, 0:1024].rearrange("p (s f) -> p s f", s=8)
            for c0 in range(0, 17, 8):
                ng = min(8, 17 - c0)
                for i in range(ng):
                    s.op(te, te.transpose, pT[:, i * 128:(i + 1) * 128],
                         xbc_tm[:, (c0 + i) * 128:(c0 + i + 1) * 128], id_b[:])
                if halo:
                    s.op(sc, sc.copy, xfmv[:, c0:c0 + ng, 0:3],
                         pTv[:, 0:ng, 125:128])
                else:
                    s.op(sc, sc.copy,
                         xfmv[:, c0:c0 + ng, 3 + mega_tt * 128:
                              3 + (mega_tt + 1) * 128],
                         pTv[:, 0:ng, :])

        def conv_mega(m):
            cbr = convw_bf.ap().rearrange("(c k p) i -> c p k i", k=4, p=128)
            for j, (o0, on) in enumerate(CONV_CH):
                for c in range(17):
                    buf = (c % 2) * 2048
                    s.dma(sy if c % 2 == 0 else g,
                          wpiece[:, buf:buf + 4 * on].rearrange(
                              "p (k f) -> p k f", k=4),
                          cbr[c][:, :, o0:o0 + on])
                    for k in range(4):
                        for tt in range(TPM):
                            s.op(te, te.matmul,
                                 pA[:, tt * 512:tt * 512 + on],
                                 xfm[:, c * 515 + k + tt * 128:
                                     c * 515 + k + tt * 128 + 128],
                                 wpiece[:, buf + k * on:buf + (k + 1) * on],
                                 start=(c == 0 and k == 0), stop=False)
                for tt in range(TPM):
                    s.op(te, te.matmul, pA[:, tt * 512:tt * 512 + on],
                         ones_rb[:], convb_bf[0:1, o0:o0 + on],
                         start=False, stop=True)
                    s.op(v, v.tensor_scalar, tmp5a[:, 0:on],
                         pA[:, tt * 512:tt * 512 + on], 0.0, a_col[:],
                         ALU.min, ALU.mult)
                    s.op(v, v.tensor_scalar, tmp5b[:, 0:on],
                         pA[:, tt * 512:tt * 512 + on], 0.0, None, ALU.max)
                    s.op(v, v.tensor_add,
                         xcbf[:, tt * CH + o0:tt * CH + o0 + on],
                         tmp5a[:, 0:on], tmp5b[:, 0:on])

        def ssd_tile(t):
            tt = (t - 1) % TPM
            pp = (t - 1) % 2
            t32a, t32b, asb, nasb = (t32a2[pp], t32b2[pp], asb2[pp],
                                     nasb2[pp])
            atr, wtr, wsb = atr2[pp], wtr2[pp], wsb2[pp]
            totrow, dtrow, dstage = totrow2[pp], dtrow2[pp], dstage2[pp]
            BTs, Ssb, Yfin = BTs2[pp], Ssb2[pp], Yfin2[pp]
            xc = xcbf[:, tt * CH:(tt + 1) * CH]
            dtr = dt_mega[:, tt * NH:(tt + 1) * NH]
            s.op(sc, sc.activation, t32a[:], dtr, AF.Abs)
            s.op(sc, sc.activation, t32a[:], t32a[:], AF.Exp, bias=0.0,
                 scale=-1.0)
            s.op(sc, sc.activation, t32a[:], t32a[:], AF.Ln, bias=1.0)
            s.op(v, v.tensor_scalar_max, t32b[:], dtr, 0.0)
            s.op(v, v.tensor_add, t32a[:], t32a[:], t32b[:])
            s.op(v, v.tensor_mul, t32a[:], t32a[:], A_bc[:])
            s.op(te, te.matmul, pB[:, 1024:1024 + NH], triu[:], t32a[:],
                 start=True, stop=True)
            s.op(te, te.matmul, pB[0:1, 1088:1088 + NH], ones_c[:], t32a[:],
                 start=True, stop=True)
            s.op(v, v.tensor_copy, asb[:], pB[:, 1024:1024 + NH])
            s.op(v, v.tensor_copy, totrow[:], pB[0:1, 1088:1088 + NH])
            s.op(v, v.tensor_scalar_mul, nasb[:], asb[:], -1.0)
            s.op(sc, sc.activation, expa_buf[:, (t - 1) * NH:t * NH], asb[:],
                 AF.Exp)
            s.op(v, v.tensor_copy, Ptot[0:1, (t - 1) * NH:t * NH], Rrow[:])
            s.op(v, v.tensor_add, Rrow[:], Rrow[:], totrow[:])
            s.op(te, te.transpose, pB[0:32, 1152:1280], asb[:], id_f[:])
            s.op(v, v.tensor_copy, atr[:], pB[0:32, 1152:1280])
            s.op(sc, sc.activation, wtr[:], atr[:], AF.Exp,
                 bias=atr[:, 127:128], scale=-1.0)
            s.op(te, te.transpose, pB[:, 1280:1280 + NH], wtr[:],
                 id_f[0:32, 0:32])
            s.op(v, v.tensor_copy, wsb[:], pB[:, 1280:1280 + NH])
            s.dma(sy, aflat[0:1, :].rearrange("o (p f) -> o p f", p=32),
                  atr[:])
            for q in range(8):
                po = (q % 2) * 512
                s.op(te, te.matmul, pB[:, po:po + 512], ones_r[:],
                     aflat[0:1, q * 512:(q + 1) * 512], start=True, stop=True)
                for i in range(4):
                    s.op(v, v.tensor_scalar, dstage[:, i * 128:(i + 1) * 128],
                         pB[:, po + i * 128:po + (i + 1) * 128],
                         nasb[:, 4 * q + i:4 * q + i + 1], 0.0,
                         ALU.add, ALU.min)
                s.op(sc, sc.activation, Gbuf[:, q * 512:(q + 1) * 512],
                     dstage[:], AF.Exp)
            s.op(te, te.transpose, pT[0:64, 0:128], xc[:, 2048:2112], id_b[:])
            s.op(v, v.tensor_copy, BTs[:], pT[0:64, 0:128])
            s.op(te, te.transpose, pT[0:64, 128:256], xc[:, 2112:2176], id_b[:])
            s.op(v, v.tensor_copy, CT_buf[:, (t - 1) * 128:t * 128],
                 pT[0:64, 128:256])
            s.op(te, te.matmul, pB[:, 1024:1152], BTs[:],
                 CT_buf[:, (t - 1) * 128:t * 128], start=True, stop=True)
            s.op(v, v.tensor_copy, Ssb[:], pB[:, 1024:1152])
            s.op(g, g.affine_select, Ssb[:], Ssb[:], [[1, 128]], ALU.is_ge,
                 0.0, base=0, channel_multiplier=-1)
            s.op(v, v.tensor_tensor,
                 Gbuf[:].rearrange("p (h l) -> p h l", h=NH),
                 Gbuf[:].rearrange("p (h l) -> p h l", h=NH),
                 Ssb[:].unsqueeze(1).broadcast_to([128, NH, 128]), ALU.mult)
            for h in range(NH):
                s.op(te, te.matmul, pA[:, h * HD:(h + 1) * HD],
                     Gbuf[:, h * 128:(h + 1) * 128],
                     xc[:, h * HD:(h + 1) * HD], start=True, stop=True)
            s.op(v, v.tensor_copy, Yfin[:], pA[:, 0:DI])
            s.op(sc, sc.copy, statebf[:], state[:])
            for q in range(4):
                s.op(te, te.matmul, pA[:, q * 512:(q + 1) * 512],
                     CT_buf[:, (t - 1) * 128:t * 128],
                     statebf[:, q * 512:(q + 1) * 512], start=True, stop=True)
            for h in range(NH):
                s.op(v, v.scalar_tensor_tensor, Yfin[:, h * HD:(h + 1) * HD],
                     pA[:, h * HD:(h + 1) * HD],
                     expa_buf[:, (t - 1) * NH + h:(t - 1) * NH + h + 1],
                     Yfin[:, h * HD:(h + 1) * HD], ALU.mult, ALU.add)
            s.dma(g, yspill.ap().rearrange("(t p) f -> t p f", p=128)[t - 1],
                  Yfin[:])
            for h in range(NH):
                s.op(v, v.tensor_scalar, xw[:, h * HD:(h + 1) * HD],
                     xc[:, h * HD:(h + 1) * HD], wsb[:, h:h + 1], None,
                     ALU.mult)
            for q in range(4):
                s.op(te, te.matmul, pA[64:128, q * 512:(q + 1) * 512],
                     xc[:, 2048:2112], xw[:, q * 512:(q + 1) * 512],
                     start=True, stop=True)
            s.op(sc, sc.activation, dtrow[:], totrow[:], AF.Exp)
            for h in range(NH):
                s.op(v, v.tensor_copy, aflat[0:1, h * HD:(h + 1) * HD],
                     dtrow[0:1, h:h + 1].broadcast_to([1, HD]))
            for q in range(4):
                s.op(te, te.matmul, pA[0:64, q * 512:(q + 1) * 512],
                     ones_r[0:1, 0:64], aflat[0:1, q * 512:(q + 1) * 512],
                     start=True, stop=True)
            s.op(v, v.tensor_tensor, state[:], state[:], pA[0:64, 0:DI],
                 ALU.mult)
            s.op(v, v.tensor_tensor, state[:], state[:], pA[64:128, 0:DI],
                 ALU.add)

        if n_megas > 0:
            front_tile(0)
        for m in range(n_megas):
            if m > 0:
                s.op(v, v.tensor_copy,
                     xfm[:, 0:17 * 515].rearrange(
                         "p (c f) -> p c f", c=17)[:, :, 0:3],
                     xfm[:, 0:17 * 515].rearrange(
                         "p (c f) -> p c f", c=17)[:, :, 512:515])
            for tt in range(TPM):
                front_tile(m * TPM + tt + 1)
            if "c" in parts:
                conv_mega(m)
            if "s" in parts:
                for tt in range(TPM):
                    ssd_tile(m * TPM + tt + 1)

        if n_megas > 0:
            s.dma(sy, ccsrc.ap(), state[:])
        if do_cc:
            s._deps(g, ("cc",), [ccsrc.ap()], [ccgat.ap()])
            g.collective_compute(
                "AllGather", ALU.bypass, [[0, 1, 2, 3, 4, 5, 6, 7]],
                [ccsrc.ap().opt()], [ccgat.ap().opt()],
            ).then_inc(s.cc_sem)
            s._record(("cc",), 1, [ccsrc.ap()], [ccgat.ap()])

    es1.close()
    s.barrier()
    if not full:
        s.fence(sy)
        es.close()
        nc.finalize()
        return nc

    FM = 8
    es2 = contextlib.ExitStack()
    _ees2 = es2.enter_context
    pC = _ees2(nc.psum_tensor("pC", [128, 2048], F32))
    pU = _ees2(nc.psum_tensor("pU", [128, 1024], BF16))
    w2p = _ees2(nc.sbuf_tensor("w2p", [128, 8 * 512], BF16))
    xnbt = _ees2(nc.sbuf_tensor("xnbt", [128, DM], BF16))
    xnT4 = _ees2(nc.sbuf_tensor("xnT4", [128, TPM * DM], BF16))
    xqT = _ees2(nc.sbuf_tensor("xqT", [128, TPM * DM], BF16))
    pmega = _ees2(nc.sbuf_tensor("pmega", [128, TPM * FFN_I], F32))
    gw = _ees2(nc.sbuf_tensor("gw", [128, 2 * 8 * 512], BF16))
    uw = _ees2(nc.sbuf_tensor("uw", [128, 8 * 512], BF16))
    gstage2 = [_ees2(nc.sbuf_tensor(f"gstage{i}", [128, 512], F32))
               for i in (0, 1)]
    ustage2 = [_ees2(nc.sbuf_tensor(f"ustage{i}", [128, 512], F32))
               for i in (0, 1)]
    scg = _ees2(nc.sbuf_tensor("scg", [128, 3 * TPM], F32))
    xq2b = _ees2(nc.sbuf_tensor("xq2b", [128, FFN_I], BF16))
    xq2T = _ees2(nc.sbuf_tensor("xq2T", [128, TPM * FFN_I], BF16))
    dw = _ees2(nc.sbuf_tensor("dw", [128, 2 * 2048], BF16))
    gout = _ees2(nc.sbuf_tensor("gout", [128, DM], F32))
    if True:

        w2r = w2bf.ap().rearrange("(n p) f -> p n f", p=128)
        xnsp = xnspill.ap().rearrange("(t p) f -> t p f", p=128)
        xqsp = xqspill.ap().rearrange("(t p) f -> t p f", p=128)
        u2sp = u2spill.ap().rearrange("(t p) f -> t p f", p=128)
        gsp = gspill.ap().rearrange("(t p) f -> t p f", p=128)
        wgr = wqg.ap().rearrange("(n p) f -> p n f", p=128)
        wur = wqu.ap().rearrange("(n p) f -> p n f", p=128)
        wdr = wqd.ap().rearrange("(n p) f -> p n f", p=128)

        for fm in range(FM):
            for tt in range(TPM):
                t = fm * TPM + tt + 1
                s.dma(sy, xnbt[:], xnsp[t - 1])
                for k in range(8):
                    s.op(te, te.transpose, pU[:, k * 128:(k + 1) * 128],
                         xnbt[:, k * 128:(k + 1) * 128], id_b[:])
                s.op(sc, sc.copy, xnT4[:, tt * DM:(tt + 1) * DM], pU[:])
                s.dma(sy, xnbt[:], xqsp[t - 1])
                for k in range(8):
                    s.op(te, te.transpose, pU[:, k * 128:(k + 1) * 128],
                         xnbt[:, k * 128:(k + 1) * 128], id_b[:])
                s.op(sc, sc.copy, xqT[:, tt * DM:(tt + 1) * DM], pU[:])
                s.op(v, v.tensor_mul, scg[:, 3 * tt:3 * tt + 1],
                     isx_buf[:, t - 1:t], mcols[:, 1:2])
                s.op(v, v.tensor_mul, scg[:, 3 * tt + 1:3 * tt + 2],
                     isx_buf[:, t - 1:t], mcols[:, 3:4])
            for j in range(4):
                s.dma(sy, w2p[:].rearrange("p (n f) -> p n f", n=8),
                      w2r[:, :, j * 512:(j + 1) * 512])
                for tt in range(TPM):
                    t = fm * TPM + tt + 1
                    gstage = gstage2[tt % 2]
                    for k in range(8):
                        s.op(te, te.matmul, pC[:, tt * 512:(tt + 1) * 512],
                             xnT4[:, tt * DM + k * 128:tt * DM + (k + 1) * 128],
                             w2p[:, k * 512:(k + 1) * 512],
                             start=(k == 0), stop=(k == 7))
                    s.op(sc, sc.activation, gstage[:],
                         pC[:, tt * 512:(tt + 1) * 512], AF.Gelu)
                    s.dma(g, u2sp[t - 1][:, j * 512:(j + 1) * 512], gstage[:])
            for j in range(8):
                bo = (j % 2) * 4096
                s.dma(sy if j % 2 == 0 else g, gw[:, bo:bo + 4096].rearrange(
                    "p (n f) -> p n f", n=8), wgr[:, :, j * 512:(j + 1) * 512])
                for tt in range(TPM):
                    for k in range(8):
                        s.op(te, te.matmul, pC[:, tt * 512:(tt + 1) * 512],
                             xqT[:, tt * DM + k * 128:tt * DM + (k + 1) * 128],
                             gw[:, bo + k * 512:bo + (k + 1) * 512],
                             start=(k == 0), stop=(k == 7))
                s.dma(g if j % 2 == 0 else sy, uw[:].rearrange(
                    "p (n f) -> p n f", n=8), wur[:, :, j * 512:(j + 1) * 512])
                for tt in range(TPM):
                    gstage, ustage = gstage2[tt % 2], ustage2[tt % 2]
                    s.op(sc, sc.activation, gstage[:],
                         pC[:, tt * 512:(tt + 1) * 512], AF.Silu,
                         bias=0.0, scale=scg[:, 3 * tt:3 * tt + 1])
                    for k in range(8):
                        s.op(te, te.matmul, pC[:, tt * 512:(tt + 1) * 512],
                             xqT[:, tt * DM + k * 128:tt * DM + (k + 1) * 128],
                             uw[:, k * 512:(k + 1) * 512],
                             start=(k == 0), stop=(k == 7))
                    s.op(sc, sc.activation, ustage[:],
                         pC[:, tt * 512:(tt + 1) * 512], AF.Copy,
                         bias=0.0, scale=scg[:, 3 * tt + 1:3 * tt + 2])
                    s.op(v, v.tensor_mul,
                         pmega[:, tt * FFN_I + j * 512:
                               tt * FFN_I + (j + 1) * 512],
                         gstage[:], ustage[:])
            for tt in range(TPM):
                t = fm * TPM + tt + 1
                pm = pmega[:, tt * FFN_I:(tt + 1) * FFN_I]
                s.op(v, v.tensor_reduce, colA[:, 0:1], pm, AX.X, ALU.max,
                     apply_absolute_value=True)
                s.op(v, v.tensor_scalar_max, colA[:, 0:1], colA[:, 0:1], 1e-5)
                s.op(v, v.reciprocal, colA[:, 1:2], colA[:, 0:1])
                s.op(v, v.tensor_scalar_mul, colA[:, 1:2], colA[:, 1:2],
                     127.0)
                s.op(v, v.tensor_scalar_mul, colA[:, 2:3], colA[:, 0:1],
                     1.0 / 127.0)
                s.op(v, v.tensor_mul, scg[:, 3 * tt + 2:3 * tt + 3],
                     colA[:, 2:3], mcols[:, 5:6])
                s.op(v, v.tensor_scalar, pm, pm, colA[:, 1:2], MAGIC,
                     ALU.mult, ALU.add)
                s.op(v, v.tensor_scalar, pm, pm, -MAGIC, 127.0,
                     ALU.add, ALU.min)
                s.op(v, v.tensor_scalar, xq2b[:], pm, -128.0, None,
                     ALU.max)
                for k0 in range(0, 32, 8):
                    for i in range(8):
                        s.op(te, te.transpose, pU[:, i * 128:(i + 1) * 128],
                             xq2b[:, (k0 + i) * 128:(k0 + i + 1) * 128],
                             id_b[:])
                    s.op(sc, sc.copy,
                         xq2T[:, tt * FFN_I + k0 * 128:
                              tt * FFN_I + (k0 + 8) * 128], pU[:])
            for hp in range(2):
                for kb in range(16):
                    bo = (kb % 2) * 2048
                    s.dma(sy if kb % 2 == 0 else g,
                          dw[:, bo:bo + 2048].rearrange(
                              "p (n f) -> p n f", n=2),
                          wdr[:, 2 * kb:2 * kb + 2, :])
                    for kk in range(2):
                        k = 2 * kb + kk
                        for tt in (2 * hp, 2 * hp + 1):
                            po = (tt % 2) * 1024
                            for q in range(2):
                                s.op(te, te.matmul,
                                     pC[:, po + q * 512:po + (q + 1) * 512],
                                     xq2T[:, tt * FFN_I + k * 128:
                                          tt * FFN_I + (k + 1) * 128],
                                     dw[:, bo + kk * 1024 + q * 512:
                                        bo + kk * 1024 + (q + 1) * 512],
                                     start=(k == 0), stop=(k == 31))
                for tt in (2 * hp, 2 * hp + 1):
                    t = fm * TPM + tt + 1
                    po = (tt % 2) * 1024
                    s.op(sc, sc.activation, gout[:], pC[:, po:po + 1024],
                         AF.Silu, bias=0.0, scale=scg[:, 3 * tt + 2:3 * tt + 3])
                    s.dma(g, gsp[t - 1], gout[:])

    es2.close()
    s.barrier()
    es3 = contextlib.ExitStack()
    _ees3 = es3.enter_context
    pE = _ees3(nc.psum_tensor("pE", [128, 2048], F32))
    pF = _ees3(nc.psum_tensor("pF", [128, 1024], F32))
    pV = _ees3(nc.psum_tensor("pV", [128, 1024], BF16))
    w3sb = _ees3(nc.sbuf_tensor("w3sb", [128, 16 * DM], BF16))
    gat = _ees3(nc.sbuf_tensor("gat", [DS, DI], F32))
    rmb = _ees3(nc.sbuf_tensor("rmb", [DS, 8], F32))
    sin_a = _ees3(nc.sbuf_tensor("sin_a", [DS, DI], F32))
    sinbf = _ees3(nc.sbuf_tensor("sinbf", [DS, DI], BF16))
    lngbc = _ees3(nc.sbuf_tensor("lngbc", [128, DI], F32))
    lnbbc = _ees3(nc.sbuf_tensor("lnbbc", [128, DI], F32))
    yt2 = [_ees3(nc.sbuf_tensor(f"yt{i}", [128, DI], F32)) for i in (0, 1)]
    u2t2 = [_ees3(nc.sbuf_tensor(f"u2t{i}", [128, DI], F32)) for i in (0, 1)]
    gt2 = [_ees3(nc.sbuf_tensor(f"gt{i}", [128, DM], F32)) for i in (0, 1)]
    ut2 = [_ees3(nc.sbuf_tensor(f"ut{i}", [128, DM], F32)) for i in (0, 1)]
    h22 = [_ees3(nc.sbuf_tensor(f"h2{i}", [128, DI], F32)) for i in (0, 1)]
    sq22 = [_ees3(nc.sbuf_tensor(f"sq2{i}", [128, DI], F32)) for i in (0, 1)]
    hnb2 = [_ees3(nc.sbuf_tensor(f"hnb{i}", [128, DI], BF16)) for i in (0, 1)]
    hnT2 = [_ees3(nc.sbuf_tensor(f"hnT{i}", [128, 16 * 128], BF16))
            for i in (0, 1)]
    Fsb2 = [_ees3(nc.sbuf_tensor(f"Fsb{i}", [128, NH], F32)) for i in (0, 1)]
    prow2 = [_ees3(nc.sbuf_tensor(f"prow{i}", [1, NH], F32)) for i in (0, 1)]
    ssb = _ees3(nc.sbuf_tensor("ssb", [128, DM], F32))
    osb2 = [_ees3(nc.sbuf_tensor(f"osb{i}", [128, DM], F32)) for i in (0, 1)]
    if True:

        w3r = w3bf.ap().rearrange("(n p) f -> n p f", p=128)
        for i in range(16):
            s.dma(sy, w3sb[:, i * DM:(i + 1) * DM], w3r[i])
        h2, sq2 = h22[0], sq22[0]
        s.dma(sy, h2[0:1, :], lng)
        s.dma(sy, sq2[0:1, :], lnb)
        for q in range(4):
            s.op(te, te.matmul, pE[:, q * 512:(q + 1) * 512], ones_r[:],
                 h2[0:1, q * 512:(q + 1) * 512], start=True, stop=True)
            s.op(te, te.matmul, pF[:, (q % 2) * 512:(q % 2) * 512 + 512],
                 ones_r[:], sq2[0:1, q * 512:(q + 1) * 512],
                 start=True, stop=True)
            s.op(v, v.tensor_copy, lngbc[:, q * 512:(q + 1) * 512],
                 pE[:, q * 512:(q + 1) * 512])
            s.op(v, v.tensor_copy, lnbbc[:, q * 512:(q + 1) * 512],
                 pF[:, (q % 2) * 512:(q % 2) * 512 + 512])
        s.dma(sy, rmb[0:1, 0:8], rmask)
        s.op(te, te.matmul, pF[0:DS, 0:8], ones_r[0:1, 0:DS], rmb[0:1, 0:8],
             start=True, stop=True)
        s.op(v, v.tensor_copy, rmb[:], pF[0:DS, 0:8])
        s.op(v, v.memset, sin_a[:], 0.0)
        gr = ccgat.ap().rearrange("(r n) f -> r n f", n=DS)
        for r in range(8):
            s.dma(sy, gat[:], gr[r])
            s.op(v, v.scalar_tensor_tensor, sin_a[:],
                 gat[:], rmb[:, r:r + 1], sin_a[:],
                 ALU.mult, ALU.add)
        s.op(sc, sc.copy, sinbf[:], sin_a[:])

        ysp = yspill.ap().rearrange("(t p) f -> t p f", p=128)
        u2sp = u2spill.ap().rearrange("(t p) f -> t p f", p=128)
        gsp = gspill.ap().rearrange("(t p) f -> t p f", p=128)
        orr = o.rearrange("(t p) f -> t p f", p=128)

        for t in range(1, NT + 1):
            pp = t % 2
            yt, u2t, gt, ut = yt2[pp], u2t2[pp], gt2[pp], ut2[pp]
            h2, sq2, hnb, hnT = h22[pp], sq22[pp], hnb2[pp], hnT2[pp]
            Fsb, prow, osb = Fsb2[pp], prow2[pp], osb2[pp]
            s.dma(sy, yt[:], ysp[t - 1])
            s.dma(g, u2t[:], u2sp[t - 1])
            s.dma(sy, gt[:], gsp[t - 1])
            s.dma(g, ut[:], ur[t])
            s.op(sc, sc.activation, prow[:], Ptot[0:1, (t - 1) * NH:t * NH],
                 AF.Exp)
            s.op(te, te.matmul, pF[:, 0:NH], ones_r[:], prow[:],
                 start=True, stop=True)
            s.op(v, v.tensor_tensor, Fsb[:], pF[:, 0:NH],
                 expa_buf[:, (t - 1) * NH:t * NH], ALU.mult)
            for q in range(4):
                s.op(te, te.matmul, pE[:, q * 512:(q + 1) * 512],
                     CT_buf[:, (t - 1) * 128:t * 128],
                     sinbf[:, q * 512:(q + 1) * 512], start=True, stop=True)
            for h in range(NH):
                s.op(v, v.scalar_tensor_tensor, yt[:, h * HD:(h + 1) * HD],
                     pE[:, h * HD:(h + 1) * HD], Fsb[:, h:h + 1],
                     yt[:, h * HD:(h + 1) * HD], ALU.mult, ALU.add)
            s.op(v, v.tensor_mul, h2[:], u2t[:], yt[:])
            s.op(v, v.reduce_sum, colA[:, 8 * pp + 0:8 * pp + 1], h2[:], AX.X)
            s.op(v, v.tensor_scalar_mul, colA[:, 8 * pp + 1:8 * pp + 2], colA[:, 8 * pp + 0:8 * pp + 1], 1.0 / DI)
            s.op(sc, sc.square, sq2[:], h2[:])
            s.op(v, v.reduce_sum, colA[:, 8 * pp + 2:8 * pp + 3], sq2[:], AX.X)
            s.op(v, v.tensor_scalar_mul, colA[:, 8 * pp + 3:8 * pp + 4], colA[:, 8 * pp + 2:8 * pp + 3], 1.0 / DI)
            s.op(v, v.tensor_mul, colA[:, 8 * pp + 4:8 * pp + 5], colA[:, 8 * pp + 1:8 * pp + 2], colA[:, 8 * pp + 1:8 * pp + 2])
            s.op(v, v.tensor_sub, colA[:, 8 * pp + 5:8 * pp + 6], colA[:, 8 * pp + 3:8 * pp + 4], colA[:, 8 * pp + 4:8 * pp + 5])
            s.op(v, v.tensor_scalar_add, colA[:, 8 * pp + 5:8 * pp + 6], colA[:, 8 * pp + 5:8 * pp + 6], LN_EPS)
            s.op(sc, sc.sqrt, colA[:, 8 * pp + 6:8 * pp + 7], colA[:, 8 * pp + 5:8 * pp + 6])
            s.op(v, v.reciprocal, colA[:, 8 * pp + 6:8 * pp + 7], colA[:, 8 * pp + 6:8 * pp + 7])
            s.op(v, v.tensor_mul, colA[:, 8 * pp + 7:8 * pp + 8], colA[:, 8 * pp + 1:8 * pp + 2], colA[:, 8 * pp + 6:8 * pp + 7])
            s.op(v, v.tensor_scalar_mul, colA[:, 8 * pp + 7:8 * pp + 8], colA[:, 8 * pp + 7:8 * pp + 8], -1.0)
            s.op(sc, sc.activation, sq2[:], h2[:], AF.Identity,
                 bias=colA[:, 8 * pp + 7:8 * pp + 8], scale=colA[:, 8 * pp + 6:8 * pp + 7])
            s.op(v, v.tensor_mul, sq2[:], sq2[:], lngbc[:])
            s.op(v, v.tensor_add, hnb[:], sq2[:], lnbbc[:])
            for k0 in range(0, 16, 8):
                for i in range(8):
                    s.op(te, te.transpose, pV[:, i * 128:(i + 1) * 128],
                         hnb[:, (k0 + i) * 128:(k0 + i + 1) * 128], id_b[:])
                s.op(sc, sc.copy, hnT[:, k0 * 128:(k0 + 8) * 128], pV[:])
            for j in range(2):
                for k in range(16):
                    s.op(te, te.matmul, pF[:, j * 512:(j + 1) * 512],
                         hnT[:, k * 128:(k + 1) * 128],
                         w3sb[:, k * DM + j * 512:k * DM + (j + 1) * 512],
                         start=(k == 0), stop=(k == 15))
            s.op(v, v.tensor_copy, ssb[:], pF[:, 0:DM])
            s.op(v, v.tensor_mul, osb[:], ssb[:], gt[:])
            s.op(v, v.tensor_add, osb[:], osb[:], ut[:])
            s.dma(g, orr[t - 1], osb[:])

        s.fence(sy)
    es3.close()
    es.close()
    nc.finalize()
    return nc


def make_in_maps(inputs):
    u_full = np.ascontiguousarray(inputs["L0_attn_u"], dtype=np.float32)
    w1t = np.ascontiguousarray(
        inputs["L0_attn_W_in1"].T[:, 2048:2048 + NP], dtype=np.float32)
    convwt = np.ascontiguousarray(
        np.transpose(inputs["L0_attn_conv_w"], (2, 1, 0)), dtype=np.float32)
    convb = np.ascontiguousarray(
        inputs["L0_attn_conv_b"].reshape(1, CH), dtype=np.float32)
    alog = np.ascontiguousarray(
        inputs["L0_attn_A_log"].reshape(1, NH), dtype=np.float32)
    prel = np.ascontiguousarray(
        np.asarray(inputs["L0_attn_prelu_a"]).reshape(1, 1), dtype=np.float32)
    w2t = np.ascontiguousarray(inputs["L0_attn_W_in2"].T, dtype=np.float32)
    w3t = np.ascontiguousarray(inputs["L0_attn_W_in3"].T, dtype=np.float32)
    wgt = np.ascontiguousarray(inputs["L0_ffn_w_gate"].T, dtype=np.float32)
    wut = np.ascontiguousarray(inputs["L0_ffn_w_up"].T, dtype=np.float32)
    wdt = np.ascontiguousarray(inputs["L0_ffn_w_down"].T, dtype=np.float32)
    lng = np.ascontiguousarray(
        inputs["L0_attn_ln_g"].reshape(1, DI), dtype=np.float32)
    lnb = np.ascontiguousarray(
        inputs["L0_attn_ln_b"].reshape(1, DI), dtype=np.float32)

    in_maps = []
    for c in range(8):
        b, half = c % 4, c // 4
        span = u_full[b, half * TPC:(half + 1) * TPC]
        head = np.zeros((128, DM), dtype=np.float32)
        if half == 1:
            head[125:128] = u_full[b, TPC - 3:TPC]
        ucore = np.ascontiguousarray(np.concatenate([head, span], axis=0))
        rmask = np.zeros((1, 8), dtype=np.float32)
        if half == 1:
            rmask[0, b] = 1.0
        in_maps.append(dict(
            u=ucore, w1t=w1t, convwt=convwt, convb=convb, alog=alog,
            prel=prel, w2t=w2t, w3t=w3t, wgt=wgt, wut=wut, wdt=wdt,
            lng=lng, lnb=lnb, rmask=rmask))
    return in_maps


def kernel(**inputs):
    nc = build_program()
    in_maps = make_in_maps(inputs)
    res = run_bass_kernel_spmd(nc, in_maps, core_ids=list(range(8)))
    out = np.empty((4, 2 * TPC, DM), dtype=np.float32)
    for c in range(8):
        b, half = c % 4, c // 4
        out[b, half * TPC:(half + 1) * TPC] = np.asarray(res.results[c]["o"])
    return out


# version 08 (final)
# speedup vs baseline: 1.1294x; 1.0545x over previous
import contextlib
import numpy as np
import concourse.bass as bass
import concourse.mybir as mybir
from concourse.bass_utils import run_bass_kernel_spmd

F32 = mybir.dt.float32
BF16 = mybir.dt.bfloat16
AF = mybir.ActivationFunctionType
ALU = mybir.AluOpType
AX = mybir.AxisListType

MAGIC = float(np.float32(1.5 * 2 ** 23))

DM = 1024
DI = 2048
DS = 64
HD = 64
NH = 32
CH = 2176
NP = 2208
TPC = 4096
NT = 32
TPM = 4
NMEGA = 8
LN_EPS = 1e-5
FFN_I = 4096

IN_CH = [(0, 512), (512, 512), (1024, 512), (1536, 512), (2048, 160)]
CONV_CH = [(0, 512), (512, 512), (1024, 512), (1536, 512), (2048, 128)]


class Ser:

    def __init__(self, sems):
        self.sems = sems
        self.idx = 0
        self.count = 0
        self.prev = None

    def _pre(self, eng):
        if self.prev is not None:
            eng.wait_ge(*self.prev)
            self.prev = None
        elif self.count > 0:
            eng.wait_ge(self.sems[self.idx], self.count)

    def _post(self, inst, inc):
        inst.then_inc(self.sems[self.idx], inc)
        self.count += inc
        if self.count > 40000:
            self.prev = (self.sems[self.idx], self.count)
            self.idx += 1
            self.count = 0

    def dma(self, eng, out, in_):
        self._pre(eng)
        self._post(eng.dma_start(out=out, in_=in_), 16)

    def op(self, eng, fn, *args, **kw):
        self._pre(eng)
        self._post(fn(*args, **kw), 1)

    def fence(self, eng):
        if self.prev is not None:
            eng.wait_ge(*self.prev)
        elif self.count > 0:
            eng.wait_ge(self.sems[self.idx], self.count)


FULLRANGE_TENSORS = set()


def _ap_range(ap):
    t = ap.tensor
    if t.name in FULLRANGE_TENSORS:
        return (0, 1 << 20, 0, 1 << 40)
    off = int(ap.offset)
    pairs = [(int(p[0]), int(p[1])) for p in ap.ap]
    if type(t).__name__.startswith("DRam"):
        hi = off + sum(st * (ct - 1) for st, ct in pairs) + 1
        return (0, 1, off, hi)
    shape = list(t.shape)
    rs = 1
    for d in shape[1:]:
        rs *= d
    p0 = off // rs
    f0 = off % rs
    pspan = 1
    fhi = f0
    for st, ct in pairs:
        ext = st * (ct - 1)
        if st != 0 and st % rs == 0:
            pspan += ext // rs
        else:
            fhi += ext
    fhi += 1
    if type(t).__name__.startswith("PSum"):
        be = 2048 // (2 if t.dtype == BF16 else 4)
        f0 = (f0 // be) * be
        fhi = ((fhi + be - 1) // be) * be
        return (0, 128, f0, fhi)
    return (p0, p0 + pspan, f0, fhi)


def _is_ap(x):
    return hasattr(x, "tensor") and hasattr(x, "ap") and hasattr(x, "offset")


def _overlap(a, b):
    return a[0] < b[1] and b[0] < a[1] and a[2] < b[3] and b[2] < a[3]


class Auto:

    serialize_all = False
    serialize_dma = False

    def __init__(self, nc, engs, esg):
        self.last = None
        self.nc = nc
        self.names = {id(e): n for e, n in engs}
        self.byname = {n: e for e, n in engs}
        self.osem = {n: esg(nc.semaphore(f"O{n}")) for _, n in engs}
        self.dsem = {n: esg(nc.semaphore(f"D{n}")) for _, n in engs}
        self.ocnt = {n: 0 for _, n in engs}
        self.dcnt = {n: 0 for _, n in engs}
        self.cc_sem = esg(nc.semaphore("CC"))
        self.vc = {}
        self.tmap = {}

    def _stream(self, eng, dma):
        return ("dma" if dma else "op", self.names[id(eng)])

    def _sem_wait(self, eng, st, sq):
        if st == ("cc",):
            eng.wait_ge(self.cc_sem, 1)
            return
        kind, n = st
        if kind == "op":
            eng.wait_ge(self.osem[n], sq)
        else:
            eng.wait_ge(self.dsem[n], 16 * sq)

    def _deps(self, eng, S, reads, writes):
        need = {}

        def add(st, sq):
            if st == S and S == ("op", "PE"):
                return
            if need.get(st, -1) < sq:
                need[st] = sq

        for ap in reads:
            ent = self.tmap.get(ap.tensor.name)
            if not ent:
                continue
            r = _ap_range(ap)
            for rec in ent[0]:
                if _overlap(r, rec[0]):
                    add(rec[1], rec[2])
        for ap in writes:
            ent = self.tmap.get(ap.tensor.name)
            if not ent:
                continue
            r = _ap_range(ap)
            for rec in ent[0]:
                if _overlap(r, rec[0]):
                    add(rec[1], rec[2])
            for rec in ent[1]:
                if _overlap(r, rec[0]):
                    add(rec[1], rec[2])
        vc = self.vc.setdefault(S, {})
        for st, sq in sorted(need.items(), key=lambda x: str(x[0])):
            if st == S and st[0] == "op":
                pass
            if vc.get(st, 0) >= sq:
                continue
            self._sem_wait(eng, st, sq)
            vc[st] = sq

    def _record(self, S, seq, reads, writes):
        for ap in writes:
            ent = self.tmap.setdefault(ap.tensor.name, [[], []])
            r = _ap_range(ap)

            def covered(rec):
                q = rec[0]
                return (r[0] <= q[0] and q[1] <= r[1]
                        and r[2] <= q[2] and q[3] <= r[3])

            ent[0] = [x for x in ent[0] if not covered(x)]
            ent[1] = [x for x in ent[1] if not covered(x)]
            ent[0].append((r, S, seq))
        for ap in reads:
            ent = self.tmap.setdefault(ap.tensor.name, [[], []])
            ent[1].append((_ap_range(ap), S, seq))

    def barrier(self):
        for n, e in self.byname.items():
            for m, c in self.ocnt.items():
                if c > 0:
                    e.wait_ge(self.osem[m], c)
            for m, c in self.dcnt.items():
                if c > 0:
                    e.wait_ge(self.dsem[m], 16 * c)
        for n in self.byname:
            for S in (("op", n), ("dma", n)):
                vc = self.vc.setdefault(S, {})
                for m, c in self.ocnt.items():
                    vc[("op", m)] = c
                for m, c in self.dcnt.items():
                    vc[("dma", m)] = c
        keep = {k: v for k, v in self.tmap.items() if k == "ccgat"}
        self.tmap = keep

    def _ser(self, eng, S, isdma=False):
        if self.serialize_dma and self.last is not None and isdma:
            st, sq = self.last
            vc = self.vc.setdefault(S, {})
            if st != S and vc.get(st, 0) < sq:
                self._sem_wait(eng, st, sq)
                vc[st] = sq
        if self.serialize_all and self.last is not None:
            st, sq = self.last
            vc = self.vc.setdefault(S, {})
            if st != S and vc.get(st, 0) < sq:
                self._sem_wait(eng, st, sq)
                vc[st] = sq

    def op(self, eng, fn, *args, **kw):
        S = self._stream(eng, False)
        aps = [a for a in args if _is_ap(a)]
        aps += [a for a in kw.values() if _is_ap(a)]
        writes, reads = aps[:1], aps[1:]
        self._deps(eng, S, reads, writes)
        self._ser(eng, S)
        if self.serialize_dma and getattr(self, "lastdma", None) is not None:
            st, sq = self.lastdma
            vc = self.vc.setdefault(S, {})
            if st != S and vc.get(st, 0) < sq:
                self._sem_wait(eng, st, sq)
                vc[st] = sq
        n = self.names[id(eng)]
        self.ocnt[n] += 1
        fn(*args, **kw).then_inc(self.osem[n], 1)
        self._record(S, self.ocnt[n], reads, writes)
        self.last = (S, self.ocnt[n])

    def dma(self, eng, out, in_):
        S = self._stream(eng, True)
        self._deps(eng, S, [in_], [out])
        n = self.names[id(eng)]
        self._ser(eng, S, isdma=True)
        vc = self.vc.setdefault(S, {})
        if self.dcnt[n] > 0 and vc.get(S, 0) < self.dcnt[n]:
            eng.wait_ge(self.dsem[n], 16 * self.dcnt[n])
            vc[S] = self.dcnt[n]
        self.dcnt[n] += 1
        eng.dma_start(out=out, in_=in_).then_inc(self.dsem[n], 16)
        self._record(S, self.dcnt[n], [in_], [out])
        self.last = (S, self.dcnt[n])
        self.lastdma = (S, self.dcnt[n])


    def fence(self, eng):
        for n, c in self.ocnt.items():
            if c > 0:
                eng.wait_ge(self.osem[n], c)
        for n, c in self.dcnt.items():
            if c > 0:
                eng.wait_ge(self.dsem[n], 16 * c)


def build_program(debug=False, n_megas=NMEGA, full=True, do_cc=True, parts="fcs"):
    nc = bass.Bass("TRN2", target_bir_lowering=False)

    def din(name, shape):
        return nc.dram_tensor(name, shape, F32, kind="ExternalInput").ap()

    u = din("u", [128 * (NT + 1), DM])
    w1t = din("w1t", [DM, NP])
    convwt = din("convwt", [4, CH, CH])
    convb = din("convb", [1, CH])
    alog = din("alog", [1, NH])
    prel = din("prel", [1, 1])
    w2t = din("w2t", [DM, DI])
    w3t = din("w3t", [DI, DM])
    wgt = din("wgt", [DM, FFN_I])
    wut = din("wut", [DM, FFN_I])
    wdt = din("wdt", [FFN_I, DM])
    lng = din("lng", [1, DI])
    lnb = din("lnb", [1, DI])
    rmask = din("rmask", [1, 8])

    o = nc.dram_tensor("o", [TPC, DM], F32, kind="ExternalOutput").ap()

    dkind = "ExternalOutput" if debug else "Internal"

    convw_bf = nc.dram_tensor("convw_bf", [17 * 4 * 128, CH], BF16)
    w2bf = nc.dram_tensor("w2bf", [DM, DI], BF16)
    w3bf = nc.dram_tensor("w3bf", [DI, DM], BF16)
    wqg = nc.dram_tensor("wqg", [DM, FFN_I], BF16)
    wqu = nc.dram_tensor("wqu", [DM, FFN_I], BF16)
    wqd = nc.dram_tensor("wqd", [FFN_I, DM], BF16)
    xnspill = nc.dram_tensor("xnspill", [TPC, DM], BF16, kind=dkind)
    xqspill = nc.dram_tensor("xqspill", [TPC, DM], BF16, kind=dkind)
    yspill = nc.dram_tensor("yspill", [TPC, DI], F32, kind=dkind)
    u2spill = nc.dram_tensor("u2spill", [TPC, DI], F32, kind=dkind)
    gspill = nc.dram_tensor("gspill", [TPC, DM], F32, kind=dkind)
    ccsrc = nc.dram_tensor("ccsrc", [DS, NH * HD], F32)
    ccgat = nc.dram_tensor("ccgat", [8 * DS, NH * HD], F32)

    es = contextlib.ExitStack()
    esg = es.enter_context

    v, sc, te, g, sy = nc.vector, nc.scalar, nc.tensor, nc.gpsimd, nc.sync
    s = Auto(nc, [(te, "PE"), (v, "DVE"), (sc, "ACT"), (g, "POOL"),
                  (sy, "SP")], esg)

    ones_c = esg(nc.sbuf_tensor("ones_c", [128, 1], F32))
    ones_r = esg(nc.sbuf_tensor("ones_r", [1, 128], F32))
    ones_rb = esg(nc.sbuf_tensor("ones_rb", [1, 128], BF16))
    id_f = esg(nc.sbuf_tensor("id_f", [128, 128], F32))
    id_b = esg(nc.sbuf_tensor("id_b", [128, 128], BF16))
    triu = esg(nc.sbuf_tensor("triu", [128, 128], F32))
    c127 = esg(nc.sbuf_tensor("c127", [128, 1], F32))
    a_col = esg(nc.sbuf_tensor("a_col", [128, 1], F32))
    A_bc = esg(nc.sbuf_tensor("A_bc", [128, NH], F32))
    convb_bf = esg(nc.sbuf_tensor("convb_bf", [1, CH], BF16))
    mcols = esg(nc.sbuf_tensor("mcols", [128, 6], F32))
    isx_buf = esg(nc.sbuf_tensor("isx_buf", [128, NT], F32))
    expa_buf = esg(nc.sbuf_tensor("expa_buf", [128, NT * NH], F32))
    CT_buf = esg(nc.sbuf_tensor("CT_buf", [DS, NT * 128], BF16))
    Ptot = esg(nc.sbuf_tensor("Ptot", [1, NT * NH], F32))
    Rrow = esg(nc.sbuf_tensor("Rrow", [1, NH], F32))
    state = esg(nc.sbuf_tensor("state", [DS, DI], F32))
    colA = esg(nc.sbuf_tensor("colA", [128, 24], F32))
    rowA = esg(nc.sbuf_tensor("rowA", [1, 64], F32))

    with nc.psum_tensor("pp", [128, 2048], F32) as pp, \
         nc.sbuf_tensor("wstage", [128, 4096], F32) as wstage, \
         nc.sbuf_tensor("wstageb", [128, 4096], F32) as wstageb, \
         nc.sbuf_tensor("qstage", [128, 4096], BF16) as qstage, \
         nc.sbuf_tensor("qstageb", [128, 4096], BF16) as qstageb, \
         nc.sbuf_tensor("accc", [128, 1], F32) as accc, \
         nc.sbuf_tensor("vals", [1, 8], F32) as vals:
        wst2 = [wstage, wstageb]
        qst2 = [qstage, qstageb]

        s.op(v, v.memset, ones_c[:], 1.0)
        s.op(v, v.memset, ones_r[:], 1.0)
        s.op(sc, sc.copy, ones_rb[:], ones_r[:])
        s.op(v, v.memset, c127[:], 127.0)
        s.op(v, v.memset, id_f[:], 1.0)
        s.op(g, g.affine_select, id_f[:], id_f[:], [[-1, 128]], ALU.is_equal,
             0.0, base=0, channel_multiplier=1)
        s.op(sc, sc.copy, id_b[:], id_f[:])
        s.op(v, v.memset, triu[:], 1.0)
        s.op(g, g.affine_select, triu[:], triu[:], [[1, 128]], ALU.is_ge,
             0.0, base=0, channel_multiplier=-1)
        s.op(v, v.memset, Rrow[:], 0.0)
        s.op(v, v.memset, state[:], 0.0)

        s.dma(sy, vals[0:1, 0:1], prel)
        s.op(te, te.matmul, pp[:, 0:1], ones_r[:], vals[0:1, 0:1],
             start=True, stop=True)
        s.op(v, v.tensor_copy, a_col[:], pp[:, 0:1])
        s.dma(sy, rowA[0:1, 0:32], alog)
        s.op(sc, sc.activation, rowA[0:1, 32:64], rowA[0:1, 0:32], AF.Exp)
        s.op(v, v.tensor_scalar_mul, rowA[0:1, 32:64], rowA[0:1, 32:64], -1.0)
        s.op(te, te.matmul, pp[:, 0:NH], ones_r[:], rowA[0:1, 32:64],
             start=True, stop=True)
        s.op(v, v.tensor_copy, A_bc[:], pp[:, 0:NH])

        s.dma(sy, wstage[0:1, 0:CH], convb)
        s.op(sc, sc.copy, convb_bf[:], wstage[0:1, 0:CH])

        cw = convwt.rearrange("k (c p) i -> k c p i", p=128)
        cb = convw_bf.ap().rearrange("(c k p) i -> c k p i", k=4, p=128)
        for c in range(17):
            for k in range(4):
                pq = (c * 4 + k) % 2
                s.dma(sy, wst2[pq][:, 0:CH], cw[k, c])
                s.op(sc, sc.copy, qst2[pq][:, 0:CH], wst2[pq][:, 0:CH])
                s.dma(g, cb[c, k], qst2[pq][:, 0:CH])

        for src, dst, rows, cols in ((w2t, w2bf, DM, DI), (w3t, w3bf, DI, DM)):
            sr = src.rearrange("(n p) f -> n p f", p=128)
            dr = dst.ap().rearrange("(n p) f -> n p f", p=128)
            for i in range(rows // 128):
                s.dma(sy, wst2[i % 2][:, 0:cols], sr[i])
                s.op(sc, sc.copy, qst2[i % 2][:, 0:cols], wst2[i % 2][:, 0:cols])
                s.dma(g, dr[i], qst2[i % 2][:, 0:cols])

        for wi, (src, dst, rows, cols) in enumerate(
                ((wgt, wqg, DM, FFN_I), (wut, wqu, DM, FFN_I),
                 (wdt, wqd, FFN_I, DM))):
            sr = src.rearrange("(n p) f -> n p f", p=128)
            dr = dst.ap().rearrange("(n p) f -> n p f", p=128)
            nchunks = rows // 128
            s.op(v, v.memset, accc[:], 0.0)
            for i in range(nchunks):
                s.dma(sy, wst2[i % 2][:, 0:cols], sr[i])
                s.op(v, v.tensor_reduce, colA[:, 0:1], wst2[i % 2][:, 0:cols],
                     AX.X, ALU.add, apply_absolute_value=True)
                s.op(v, v.tensor_add, accc[:], accc[:], colA[:, 0:1])
            s.op(te, te.matmul, pp[0:1, 0:1], ones_c[:], accc[:],
                 start=True, stop=True)
            s.op(v, v.tensor_copy, vals[0:1, 0:1], pp[0:1, 0:1])
            s.op(v, v.tensor_scalar, vals[0:1, 1:2], vals[0:1, 0:1],
                 1.0 / (rows * cols), 1e-5, ALU.mult, ALU.max)
            s.op(v, v.reciprocal, vals[0:1, 2:3], vals[0:1, 1:2])
            s.op(te, te.matmul, pp[:, 0:2], ones_r[:], vals[0:1, 1:3],
                 start=True, stop=True)
            s.op(v, v.tensor_copy, mcols[:, 2 * wi + 1:2 * wi + 2], pp[:, 0:1])
            s.op(v, v.tensor_copy, mcols[:, 2 * wi:2 * wi + 1], pp[:, 1:2])
            for i in range(nchunks):
                ws, qs = wst2[i % 2], qst2[i % 2]
                s.dma(sy, ws[:, 0:cols], sr[i])
                s.op(v, v.tensor_scalar, ws[:, 0:cols], ws[:, 0:cols],
                     mcols[:, 2 * wi:2 * wi + 1], MAGIC, ALU.mult, ALU.add)
                s.op(v, v.tensor_scalar, ws[:, 0:cols], ws[:, 0:cols],
                     -MAGIC, 1.0, ALU.add, ALU.min)
                s.op(v, v.tensor_scalar, qs[:, 0:cols], ws[:, 0:cols],
                     -1.0, None, ALU.max)
                s.dma(g, dr[i], qs[:, 0:cols])

    s.barrier()
    w1r = w1t.rearrange("(n p) f -> n p f", p=128)
    ur = u.rearrange("(t p) f -> t p f", p=128)

    es1 = contextlib.ExitStack()
    _ees1 = es1.enter_context
    pA = _ees1(nc.psum_tensor("pA", [128, 2048], F32))
    pB = _ees1(nc.psum_tensor("pB", [128, 1536], F32))
    pT = _ees1(nc.psum_tensor("pT", [128, 1024], BF16))
    w1bf = _ees1(nc.sbuf_tensor("w1bf", [128, 8 * NP], BF16))
    su2 = [_ees1(nc.sbuf_tensor(f"su{i}", [128, DM], F32)) for i in (0, 1)]
    squ = _ees1(nc.sbuf_tensor("squ", [128, DM], F32))
    xnf = _ees1(nc.sbuf_tensor("xnf", [128, DM], F32))
    xnb2 = [_ees1(nc.sbuf_tensor(f"xnb{i}", [128, DM], BF16)) for i in (0, 1)]
    xqb2 = [_ees1(nc.sbuf_tensor(f"xqb{i}", [128, DM], BF16)) for i in (0, 1)]
    xnT2 = [_ees1(nc.sbuf_tensor(f"xnT{i}", [128, 8 * 128], BF16))
            for i in (0, 1)]
    xbc2 = [_ees1(nc.sbuf_tensor(f"xbc_tm{i}", [128, CH], BF16))
            for i in (0, 1)]
    xfm = _ees1(nc.sbuf_tensor("xfm", [128, 17 * 515], BF16))
    xcbf = _ees1(nc.sbuf_tensor("xcbf", [128, TPM * CH], BF16))
    dt_mega = _ees1(nc.sbuf_tensor("dt_mega", [128, TPM * NH], F32))
    wpiece = _ees1(nc.sbuf_tensor("wpiece", [128, 2 * 4 * 512], BF16))
    tmp5a = _ees1(nc.sbuf_tensor("tmp5a", [128, 512], F32))
    tmp5b = _ees1(nc.sbuf_tensor("tmp5b", [128, 512], F32))
    t32a2 = [_ees1(nc.sbuf_tensor(f"t32a{i}", [128, NH], F32)) for i in (0, 1)]
    t32b2 = [_ees1(nc.sbuf_tensor(f"t32b{i}", [128, NH], F32)) for i in (0, 1)]
    asb2 = [_ees1(nc.sbuf_tensor(f"asb{i}", [128, NH], F32)) for i in (0, 1)]
    nasb2 = [_ees1(nc.sbuf_tensor(f"nasb{i}", [128, NH], F32)) for i in (0, 1)]
    atr2 = [_ees1(nc.sbuf_tensor(f"atr{i}", [32, 128], F32)) for i in (0, 1)]
    wtr2 = [_ees1(nc.sbuf_tensor(f"wtr{i}", [32, 128], F32)) for i in (0, 1)]
    wsb2 = [_ees1(nc.sbuf_tensor(f"wsb{i}", [128, NH], F32)) for i in (0, 1)]
    aflat = _ees1(nc.sbuf_tensor("aflat", [1, NH * 128], F32))
    totrow2 = [_ees1(nc.sbuf_tensor(f"totrow{i}", [1, NH], F32))
               for i in (0, 1)]
    dtrow2 = [_ees1(nc.sbuf_tensor(f"dtrow{i}", [1, NH], F32))
              for i in (0, 1)]
    dstage2 = [_ees1(nc.sbuf_tensor(f"dstage{i}", [128, 512], F32))
               for i in (0, 1)]
    Gbuf = _ees1(nc.sbuf_tensor("Gbuf", [128, NH * 128], BF16))
    BTs2 = [_ees1(nc.sbuf_tensor(f"BTs{i}", [DS, 128], BF16)) for i in (0, 1)]
    Ssb2 = [_ees1(nc.sbuf_tensor(f"Ssb{i}", [128, 128], BF16)) for i in (0, 1)]
    Yfin2 = [_ees1(nc.sbuf_tensor(f"Yfin{i}", [128, DI], F32))
             for i in (0, 1)]
    xw = _ees1(nc.sbuf_tensor("xw", [128, DI], BF16))
    statebf = _ees1(nc.sbuf_tensor("statebf", [DS, DI], BF16))
    if True:

        for i in range(8):
            su = su2[0]
            s.dma(sy, su[:], w1r[i][:, 0:DM])
            s.dma(sy, squ[:], w1r[i][:, DM:2 * DM])
            s.op(sc, sc.copy, w1bf[:, i * NP:i * NP + DM], su[:])
            s.op(sc, sc.copy, w1bf[:, i * NP + DM:i * NP + 2 * DM], squ[:])
            s.dma(sy, su[:, 0:NP - 2 * DM], w1r[i][:, 2 * DM:NP])
            s.op(sc, sc.copy, w1bf[:, i * NP + 2 * DM:(i + 1) * NP],
                 su[:, 0:NP - 2 * DM])

        def front_tile(t):
            halo = (t == 0)
            mega_tt = 0 if halo else (t - 1) % TPM
            pp = t % 2
            su = su2[pp]
            xnb, xqb, xnT, xbc_tm = xnb2[pp], xqb2[pp], xnT2[pp], xbc2[pp]
            cb = 8 + pp * 8
            s.dma(sy, su[:], ur[t])
            s.op(v, v.reduce_sum, colA[:, cb + 0:cb + 1], su[:], AX.X)
            s.op(v, v.tensor_scalar_mul, colA[:, cb + 1:cb + 2], colA[:, cb + 0:cb + 1], 1.0 / DM)
            s.op(sc, sc.square, squ[:], su[:])
            s.op(v, v.reduce_sum, colA[:, cb + 2:cb + 3], squ[:], AX.X)
            s.op(v, v.tensor_scalar_mul, colA[:, cb + 3:cb + 4], colA[:, cb + 2:cb + 3], 1.0 / DM)
            s.op(v, v.tensor_mul, colA[:, cb + 4:cb + 5], colA[:, cb + 1:cb + 2], colA[:, cb + 1:cb + 2])
            s.op(v, v.tensor_sub, colA[:, cb + 5:cb + 6], colA[:, cb + 3:cb + 4], colA[:, cb + 4:cb + 5])
            s.op(v, v.tensor_scalar_add, colA[:, cb + 5:cb + 6], colA[:, cb + 5:cb + 6], LN_EPS)
            s.op(sc, sc.sqrt, colA[:, cb + 6:cb + 7], colA[:, cb + 5:cb + 6])
            s.op(v, v.reciprocal, colA[:, cb + 6:cb + 7], colA[:, cb + 6:cb + 7])
            s.op(v, v.tensor_mul, colA[:, cb + 7:cb + 8], colA[:, cb + 1:cb + 2], colA[:, cb + 6:cb + 7])
            s.op(v, v.tensor_scalar_mul, colA[:, cb + 7:cb + 8], colA[:, cb + 7:cb + 8], -1.0)
            s.op(sc, sc.activation, xnf[:], su[:], AF.Identity,
                 bias=colA[:, cb + 7:cb + 8], scale=colA[:, cb + 6:cb + 7])
            s.op(sc, sc.copy, xnb[:], xnf[:])
            if not halo:
                s.dma(g, xnspill.ap().rearrange("(t p) f -> t p f", p=128)[t - 1],
                      xnb[:])
                s.op(v, v.tensor_reduce, colA[:, cb + 0:cb + 1], xnf[:], AX.X, ALU.max,
                     apply_absolute_value=True)
                s.op(v, v.tensor_scalar_max, colA[:, cb + 0:cb + 1], colA[:, cb + 0:cb + 1], 1e-5)
                s.op(v, v.reciprocal, colA[:, cb + 1:cb + 2], colA[:, cb + 0:cb + 1])
                s.op(v, v.tensor_scalar_mul, colA[:, cb + 1:cb + 2], colA[:, cb + 1:cb + 2],
                     127.0)
                s.op(v, v.tensor_scalar_mul, isx_buf[:, t - 1:t],
                     colA[:, cb + 0:cb + 1], 1.0 / 127.0)
                s.op(v, v.tensor_scalar, squ[:], xnf[:], colA[:, cb + 1:cb + 2], MAGIC,
                     ALU.mult, ALU.add)
                s.op(v, v.tensor_scalar, squ[:], squ[:], -MAGIC, 127.0,
                     ALU.add, ALU.min)
                s.op(v, v.tensor_scalar, xqb[:], squ[:], -128.0, None, ALU.max)
                s.dma(g, xqspill.ap().rearrange("(t p) f -> t p f", p=128)[t - 1],
                      xqb[:])
            for k in range(8):
                s.op(te, te.transpose, pT[:, k * 128:(k + 1) * 128],
                     xnb[:, k * 128:(k + 1) * 128], id_b[:])
            s.op(sc, sc.copy, xnT[:], pT[:, 0:1024])
            for j, (o0, on) in enumerate(IN_CH):
                po = (j % 4) * 512
                for k in range(8):
                    s.op(te, te.matmul, pA[:, po:po + on],
                         xnT[:, k * 128:(k + 1) * 128],
                         w1bf[:, k * NP + o0:k * NP + o0 + on],
                         start=(k == 0), stop=(k == 7))
                if o0 < 2048:
                    s.op(sc, sc.copy, xbc_tm[:, o0:o0 + on], pA[:, po:po + on])
                else:
                    s.op(sc, sc.copy, xbc_tm[:, 2048:2176],
                         pA[:, po:po + 128])
                    s.op(v, v.tensor_copy,
                         dt_mega[:, mega_tt * NH:(mega_tt + 1) * NH],
                         pA[:, po + 128:po + 160])
            xfmv = xfm[:].rearrange("p (c f) -> p c f", c=17)
            pTv = pT[:, 0:1024].rearrange("p (s f) -> p s f", s=8)
            for c0 in range(0, 17, 8):
                ng = min(8, 17 - c0)
                for i in range(ng):
                    s.op(te, te.transpose, pT[:, i * 128:(i + 1) * 128],
                         xbc_tm[:, (c0 + i) * 128:(c0 + i + 1) * 128], id_b[:])
                if halo:
                    s.op(sc, sc.copy, xfmv[:, c0:c0 + ng, 0:3],
                         pTv[:, 0:ng, 125:128])
                else:
                    s.op(sc, sc.copy,
                         xfmv[:, c0:c0 + ng, 3 + mega_tt * 128:
                              3 + (mega_tt + 1) * 128],
                         pTv[:, 0:ng, :])

        def conv_mega(m):
            cbr = convw_bf.ap().rearrange("(c k p) i -> c p k i", k=4, p=128)
            for j, (o0, on) in enumerate(CONV_CH):
                for c in range(17):
                    buf = (c % 2) * 2048
                    s.dma(sy if c % 2 == 0 else g,
                          wpiece[:, buf:buf + 4 * on].rearrange(
                              "p (k f) -> p k f", k=4),
                          cbr[c][:, :, o0:o0 + on])
                    for k in range(4):
                        for tt in range(TPM):
                            s.op(te, te.matmul,
                                 pA[:, tt * 512:tt * 512 + on],
                                 xfm[:, c * 515 + k + tt * 128:
                                     c * 515 + k + tt * 128 + 128],
                                 wpiece[:, buf + k * on:buf + (k + 1) * on],
                                 start=(c == 0 and k == 0), stop=False)
                for tt in range(TPM):
                    s.op(te, te.matmul, pA[:, tt * 512:tt * 512 + on],
                         ones_rb[:], convb_bf[0:1, o0:o0 + on],
                         start=False, stop=True)
                    s.op(v, v.tensor_scalar, tmp5a[:, 0:on],
                         pA[:, tt * 512:tt * 512 + on], 0.0, a_col[:],
                         ALU.min, ALU.mult)
                    s.op(v, v.tensor_scalar, tmp5b[:, 0:on],
                         pA[:, tt * 512:tt * 512 + on], 0.0, None, ALU.max)
                    s.op(v, v.tensor_add,
                         xcbf[:, tt * CH + o0:tt * CH + o0 + on],
                         tmp5a[:, 0:on], tmp5b[:, 0:on])

        def ssd_tile(t):
            tt = (t - 1) % TPM
            pp = (t - 1) % 2
            t32a, t32b, asb, nasb = (t32a2[pp], t32b2[pp], asb2[pp],
                                     nasb2[pp])
            atr, wtr, wsb = atr2[pp], wtr2[pp], wsb2[pp]
            totrow, dtrow, dstage = totrow2[pp], dtrow2[pp], dstage2[pp]
            BTs, Ssb, Yfin = BTs2[pp], Ssb2[pp], Yfin2[pp]
            xc = xcbf[:, tt * CH:(tt + 1) * CH]
            dtr = dt_mega[:, tt * NH:(tt + 1) * NH]
            s.op(sc, sc.activation, t32a[:], dtr, AF.Abs)
            s.op(sc, sc.activation, t32a[:], t32a[:], AF.Exp, bias=0.0,
                 scale=-1.0)
            s.op(sc, sc.activation, t32a[:], t32a[:], AF.Ln, bias=1.0)
            s.op(v, v.tensor_scalar_max, t32b[:], dtr, 0.0)
            s.op(v, v.tensor_add, t32a[:], t32a[:], t32b[:])
            s.op(v, v.tensor_mul, t32a[:], t32a[:], A_bc[:])
            s.op(te, te.matmul, pB[:, 1024:1024 + NH], triu[:], t32a[:],
                 start=True, stop=True)
            s.op(te, te.matmul, pB[0:1, 1088:1088 + NH], ones_c[:], t32a[:],
                 start=True, stop=True)
            s.op(v, v.tensor_copy, asb[:], pB[:, 1024:1024 + NH])
            s.op(v, v.tensor_copy, totrow[:], pB[0:1, 1088:1088 + NH])
            s.op(v, v.tensor_scalar_mul, nasb[:], asb[:], -1.0)
            s.op(sc, sc.activation, expa_buf[:, (t - 1) * NH:t * NH], asb[:],
                 AF.Exp)
            s.op(v, v.tensor_copy, Ptot[0:1, (t - 1) * NH:t * NH], Rrow[:])
            s.op(v, v.tensor_add, Rrow[:], Rrow[:], totrow[:])
            s.op(te, te.transpose, pB[0:32, 1152:1280], asb[:], id_f[:])
            s.op(v, v.tensor_copy, atr[:], pB[0:32, 1152:1280])
            s.op(sc, sc.activation, wtr[:], atr[:], AF.Exp,
                 bias=atr[:, 127:128], scale=-1.0)
            s.op(te, te.transpose, pB[:, 1280:1280 + NH], wtr[:],
                 id_f[0:32, 0:32])
            s.op(v, v.tensor_copy, wsb[:], pB[:, 1280:1280 + NH])
            s.dma(sy, aflat[0:1, :].rearrange("o (p f) -> o p f", p=32),
                  atr[:])
            for q in range(8):
                po = (q % 2) * 512
                s.op(te, te.matmul, pB[:, po:po + 512], ones_r[:],
                     aflat[0:1, q * 512:(q + 1) * 512], start=True, stop=True)
                for i in range(4):
                    s.op(v, v.tensor_scalar, dstage[:, i * 128:(i + 1) * 128],
                         pB[:, po + i * 128:po + (i + 1) * 128],
                         nasb[:, 4 * q + i:4 * q + i + 1], 0.0,
                         ALU.add, ALU.min)
                s.op(sc, sc.activation, Gbuf[:, q * 512:(q + 1) * 512],
                     dstage[:], AF.Exp)
            s.op(te, te.transpose, pT[0:64, 0:128], xc[:, 2048:2112], id_b[:])
            s.op(v, v.tensor_copy, BTs[:], pT[0:64, 0:128])
            s.op(te, te.transpose, pT[0:64, 128:256], xc[:, 2112:2176], id_b[:])
            s.op(v, v.tensor_copy, CT_buf[:, (t - 1) * 128:t * 128],
                 pT[0:64, 128:256])
            s.op(te, te.matmul, pB[:, 1024:1152], BTs[:],
                 CT_buf[:, (t - 1) * 128:t * 128], start=True, stop=True)
            s.op(v, v.tensor_copy, Ssb[:], pB[:, 1024:1152])
            s.op(g, g.affine_select, Ssb[:], Ssb[:], [[1, 128]], ALU.is_ge,
                 0.0, base=0, channel_multiplier=-1)
            s.op(v, v.tensor_tensor,
                 Gbuf[:].rearrange("p (h l) -> p h l", h=NH),
                 Gbuf[:].rearrange("p (h l) -> p h l", h=NH),
                 Ssb[:].unsqueeze(1).broadcast_to([128, NH, 128]), ALU.mult)
            for h in range(NH):
                s.op(te, te.matmul, pA[:, h * HD:(h + 1) * HD],
                     Gbuf[:, h * 128:(h + 1) * 128],
                     xc[:, h * HD:(h + 1) * HD], start=True, stop=True)
            s.op(sc, sc.copy, Yfin[:], pA[:, 0:DI])
            s.op(sc, sc.copy, statebf[:], state[:])
            for q in range(4):
                s.op(te, te.matmul, pA[:, q * 512:(q + 1) * 512],
                     CT_buf[:, (t - 1) * 128:t * 128],
                     statebf[:, q * 512:(q + 1) * 512], start=True, stop=True)
            for h in range(NH):
                s.op(v, v.scalar_tensor_tensor, Yfin[:, h * HD:(h + 1) * HD],
                     pA[:, h * HD:(h + 1) * HD],
                     expa_buf[:, (t - 1) * NH + h:(t - 1) * NH + h + 1],
                     Yfin[:, h * HD:(h + 1) * HD], ALU.mult, ALU.add)
            s.dma(g, yspill.ap().rearrange("(t p) f -> t p f", p=128)[t - 1],
                  Yfin[:])
            for h in range(NH):
                s.op(v, v.tensor_scalar, xw[:, h * HD:(h + 1) * HD],
                     xc[:, h * HD:(h + 1) * HD], wsb[:, h:h + 1], None,
                     ALU.mult)
            for q in range(4):
                s.op(te, te.matmul, pA[64:128, q * 512:(q + 1) * 512],
                     xc[:, 2048:2112], xw[:, q * 512:(q + 1) * 512],
                     start=True, stop=True)
            s.op(sc, sc.activation, dtrow[:], totrow[:], AF.Exp)
            for h in range(NH):
                s.op(v, v.tensor_copy, aflat[0:1, h * HD:(h + 1) * HD],
                     dtrow[0:1, h:h + 1].broadcast_to([1, HD]))
            for q in range(4):
                s.op(te, te.matmul, pA[0:64, q * 512:(q + 1) * 512],
                     ones_r[0:1, 0:64], aflat[0:1, q * 512:(q + 1) * 512],
                     start=True, stop=True)
            s.op(v, v.tensor_tensor, state[:], state[:], pA[0:64, 0:DI],
                 ALU.mult)
            s.op(v, v.tensor_tensor, state[:], state[:], pA[64:128, 0:DI],
                 ALU.add)

        if n_megas > 0:
            front_tile(0)
        for m in range(n_megas):
            if m > 0:
                s.op(v, v.tensor_copy,
                     xfm[:, 0:17 * 515].rearrange(
                         "p (c f) -> p c f", c=17)[:, :, 0:3],
                     xfm[:, 0:17 * 515].rearrange(
                         "p (c f) -> p c f", c=17)[:, :, 512:515])
            for tt in range(TPM):
                front_tile(m * TPM + tt + 1)
            if "c" in parts:
                conv_mega(m)
            if "s" in parts:
                for tt in range(TPM):
                    ssd_tile(m * TPM + tt + 1)

        if n_megas > 0:
            s.dma(sy, ccsrc.ap(), state[:])
        if do_cc:
            s._deps(g, ("cc",), [ccsrc.ap()], [ccgat.ap()])
            g.collective_compute(
                "AllGather", ALU.bypass, [[0, 1, 2, 3, 4, 5, 6, 7]],
                [ccsrc.ap().opt()], [ccgat.ap().opt()],
            ).then_inc(s.cc_sem)
            s._record(("cc",), 1, [ccsrc.ap()], [ccgat.ap()])

    es1.close()
    s.barrier()
    if not full:
        s.fence(sy)
        es.close()
        nc.finalize()
        return nc

    FM = 8
    es2 = contextlib.ExitStack()
    _ees2 = es2.enter_context
    pC = _ees2(nc.psum_tensor("pC", [128, 2048], F32))
    pU = _ees2(nc.psum_tensor("pU", [128, 1024], BF16))
    w2p = _ees2(nc.sbuf_tensor("w2p", [128, 8 * 512], BF16))
    xnbt = _ees2(nc.sbuf_tensor("xnbt", [128, DM], BF16))
    xnT4 = _ees2(nc.sbuf_tensor("xnT4", [128, TPM * DM], BF16))
    xqT = _ees2(nc.sbuf_tensor("xqT", [128, TPM * DM], BF16))
    pmega = _ees2(nc.sbuf_tensor("pmega", [128, TPM * FFN_I], F32))
    gw = _ees2(nc.sbuf_tensor("gw", [128, 2 * 8 * 512], BF16))
    uw = _ees2(nc.sbuf_tensor("uw", [128, 8 * 512], BF16))
    gstage2 = [_ees2(nc.sbuf_tensor(f"gstage{i}", [128, 512], F32))
               for i in (0, 1)]
    ustage2 = [_ees2(nc.sbuf_tensor(f"ustage{i}", [128, 512], F32))
               for i in (0, 1)]
    scg = _ees2(nc.sbuf_tensor("scg", [128, 3 * TPM], F32))
    xq2b = _ees2(nc.sbuf_tensor("xq2b", [128, FFN_I], BF16))
    xq2T = _ees2(nc.sbuf_tensor("xq2T", [128, TPM * FFN_I], BF16))
    dw = _ees2(nc.sbuf_tensor("dw", [128, 2 * 2048], BF16))
    gout = _ees2(nc.sbuf_tensor("gout", [128, DM], F32))
    if True:

        w2r = w2bf.ap().rearrange("(n p) f -> p n f", p=128)
        xnsp = xnspill.ap().rearrange("(t p) f -> t p f", p=128)
        xqsp = xqspill.ap().rearrange("(t p) f -> t p f", p=128)
        u2sp = u2spill.ap().rearrange("(t p) f -> t p f", p=128)
        gsp = gspill.ap().rearrange("(t p) f -> t p f", p=128)
        wgr = wqg.ap().rearrange("(n p) f -> p n f", p=128)
        wur = wqu.ap().rearrange("(n p) f -> p n f", p=128)
        wdr = wqd.ap().rearrange("(n p) f -> p n f", p=128)

        for fm in range(FM):
            for tt in range(TPM):
                t = fm * TPM + tt + 1
                s.dma(sy, xnbt[:], xnsp[t - 1])
                for k in range(8):
                    s.op(te, te.transpose, pU[:, k * 128:(k + 1) * 128],
                         xnbt[:, k * 128:(k + 1) * 128], id_b[:])
                s.op(sc, sc.copy, xnT4[:, tt * DM:(tt + 1) * DM], pU[:])
                s.dma(sy, xnbt[:], xqsp[t - 1])
                for k in range(8):
                    s.op(te, te.transpose, pU[:, k * 128:(k + 1) * 128],
                         xnbt[:, k * 128:(k + 1) * 128], id_b[:])
                s.op(sc, sc.copy, xqT[:, tt * DM:(tt + 1) * DM], pU[:])
                s.op(v, v.tensor_mul, scg[:, 3 * tt:3 * tt + 1],
                     isx_buf[:, t - 1:t], mcols[:, 1:2])
                s.op(v, v.tensor_mul, scg[:, 3 * tt + 1:3 * tt + 2],
                     isx_buf[:, t - 1:t], mcols[:, 3:4])
            for j in range(4):
                s.dma(sy, w2p[:].rearrange("p (n f) -> p n f", n=8),
                      w2r[:, :, j * 512:(j + 1) * 512])
                for tt in range(TPM):
                    t = fm * TPM + tt + 1
                    gstage = gstage2[tt % 2]
                    for k in range(8):
                        s.op(te, te.matmul, pC[:, tt * 512:(tt + 1) * 512],
                             xnT4[:, tt * DM + k * 128:tt * DM + (k + 1) * 128],
                             w2p[:, k * 512:(k + 1) * 512],
                             start=(k == 0), stop=(k == 7))
                    s.op(sc, sc.activation, gstage[:],
                         pC[:, tt * 512:(tt + 1) * 512], AF.Gelu)
                    s.dma(g, u2sp[t - 1][:, j * 512:(j + 1) * 512], gstage[:])
            for j in range(8):
                bo = (j % 2) * 4096
                s.dma(sy if j % 2 == 0 else g, gw[:, bo:bo + 4096].rearrange(
                    "p (n f) -> p n f", n=8), wgr[:, :, j * 512:(j + 1) * 512])
                for tt in range(TPM):
                    for k in range(8):
                        s.op(te, te.matmul, pC[:, tt * 512:(tt + 1) * 512],
                             xqT[:, tt * DM + k * 128:tt * DM + (k + 1) * 128],
                             gw[:, bo + k * 512:bo + (k + 1) * 512],
                             start=(k == 0), stop=(k == 7))
                s.dma(g if j % 2 == 0 else sy, uw[:].rearrange(
                    "p (n f) -> p n f", n=8), wur[:, :, j * 512:(j + 1) * 512])
                for tt in range(TPM):
                    gstage, ustage = gstage2[tt % 2], ustage2[tt % 2]
                    s.op(sc, sc.activation, gstage[:],
                         pC[:, tt * 512:(tt + 1) * 512], AF.Silu,
                         bias=0.0, scale=scg[:, 3 * tt:3 * tt + 1])
                    for k in range(8):
                        s.op(te, te.matmul, pC[:, tt * 512:(tt + 1) * 512],
                             xqT[:, tt * DM + k * 128:tt * DM + (k + 1) * 128],
                             uw[:, k * 512:(k + 1) * 512],
                             start=(k == 0), stop=(k == 7))
                    s.op(sc, sc.activation, ustage[:],
                         pC[:, tt * 512:(tt + 1) * 512], AF.Copy,
                         bias=0.0, scale=scg[:, 3 * tt + 1:3 * tt + 2])
                    s.op(v, v.tensor_mul,
                         pmega[:, tt * FFN_I + j * 512:
                               tt * FFN_I + (j + 1) * 512],
                         gstage[:], ustage[:])
            for tt in range(TPM):
                t = fm * TPM + tt + 1
                pm = pmega[:, tt * FFN_I:(tt + 1) * FFN_I]
                s.op(v, v.tensor_reduce, colA[:, 0:1], pm, AX.X, ALU.max,
                     apply_absolute_value=True)
                s.op(v, v.tensor_scalar_max, colA[:, 0:1], colA[:, 0:1], 1e-5)
                s.op(v, v.reciprocal, colA[:, 1:2], colA[:, 0:1])
                s.op(v, v.tensor_scalar_mul, colA[:, 1:2], colA[:, 1:2],
                     127.0)
                s.op(v, v.tensor_scalar_mul, colA[:, 2:3], colA[:, 0:1],
                     1.0 / 127.0)
                s.op(v, v.tensor_mul, scg[:, 3 * tt + 2:3 * tt + 3],
                     colA[:, 2:3], mcols[:, 5:6])
                s.op(v, v.tensor_scalar, pm, pm, colA[:, 1:2], MAGIC,
                     ALU.mult, ALU.add)
                s.op(v, v.tensor_scalar, pm, pm, -MAGIC, 127.0,
                     ALU.add, ALU.min)
                s.op(v, v.tensor_scalar, xq2b[:], pm, -128.0, None,
                     ALU.max)
                for k0 in range(0, 32, 8):
                    for i in range(8):
                        s.op(te, te.transpose, pU[:, i * 128:(i + 1) * 128],
                             xq2b[:, (k0 + i) * 128:(k0 + i + 1) * 128],
                             id_b[:])
                    s.op(sc, sc.copy,
                         xq2T[:, tt * FFN_I + k0 * 128:
                              tt * FFN_I + (k0 + 8) * 128], pU[:])
            for hp in range(2):
                for kb in range(16):
                    bo = (kb % 2) * 2048
                    s.dma(sy if kb % 2 == 0 else g,
                          dw[:, bo:bo + 2048].rearrange(
                              "p (n f) -> p n f", n=2),
                          wdr[:, 2 * kb:2 * kb + 2, :])
                    for kk in range(2):
                        k = 2 * kb + kk
                        for tt in (2 * hp, 2 * hp + 1):
                            po = (tt % 2) * 1024
                            for q in range(2):
                                s.op(te, te.matmul,
                                     pC[:, po + q * 512:po + (q + 1) * 512],
                                     xq2T[:, tt * FFN_I + k * 128:
                                          tt * FFN_I + (k + 1) * 128],
                                     dw[:, bo + kk * 1024 + q * 512:
                                        bo + kk * 1024 + (q + 1) * 512],
                                     start=(k == 0), stop=(k == 31))
                for tt in (2 * hp, 2 * hp + 1):
                    t = fm * TPM + tt + 1
                    po = (tt % 2) * 1024
                    s.op(sc, sc.activation, gout[:], pC[:, po:po + 1024],
                         AF.Silu, bias=0.0, scale=scg[:, 3 * tt + 2:3 * tt + 3])
                    s.dma(g, gsp[t - 1], gout[:])

    es2.close()
    s.barrier()
    es3 = contextlib.ExitStack()
    _ees3 = es3.enter_context
    pE = _ees3(nc.psum_tensor("pE", [128, 2048], F32))
    pF = _ees3(nc.psum_tensor("pF", [128, 1024], F32))
    pV = _ees3(nc.psum_tensor("pV", [128, 1024], BF16))
    w3sb = _ees3(nc.sbuf_tensor("w3sb", [128, 16 * DM], BF16))
    gat = _ees3(nc.sbuf_tensor("gat", [DS, DI], F32))
    rmb = _ees3(nc.sbuf_tensor("rmb", [DS, 8], F32))
    sin_a = _ees3(nc.sbuf_tensor("sin_a", [DS, DI], F32))
    sinbf = _ees3(nc.sbuf_tensor("sinbf", [DS, DI], BF16))
    lngbc = _ees3(nc.sbuf_tensor("lngbc", [128, DI], F32))
    lnbbc = _ees3(nc.sbuf_tensor("lnbbc", [128, DI], F32))
    yt2 = [_ees3(nc.sbuf_tensor(f"yt{i}", [128, DI], F32)) for i in (0, 1)]
    u2t2 = [_ees3(nc.sbuf_tensor(f"u2t{i}", [128, DI], F32)) for i in (0, 1)]
    gt2 = [_ees3(nc.sbuf_tensor(f"gt{i}", [128, DM], F32)) for i in (0, 1)]
    ut2 = [_ees3(nc.sbuf_tensor(f"ut{i}", [128, DM], F32)) for i in (0, 1)]
    h22 = [_ees3(nc.sbuf_tensor(f"h2{i}", [128, DI], F32)) for i in (0, 1)]
    sq22 = [_ees3(nc.sbuf_tensor(f"sq2{i}", [128, DI], F32)) for i in (0, 1)]
    hnb2 = [_ees3(nc.sbuf_tensor(f"hnb{i}", [128, DI], BF16)) for i in (0, 1)]
    hnT2 = [_ees3(nc.sbuf_tensor(f"hnT{i}", [128, 16 * 128], BF16))
            for i in (0, 1)]
    Fsb2 = [_ees3(nc.sbuf_tensor(f"Fsb{i}", [128, NH], F32)) for i in (0, 1)]
    prow2 = [_ees3(nc.sbuf_tensor(f"prow{i}", [1, NH], F32)) for i in (0, 1)]
    ssb = _ees3(nc.sbuf_tensor("ssb", [128, DM], F32))
    osb2 = [_ees3(nc.sbuf_tensor(f"osb{i}", [128, DM], F32)) for i in (0, 1)]
    if True:

        w3r = w3bf.ap().rearrange("(n p) f -> n p f", p=128)
        for i in range(16):
            s.dma(sy, w3sb[:, i * DM:(i + 1) * DM], w3r[i])
        h2, sq2 = h22[0], sq22[0]
        s.dma(sy, h2[0:1, :], lng)
        s.dma(sy, sq2[0:1, :], lnb)
        for q in range(4):
            s.op(te, te.matmul, pE[:, q * 512:(q + 1) * 512], ones_r[:],
                 h2[0:1, q * 512:(q + 1) * 512], start=True, stop=True)
            s.op(te, te.matmul, pF[:, (q % 2) * 512:(q % 2) * 512 + 512],
                 ones_r[:], sq2[0:1, q * 512:(q + 1) * 512],
                 start=True, stop=True)
            s.op(v, v.tensor_copy, lngbc[:, q * 512:(q + 1) * 512],
                 pE[:, q * 512:(q + 1) * 512])
            s.op(v, v.tensor_copy, lnbbc[:, q * 512:(q + 1) * 512],
                 pF[:, (q % 2) * 512:(q % 2) * 512 + 512])
        s.dma(sy, rmb[0:1, 0:8], rmask)
        s.op(te, te.matmul, pF[0:DS, 0:8], ones_r[0:1, 0:DS], rmb[0:1, 0:8],
             start=True, stop=True)
        s.op(v, v.tensor_copy, rmb[:], pF[0:DS, 0:8])
        s.op(v, v.memset, sin_a[:], 0.0)
        gr = ccgat.ap().rearrange("(r n) f -> r n f", n=DS)
        for r in range(8):
            s.dma(sy, gat[:], gr[r])
            s.op(v, v.scalar_tensor_tensor, sin_a[:],
                 gat[:], rmb[:, r:r + 1], sin_a[:],
                 ALU.mult, ALU.add)
        s.op(sc, sc.copy, sinbf[:], sin_a[:])

        ysp = yspill.ap().rearrange("(t p) f -> t p f", p=128)
        u2sp = u2spill.ap().rearrange("(t p) f -> t p f", p=128)
        gsp = gspill.ap().rearrange("(t p) f -> t p f", p=128)
        orr = o.rearrange("(t p) f -> t p f", p=128)

        for t in range(1, NT + 1):
            pp = t % 2
            yt, u2t, gt, ut = yt2[pp], u2t2[pp], gt2[pp], ut2[pp]
            h2, sq2, hnb, hnT = h22[pp], sq22[pp], hnb2[pp], hnT2[pp]
            Fsb, prow, osb = Fsb2[pp], prow2[pp], osb2[pp]
            s.dma(sy, yt[:], ysp[t - 1])
            s.dma(g, u2t[:], u2sp[t - 1])
            s.dma(sy, gt[:], gsp[t - 1])
            s.dma(g, ut[:], ur[t])
            s.op(sc, sc.activation, prow[:], Ptot[0:1, (t - 1) * NH:t * NH],
                 AF.Exp)
            s.op(te, te.matmul, pF[:, 0:NH], ones_r[:], prow[:],
                 start=True, stop=True)
            s.op(v, v.tensor_tensor, Fsb[:], pF[:, 0:NH],
                 expa_buf[:, (t - 1) * NH:t * NH], ALU.mult)
            for q in range(4):
                s.op(te, te.matmul, pE[:, q * 512:(q + 1) * 512],
                     CT_buf[:, (t - 1) * 128:t * 128],
                     sinbf[:, q * 512:(q + 1) * 512], start=True, stop=True)
            for h in range(NH):
                s.op(v, v.scalar_tensor_tensor, yt[:, h * HD:(h + 1) * HD],
                     pE[:, h * HD:(h + 1) * HD], Fsb[:, h:h + 1],
                     yt[:, h * HD:(h + 1) * HD], ALU.mult, ALU.add)
            s.op(v, v.tensor_mul, h2[:], u2t[:], yt[:])
            s.op(v, v.reduce_sum, colA[:, 8 * pp + 0:8 * pp + 1], h2[:], AX.X)
            s.op(v, v.tensor_scalar_mul, colA[:, 8 * pp + 1:8 * pp + 2], colA[:, 8 * pp + 0:8 * pp + 1], 1.0 / DI)
            s.op(sc, sc.square, sq2[:], h2[:])
            s.op(v, v.reduce_sum, colA[:, 8 * pp + 2:8 * pp + 3], sq2[:], AX.X)
            s.op(v, v.tensor_scalar_mul, colA[:, 8 * pp + 3:8 * pp + 4], colA[:, 8 * pp + 2:8 * pp + 3], 1.0 / DI)
            s.op(v, v.tensor_mul, colA[:, 8 * pp + 4:8 * pp + 5], colA[:, 8 * pp + 1:8 * pp + 2], colA[:, 8 * pp + 1:8 * pp + 2])
            s.op(v, v.tensor_sub, colA[:, 8 * pp + 5:8 * pp + 6], colA[:, 8 * pp + 3:8 * pp + 4], colA[:, 8 * pp + 4:8 * pp + 5])
            s.op(v, v.tensor_scalar_add, colA[:, 8 * pp + 5:8 * pp + 6], colA[:, 8 * pp + 5:8 * pp + 6], LN_EPS)
            s.op(sc, sc.sqrt, colA[:, 8 * pp + 6:8 * pp + 7], colA[:, 8 * pp + 5:8 * pp + 6])
            s.op(v, v.reciprocal, colA[:, 8 * pp + 6:8 * pp + 7], colA[:, 8 * pp + 6:8 * pp + 7])
            s.op(v, v.tensor_mul, colA[:, 8 * pp + 7:8 * pp + 8], colA[:, 8 * pp + 1:8 * pp + 2], colA[:, 8 * pp + 6:8 * pp + 7])
            s.op(v, v.tensor_scalar_mul, colA[:, 8 * pp + 7:8 * pp + 8], colA[:, 8 * pp + 7:8 * pp + 8], -1.0)
            s.op(sc, sc.activation, sq2[:], h2[:], AF.Identity,
                 bias=colA[:, 8 * pp + 7:8 * pp + 8], scale=colA[:, 8 * pp + 6:8 * pp + 7])
            s.op(v, v.tensor_mul, sq2[:], sq2[:], lngbc[:])
            s.op(v, v.tensor_add, hnb[:], sq2[:], lnbbc[:])
            for k0 in range(0, 16, 8):
                for i in range(8):
                    s.op(te, te.transpose, pV[:, i * 128:(i + 1) * 128],
                         hnb[:, (k0 + i) * 128:(k0 + i + 1) * 128], id_b[:])
                s.op(sc, sc.copy, hnT[:, k0 * 128:(k0 + 8) * 128], pV[:])
            for j in range(2):
                for k in range(16):
                    s.op(te, te.matmul, pF[:, j * 512:(j + 1) * 512],
                         hnT[:, k * 128:(k + 1) * 128],
                         w3sb[:, k * DM + j * 512:k * DM + (j + 1) * 512],
                         start=(k == 0), stop=(k == 15))
            s.op(v, v.tensor_copy, ssb[:], pF[:, 0:DM])
            s.op(v, v.tensor_mul, osb[:], ssb[:], gt[:])
            s.op(v, v.tensor_add, osb[:], osb[:], ut[:])
            s.dma(g, orr[t - 1], osb[:])

        s.fence(sy)
    es3.close()
    es.close()
    nc.finalize()
    return nc


def make_in_maps(inputs):
    u_full = np.ascontiguousarray(inputs["L0_attn_u"], dtype=np.float32)
    w1t = np.ascontiguousarray(
        inputs["L0_attn_W_in1"].T[:, 2048:2048 + NP], dtype=np.float32)
    convwt = np.ascontiguousarray(
        np.transpose(inputs["L0_attn_conv_w"], (2, 1, 0)), dtype=np.float32)
    convb = np.ascontiguousarray(
        inputs["L0_attn_conv_b"].reshape(1, CH), dtype=np.float32)
    alog = np.ascontiguousarray(
        inputs["L0_attn_A_log"].reshape(1, NH), dtype=np.float32)
    prel = np.ascontiguousarray(
        np.asarray(inputs["L0_attn_prelu_a"]).reshape(1, 1), dtype=np.float32)
    w2t = np.ascontiguousarray(inputs["L0_attn_W_in2"].T, dtype=np.float32)
    w3t = np.ascontiguousarray(inputs["L0_attn_W_in3"].T, dtype=np.float32)
    wgt = np.ascontiguousarray(inputs["L0_ffn_w_gate"].T, dtype=np.float32)
    wut = np.ascontiguousarray(inputs["L0_ffn_w_up"].T, dtype=np.float32)
    wdt = np.ascontiguousarray(inputs["L0_ffn_w_down"].T, dtype=np.float32)
    lng = np.ascontiguousarray(
        inputs["L0_attn_ln_g"].reshape(1, DI), dtype=np.float32)
    lnb = np.ascontiguousarray(
        inputs["L0_attn_ln_b"].reshape(1, DI), dtype=np.float32)

    in_maps = []
    for c in range(8):
        b, half = c % 4, c // 4
        span = u_full[b, half * TPC:(half + 1) * TPC]
        head = np.zeros((128, DM), dtype=np.float32)
        if half == 1:
            head[125:128] = u_full[b, TPC - 3:TPC]
        ucore = np.ascontiguousarray(np.concatenate([head, span], axis=0))
        rmask = np.zeros((1, 8), dtype=np.float32)
        if half == 1:
            rmask[0, b] = 1.0
        in_maps.append(dict(
            u=ucore, w1t=w1t, convwt=convwt, convb=convb, alog=alog,
            prel=prel, w2t=w2t, w3t=w3t, wgt=wgt, wut=wut, wdt=wdt,
            lng=lng, lnb=lnb, rmask=rmask))
    return in_maps


def kernel(**inputs):
    nc = build_program()
    in_maps = make_in_maps(inputs)
    res = run_bass_kernel_spmd(nc, in_maps, core_ids=list(range(8)))
    out = np.empty((4, 2 * TPC, DM), dtype=np.float32)
    for c in range(8):
        b, half = c % 4, c // 4
        out[b, half * TPC:(half + 1) * TPC] = np.asarray(res.results[c]["o"])
    return out
